# version 07
# baseline (speedup 1.0000x reference)
import contextlib

import numpy as np
import ml_dtypes
import concourse.bass as bass
import concourse.mybir as mybir
from concourse.bass_utils import run_bass_kernel_spmd

F32 = mybir.dt.float32
BF16 = mybir.dt.bfloat16
I32 = mybir.dt.int32
AF = mybir.ActivationFunctionType
OP = mybir.AluOpType

DM = 2048
T = 2048
DIN = 4096
NH = 64
HD = 64
DS = 128
CH = 256
NCH = T // CH
CONVD = 4352
DPROJ = 8512
INTER = 5504
NCORES = 8
EPS = 1e-5

MB = 67
KC = 16
OB = 43
DB = 16
KO = 43
NT = 4
TILE_EL = 128 * 128
GATE_TILES = OB * KC
ALL_TILES = 3 * GATE_TILES
RANK_TILES = ALL_TILES // NCORES
MAT_CHUNK = 6
N_MATG = (RANK_TILES + MAT_CHUNK - 1) // MAT_CHUNK

DEBUG = False
PARANOID = False


class S:
    def __init__(self, nc, es, name):
        self.h = es.enter_context(nc.semaphore(name))
        self.n = 0

    def inc(self, instr, k=1):
        instr.then_inc(self.h, k)
        self.n += k
        return self.n

    def dinc(self, instr):
        return self.inc(instr, 16)


def build():
    nc = bass.Bass("TRN2", target_bir_lowering=False, debug=False)

    d_uT = nc.dram_tensor("uT", [DM, T], F32, kind="ExternalInput").ap()
    d_win = nc.dram_tensor("w_in", [DM, DPROJ], F32, kind="ExternalInput").ap()
    d_convwT = nc.dram_tensor("convwT", [4, CONVD], F32, kind="ExternalInput").ap()
    d_convb = nc.dram_tensor("convb", [CONVD], F32, kind="ExternalInput").ap()
    d_dtb = nc.dram_tensor("dt_bias", [NH], F32, kind="ExternalInput").ap()
    d_alog = nc.dram_tensor("a_log", [NH], F32, kind="ExternalInput").ap()
    d_D = nc.dram_tensor("dvec", [NH], F32, kind="ExternalInput").ap()
    d_nw = nc.dram_tensor("nw", [DIN], F32, kind="ExternalInput").ap()
    d_wout = nc.dram_tensor("w_out", [DIN, DM], F32, kind="ExternalInput").ap()
    d_lut = nc.dram_tensor("lut", [16], F32, kind="ExternalInput").ap()
    d_idx = nc.dram_tensor("idx_sl", [RANK_TILES, 128, 128], I32, kind="ExternalInput").ap()

    d_y = nc.dram_tensor("y", [DM, T], F32, kind="ExternalOutput").ap()

    z_sp = nc.dram_tensor("z_sp", [DIN, T], BF16).ap()
    x_sp = nc.dram_tensor("x_sp", [DIN, T], BF16).ap()
    b_sp = nc.dram_tensor("b_sp", [DS, T], BF16).ap()
    y_sp = nc.dram_tensor("y_sp", [DIN, T], BF16).ap()
    acs_sp = nc.dram_tensor("acs_sp", [NH, T], F32).ap()
    eacs_sp = nc.dram_tensor("eacs_sp", [NH, T], BF16).ap()
    mix_sp = nc.dram_tensor("mix_sp", [DM, T], BF16).ap()
    x1_sp = nc.dram_tensor("x1_sp", [DM, T], F32).ap()
    h_sp = nc.dram_tensor("h_sp", [INTER, T], BF16).ap()
    ffn_sp = nc.dram_tensor("ffn_sp", [DM, T], BF16).ap()
    row_sp = nc.dram_tensor("row_sp", [16, T], F32).ap()
    w_sl = nc.dram_tensor("w_sl", [RANK_TILES * TILE_EL], BF16).ap()
    w_full = nc.dram_tensor("w_full", [ALL_TILES * TILE_EL], BF16, addr_space="Shared").ap()

    taps = {}
    if DEBUG:
        for name, shape, dt in [
            ("tap_xn", [128, KC * T], BF16), ("tap_z", [DIN, T], BF16),
            ("tap_x", [DIN, T], BF16), ("tap_b", [DS, T], BF16),
            ("tap_c", [128, T], BF16), ("tap_dt", [NH, T], F32),
            ("tap_acs", [NH, T], F32), ("tap_y", [DIN, T], BF16),
            ("tap_wfull", [ALL_TILES * TILE_EL], BF16),
            ("tap_mix", [DM, T], BF16),
            ("tap_x1", [DM, T], F32), ("tap_h", [INTER, T], BF16),
            ("tap_ffn", [DM, T], BF16), ("tap_rows", [16, T], F32),
        ]:
            taps[name] = nc.dram_tensor(name, shape, dt, kind="ExternalOutput").ap()

    TE, VE, SE, PO, SY = nc.tensor, nc.vector, nc.scalar, nc.gpsimd, nc.sync

    es = contextlib.ExitStack()
    ld = S(nc, es, "ld")
    sd = S(nc, es, "sd")
    gd = S(nc, es, "gd")
    pe = S(nc, es, "pe")
    ve = S(nc, es, "ve")
    ae = S(nc, es, "ae")
    po = S(nc, es, "po")
    cc = S(nc, es, "cc")

    ident = es.enter_context(nc.sbuf_tensor("ident", [128, 128], F32))
    trimask = es.enter_context(nc.sbuf_tensor("trimask", [128, 2 * CH], BF16))
    ones1 = es.enter_context(nc.sbuf_tensor("ones1", [1, 128], F32))
    onesb = es.enter_context(nc.sbuf_tensor("onesb", [128, 1], BF16))
    lutB = es.enter_context(nc.sbuf_tensor("lutB", [128, 16], F32))
    wcols = es.enter_context(nc.sbuf_tensor("wcols", [128, 4 * 34], F32))
    bcols = es.enter_context(nc.sbuf_tensor("bcols", [128, 34], F32))
    nwcols = es.enter_context(nc.sbuf_tensor("nwcols", [128, 32], F32))
    smallrows = es.enter_context(nc.sbuf_tensor("smallrows", [1, 256], F32))
    dtb_col = es.enter_context(nc.sbuf_tensor("dtb_col", [64, 1], F32))
    nA_col = es.enter_context(nc.sbuf_tensor("nA_col", [64, 1], F32))
    DB_sb = es.enter_context(nc.sbuf_tensor("DB_sb", [128, 64], F32))
    DExp = es.enter_context(nc.sbuf_tensor("DExp", [128, 32], F32))

    dt_raw = es.enter_context(nc.sbuf_tensor("dt_raw", [64, T], F32))
    Bq = es.enter_context(nc.sbuf_tensor("Bq", [128, T], BF16))
    Cq = es.enter_context(nc.sbuf_tensor("Cq", [128, T], BF16))

    def barrier():
        nc.all_engine_barrier()

    def phase0(scope):
        scr44 = scope.enter_context(nc.sbuf_tensor("scr44", [4, CONVD], F32))
        scr14 = scope.enter_context(nc.sbuf_tensor("scr14", [1, CONVD], F32))
        nwrow = scope.enter_context(nc.sbuf_tensor("nwrow", [1, DIN], F32))
        io2 = scope.enter_context(nc.sbuf_tensor("io2", [128, 2 * CH], F32))
        pmisc = scope.enter_context(nc.psum_tensor("pmisc0", [128, 128], F32))

        gd.dinc(PO.dma_start(out=lutB[:, :], in_=bass.AP(d_lut.tensor, 0, [[0, 128], [1, 16]])))
        ld.dinc(SY.dma_start(out=scr44[:, :], in_=d_convwT))
        ld.dinc(SY.dma_start(out=scr14[0:1, :], in_=d_convb.unsqueeze(0)))
        ld.dinc(SY.dma_start(out=nwrow[0:1, :], in_=d_nw.unsqueeze(0)))
        ld.dinc(SY.dma_start(out=smallrows[0:1, 0:64], in_=d_dtb.unsqueeze(0)))
        ld.dinc(SY.dma_start(out=smallrows[0:1, 64:128], in_=d_alog.unsqueeze(0)))
        ld.dinc(SY.dma_start(out=smallrows[0:1, 128:192], in_=d_D.unsqueeze(0)))

        po.inc(PO.iota(io2[:, 0:CH], [[1, CH]], channel_multiplier=-1,
                       allow_small_or_imprecise_dtypes=True))
        po.inc(PO.iota(io2[:, CH:2 * CH], [[1, CH]], base=-128, channel_multiplier=-1,
                       allow_small_or_imprecise_dtypes=True))
        VE.wait_ge(po.h, po.n)
        ve.inc(VE.tensor_scalar(trimask[:, 0:CH], io2[:, 0:CH], 0.0, None, OP.is_ge))
        ve.inc(VE.tensor_scalar(trimask[:, CH:2 * CH], io2[:, CH:2 * CH], 0.0, None, OP.is_ge))
        ve.inc(VE.tensor_scalar(ident[:, :], io2[:, 0:128], 0.0, None, OP.is_equal))
        ve.inc(VE.memset(ones1[:, :], 1.0))
        ve.inc(VE.memset(onesb[:, :], 1.0))

        TE.wait_ge(ve.h, ve.n)
        TE.wait_ge(ld.h, ld.n)

        def transpose_group(items):
            col = 0
            plan = []
            for dst_ap, src_ap in items:
                p, f = src_ap.shape
                if col + p > 128:
                    break
                plan.append((dst_ap, src_ap, col, p, f))
                col += p
            for dst_ap, src_ap, c0, p, f in plan:
                pe.inc(TE.transpose(pmisc[0:f, c0:c0 + p], src_ap, ident[0:p, 0:p]))
            SE.wait_ge(pe.h, pe.n)
            for dst_ap, src_ap, c0, p, f in plan:
                ae.inc(SE.copy(dst_ap, pmisc[0:f, c0:c0 + p]))
            TE.wait_ge(ae.h, ae.n)
            return len(plan)

        items = [(wcols[:, 4 * ci:4 * ci + 4], scr44[0:4, 128 * ci:128 * ci + 128]) for ci in range(34)]
        while items:
            items = items[transpose_group(items):]
        items = [(bcols[:, ci:ci + 1], scr14[0:1, 128 * ci:128 * ci + 128]) for ci in range(34)]
        items += [(nwcols[:, ci:ci + 1], nwrow[0:1, 128 * ci:128 * ci + 128]) for ci in range(32)]
        items += [(dtb_col[0:64, 0:1], smallrows[0:1, 0:64]),
                  (nA_col[0:64, 0:1], smallrows[0:1, 64:128])]
        while items:
            items = items[transpose_group(items):]
        SE.wait_ge(ae.h, ae.n)
        ae.inc(SE.activation(nA_col[0:64, 0:1], nA_col[0:64, 0:1], AF.Exp))
        VE.wait_ge(ae.h, ae.n)
        ve.inc(VE.tensor_scalar(nA_col[0:64, 0:1], nA_col[0:64, 0:1], -1.0, None, OP.mult))

        pe.inc(TE.matmul(pmisc[:, 0:64], ones1[0:1, :], smallrows[0:1, 128:192],
                         start=True, stop=True))
        SE.wait_ge(pe.h, pe.n)
        ae.inc(SE.copy(DB_sb[:, 0:64], pmisc[:, 0:64]))
        VE.wait_ge(ae.h, ae.n)
        for fc in range(32):
            VE.tensor_copy(DExp[0:64, fc:fc + 1], DB_sb[0:64, 2 * fc:2 * fc + 1])
            ve.inc(VE.tensor_copy(DExp[64:128, fc:fc + 1], DB_sb[64:128, 2 * fc + 1:2 * fc + 2]))
        VE.wait_ge(gd.h, gd.n)

    mat = {"g": 0, "ld_tgt": [], "sp_tgt": []}

    def mat_alloc(scope):
        mat["idx"] = scope.enter_context(nc.sbuf_tensor("mat_idx", [128, 2 * MAT_CHUNK * 128], BF16))
        mat["acc"] = scope.enter_context(nc.sbuf_tensor("mat_acc", [128, MAT_CHUNK * 128], BF16))
        mat["m"] = scope.enter_context(nc.sbuf_tensor("mat_m", [128, MAT_CHUNK * 128], BF16))
        for g in range(min(2, N_MATG)):
            mat_issue_load(g)

    def mat_tiles(g):
        t0 = g * MAT_CHUNK
        return t0, min(MAT_CHUNK, RANK_TILES - t0)

    def mat_issue_load(g):
        t0, ntl = mat_tiles(g)
        half = (g % 2) * MAT_CHUNK * 128
        gd.dinc(PO.dma_start(
            out=mat["idx"][:, half:half + ntl * 128].rearrange("p (j m) -> p j m", m=128),
            in_=d_idx[t0:t0 + ntl].rearrange("j p m -> p j m")))
        mat["ld_tgt"].append(gd.n)

    def mat_emit_group():
        g = mat["g"]
        if g >= N_MATG:
            return
        mat["g"] += 1
        t0, ntl = mat_tiles(g)
        half = (g % 2) * MAT_CHUNK * 128
        idxv = mat["idx"][:, half:half + ntl * 128]
        acc = mat["acc"][:, 0:ntl * 128]
        m = mat["m"][:, 0:ntl * 128]
        VE.wait_ge(gd.h, mat["ld_tgt"][g])
        if g >= 1:
            VE.wait_ge(ld.h, mat["sp_tgt"][g - 1])
        VE.tensor_scalar(acc, idxv, 0.0, lutB[:, 0:1], OP.is_equal, OP.mult)
        lastm = None
        for k in range(1, 16):
            VE.tensor_scalar(m, idxv, float(k), lutB[:, k:k + 1], OP.is_equal, OP.mult)
            lastm = VE.tensor_tensor(acc, acc, m, OP.add)
        spill_wait = ve.inc(lastm)
        SY.wait_ge(ve.h, spill_wait)
        mat["sp_tgt"].append(ld.dinc(SY.dma_start(
            out=w_sl[t0 * TILE_EL:(t0 + ntl) * TILE_EL].rearrange("(j p m) -> p j m", p=128, m=128),
            in_=acc.rearrange("p (j m) -> p j m", m=128))))
        if g + 2 < N_MATG:
            PO.wait_ge(ve.h, spill_wait)
            mat_issue_load(g + 2)

    def mat_finish_and_collective():
        while mat["g"] < N_MATG:
            mat_emit_group()
        PO.wait_ge(ld.h, ld.n)
        cc.inc(PO.collective_compute(
            "AllGather", OP.bypass,
            replica_groups=[list(range(NCORES))],
            ins=[w_sl[:]], outs=[w_full[:]]))

    def phaseA2(scope, xn):
        uin = scope.enter_context(nc.sbuf_tensor("uin", [128, 2 * T], F32))
        ubf = scope.enter_context(nc.sbuf_tensor("ubf_sl", [128, 2 * T], BF16))
        sqb = scope.enter_context(nc.sbuf_tensor("sqb", [128, 2 * T], BF16))
        mrows = scope.enter_context(nc.sbuf_tensor("mrows_a", [1, 2 * T], F32))
        mB = scope.enter_context(nc.sbuf_tensor("mB_a", [128, T], BF16))
        rB = scope.enter_context(nc.sbuf_tensor("rB_a", [128, T], BF16))
        pm = [scope.enter_context(nc.psum_tensor(f"pmA{i}", [1, 512], F32)) for i in range(4)]
        ps = [scope.enter_context(nc.psum_tensor(f"psA{i}", [1, 512], F32)) for i in range(4)]

        ld_tgt, cast_tgt, sq_tgt, mm_tgt = {}, {}, {}, {}
        for dc in range(KC):
            slot = dc % 2
            if dc >= 2:
                SY.wait_ge(ve.h, cast_tgt[dc - 2])
            ld_tgt[dc] = ld.dinc(SY.dma_start(out=uin[:, slot * T:slot * T + T],
                                              in_=d_uT[128 * dc:128 * dc + 128, :]))
            VE.wait_ge(ld.h, ld_tgt[dc])
            if dc >= 2:
                VE.wait_ge(pe.h, mm_tgt[dc - 2])
            cast_tgt[dc] = ve.inc(VE.tensor_copy(ubf[:, slot * T:slot * T + T],
                                                 uin[:, slot * T:slot * T + T]))
            if dc >= 2:
                SE.wait_ge(pe.h, mm_tgt[dc - 2])
            SE.wait_ge(ve.h, cast_tgt[dc])
            sq_tgt[dc] = ae.inc(SE.activation(sqb[:, slot * T:slot * T + T],
                                              ubf[:, slot * T:slot * T + T], AF.Square))
            TE.wait_ge(ae.h, sq_tgt[dc])
            last = None
            for t in range(NT):
                TE.matmul(pm[t][0:1, :], onesb[:, 0:1],
                          ubf[:, slot * T + 512 * t: slot * T + 512 * t + 512],
                          start=(dc == 0), stop=(dc == KC - 1))
                last = TE.matmul(ps[t][0:1, :], onesb[:, 0:1],
                                 sqb[:, slot * T + 512 * t: slot * T + 512 * t + 512],
                                 start=(dc == 0), stop=(dc == KC - 1))
            mm_tgt[dc] = pe.inc(last)
            if dc % 5 == 4:
                mat_emit_group()

        VE.wait_ge(pe.h, pe.n)
        for t in range(NT):
            VE.tensor_scalar(mrows[0:1, 512 * t:512 * t + 512], pm[t][0:1, :], 1.0 / DM, None, OP.mult)
            ve.inc(VE.tensor_scalar(mrows[0:1, T + 512 * t:T + 512 * t + 512], ps[t][0:1, :],
                                    1.0 / DM, None, OP.mult))
        VE.tensor_tensor(uin[0:1, 0:T], mrows[0:1, 0:T], mrows[0:1, 0:T], OP.mult)
        VE.tensor_tensor(mrows[0:1, T:2 * T], mrows[0:1, T:2 * T], uin[0:1, 0:T], OP.subtract)
        ve.inc(VE.tensor_scalar(mrows[0:1, T:2 * T], mrows[0:1, T:2 * T], EPS, None, OP.add))
        SE.wait_ge(ve.h, ve.n)
        ae.inc(SE.activation(mrows[0:1, T:2 * T], mrows[0:1, T:2 * T], AF.Sqrt))
        VE.wait_ge(ae.h, ae.n)
        ve.inc(VE.reciprocal(mrows[0:1, T:2 * T], mrows[0:1, T:2 * T]))
        SY.wait_ge(ve.h, ve.n)
        ld.dinc(SY.dma_start(out=row_sp.rearrange("r t -> (r t)")[0:2 * T].unsqueeze(0),
                             in_=mrows[0:1, :]))
        PO.wait_ge(ld.h, ld.n)
        for _rep in range(2):
            gd.dinc(PO.dma_start(out=mB[:, :], in_=bass.AP(row_sp.tensor, 0, [[0, 128], [1, T]])))
            gd.dinc(PO.dma_start(out=rB[:, :], in_=bass.AP(row_sp.tensor, T, [[0, 128], [1, T]])))
        ld2_tgt, nrm_tgt = {}, {}
        VE.wait_ge(gd.h, gd.n)
        for dc in range(KC):
            slot = dc % 2
            if dc >= 2:
                SY.wait_ge(ve.h, nrm_tgt[dc - 2])
            ld2_tgt[dc] = ld.dinc(SY.dma_start(out=uin[:, slot * T:slot * T + T],
                                               in_=d_uT[128 * dc:128 * dc + 128, :]))
            VE.wait_ge(ld.h, ld2_tgt[dc])
            VE.tensor_copy(ubf[:, slot * T:slot * T + T], uin[:, slot * T:slot * T + T])
            VE.tensor_tensor(ubf[:, slot * T:slot * T + T], ubf[:, slot * T:slot * T + T],
                             mB[:, :], OP.subtract)
            nrm_tgt[dc] = ve.inc(VE.tensor_tensor(xn[:, dc * T:(dc + 1) * T],
                                                  ubf[:, slot * T:slot * T + T],
                                                  rB[:, :], OP.mult))
            if dc % 4 == 3:
                mat_emit_group()
        if DEBUG:
            SY.wait_ge(ve.h, ve.n)
            ld.dinc(SY.dma_start(out=taps["tap_xn"], in_=xn[:, :]))
        SY.wait_ge(ld.h, ld.n)

    def phaseB(scope, xn):
        slab = scope.enter_context(nc.sbuf_tensor("slabB", [128, 2 * KC * 128], BF16))
        stage = scope.enter_context(nc.sbuf_tensor("stageB", [128, 2 * (T + 3)], BF16))
        xout = scope.enter_context(nc.sbuf_tensor("xoutB", [128, 2 * T], BF16))
        pz = [scope.enter_context(nc.psum_tensor(f"pB{i}", [128, 512], F32)) for i in range(8)]

        ve.inc(VE.memset(stage[:, 0:3], 0.0))
        ve.inc(VE.memset(stage[:, T + 3:T + 6], 0.0))
        stage_base = [0, T + 3]

        slab_tgt, mmdone_tgt, mmdone_t_tgt, evac_tgt = {}, {}, {}, {}
        conv_tgt, silu_tgt, spill_tgt = {}, {}, {}

        def issue_slab(mb):
            mcols = 64 if mb == MB - 1 else 128
            half = (mb % 2) * KC * 128
            if mb >= 2:
                PO.wait_ge(pe.h, mmdone_tgt[mb - 2])
            gd.dinc(PO.dma_start(
                out=slab[:, half:half + KC * mcols].rearrange("p (kc m) -> p kc m", m=mcols),
                in_=d_win[:, 128 * mb:128 * mb + mcols].rearrange("(kc p) m -> p kc m", p=128)))
            slab_tgt[mb] = gd.n

        issue_slab(0)
        issue_slab(1)
        for mb in range(MB):
            mcols = 64 if mb == MB - 1 else 128
            half = (mb % 2) * KC * 128
            sb = stage_base[mb % 2]
            is_dt = (mb == MB - 1)
            TE.wait_ge(gd.h, slab_tgt[mb])
            if mb >= 2:
                TE.wait_ge(ae.h, evac_tgt[mb - 2])
            for t in range(NT):
                bank = pz[(mb % 2) * 4 + t]
                last = None
                for kcq in range(KC):
                    last = TE.matmul(bank[0:mcols, :],
                                     slab[:, half + kcq * mcols: half + (kcq + 1) * mcols],
                                     xn[:, kcq * T + 512 * t: kcq * T + 512 * t + 512],
                                     start=(kcq == 0), stop=(kcq == KC - 1))
                mmdone_t_tgt[(mb, t)] = pe.inc(last)
            mmdone_tgt[mb] = pe.n
            if mb >= 2 and not is_dt:
                pmb = mb - 2
                if pmb < 32:
                    SE.wait_ge(ld.h, spill_tgt[pmb])
                elif pmb >= 32:
                    SE.wait_ge(ve.h, conv_tgt[pmb])
            for t in range(NT):
                bank = pz[(mb % 2) * 4 + t]
                SE.wait_ge(pe.h, mmdone_t_tgt[(mb, t)])
                if is_dt:
                    ev = SE.copy(dt_raw[0:64, 512 * t:512 * t + 512], bank[0:64, :])
                else:
                    ev = SE.copy(stage[:, sb + 3 + 512 * t: sb + 3 + 512 * t + 512], bank[:, :])
                ae.inc(ev)
            evac_tgt[mb] = ae.n
            if is_dt:
                continue
            if mb < 32:
                SY.wait_ge(ae.h, evac_tgt[mb])
                spill_tgt[mb] = ld.dinc(SY.dma_start(out=z_sp[128 * mb:128 * mb + 128, :],
                                                     in_=stage[:, sb + 3: sb + 3 + T]))
            else:
                ci = mb - 32
                xv = xout[:, (mb % 2) * T:(mb % 2) * T + T]
                VE.wait_ge(ae.h, evac_tgt[mb])
                if mb >= 34 and mb - 2 >= 32:
                    pci = mb - 34
                    if pci < 32:
                        VE.wait_ge(ld.h, spill_tgt[mb - 2])
                    else:
                        VE.wait_ge(ae.h, silu_tgt[mb - 2])
                VE.tensor_scalar(xv, stage[:, sb + 0: sb + 0 + T],
                                 wcols[:, 4 * ci:4 * ci + 1], None, OP.mult)
                lastc = None
                for k in (1, 2, 3):
                    lastc = VE.scalar_tensor_tensor(xv, stage[:, sb + k: sb + k + T],
                                                    wcols[:, 4 * ci + k:4 * ci + k + 1], xv,
                                                    OP.mult, OP.add)
                conv_tgt[mb] = ve.inc(lastc)
                SE.wait_ge(ve.h, conv_tgt[mb])
                if ci < 32:
                    silu_tgt[mb] = ae.inc(SE.activation(xv, xv, AF.Silu,
                                                        bias=bcols[:, ci:ci + 1]))
                    SY.wait_ge(ae.h, silu_tgt[mb])
                    spill_tgt[mb] = ld.dinc(SY.dma_start(
                        out=x_sp[128 * ci:128 * ci + 128, :], in_=xv))
                elif ci == 32:
                    silu_tgt[mb] = ae.inc(SE.activation(Bq[:, :], xv, AF.Silu,
                                                        bias=bcols[:, ci:ci + 1]))
                    SY.wait_ge(ae.h, silu_tgt[mb])
                    spill_tgt[mb] = ld.dinc(SY.dma_start(out=b_sp[:, :], in_=Bq[:, :]))
                else:
                    silu_tgt[mb] = ae.inc(SE.activation(Cq[:, :], xv, AF.Silu,
                                                        bias=bcols[:, ci:ci + 1]))
                    spill_tgt[mb] = ld.n
            if mb + 2 < MB:
                issue_slab(mb + 2)
            if mb % 2 == 1:
                mat_emit_group()
        SY.wait_ge(ld.h, ld.n)
        SY.wait_ge(ae.h, ae.n)
        if DEBUG:
            ld.dinc(SY.dma_start(out=taps["tap_z"], in_=z_sp))
            ld.dinc(SY.dma_start(out=taps["tap_x"], in_=x_sp))
            ld.dinc(SY.dma_start(out=taps["tap_b"], in_=b_sp))
            ld.dinc(SY.dma_start(out=taps["tap_c"], in_=Cq[:, :]))
            ld.dinc(SY.dma_start(out=taps["tap_dt"], in_=dt_raw[0:64, :]))
            SY.wait_ge(ld.h, ld.n)

    def phaseC(scope):
        dtv = scope.enter_context(nc.sbuf_tensor("dtv", [64, T], F32))
        acs = scope.enter_context(nc.sbuf_tensor("acs", [64, T], F32))
        eacs = scope.enter_context(nc.sbuf_tensor("eacs", [64, T], BF16))
        Sv = [scope.enter_context(nc.sbuf_tensor(f"Sv{i}", [128, NH * HD], F32)) for i in range(2)]
        Snb = [scope.enter_context(nc.sbuf_tensor(f"Snb{i}", [128, NH * HD], BF16)) for i in range(2)]
        xT = [scope.enter_context(nc.sbuf_tensor(f"xT{i}", [128, 2 * DIN], BF16)) for i in range(2)]
        xsrc = scope.enter_context(nc.sbuf_tensor("xsrc", [128, 32 * CH], BF16))
        identb = scope.enter_context(nc.sbuf_tensor("identb", [128, 128], BF16))
        BT = [scope.enter_context(nc.sbuf_tensor(f"BT{i}", [128, 256], BF16)) for i in range(2)]
        EAB = scope.enter_context(nc.sbuf_tensor("EAB", [128, 2 * 16 * CH], BF16))
        Yc = scope.enter_context(nc.sbuf_tensor("Yc", [128, 8 * CH], BF16))
        Gm = scope.enter_context(nc.sbuf_tensor("Gm", [128, 2 * CH], BF16))
        CqDup = scope.enter_context(nc.sbuf_tensor("CqDup", [128, 2 * CH], BF16))
        AcsT = scope.enter_context(nc.sbuf_tensor("AcsT", [128, 128], F32))
        dtT = scope.enter_context(nc.sbuf_tensor("dtT", [128, 128], F32))
        AmL = scope.enter_context(nc.sbuf_tensor("AmL", [128, 128], F32))
        ddtT = scope.enter_context(nc.sbuf_tensor("ddtT", [128, 128], F32))
        ddtmp = scope.enter_context(nc.sbuf_tensor("ddtmp", [128, 64], F32))
        cdB = scope.enter_context(nc.sbuf_tensor("cdB", [128, 64], F32))
        lrow = scope.enter_context(nc.sbuf_tensor("lrow", [1, 128], F32))
        fenceA = scope.enter_context(nc.sbuf_tensor("fenceA", [128, 4], BF16))
        rbc = scope.enter_context(nc.sbuf_tensor("rbc", [128, 8 * CH], F32))
        dbuf = scope.enter_context(nc.sbuf_tensor("dbuf", [128, 3 * 1024], F32))
        ebuf = scope.enter_context(nc.sbuf_tensor("ebuf", [128, 3 * 1024], BF16))
        tbuf = scope.enter_context(nc.sbuf_tensor("tbuf", [128, 2 * 1024], BF16))
        cdbuf = scope.enter_context(nc.sbuf_tensor("cdbuf", [128, 2 * 512], BF16))
        xdbuf = scope.enter_context(nc.sbuf_tensor("xdbuf", [128, 2 * 256], BF16))
        pG = scope.enter_context(nc.psum_tensor("pG", [128, 512], F32))
        pY = [scope.enter_context(nc.psum_tensor(f"pY{i}", [128, 256], F32)) for i in range(2)]
        pS = [scope.enter_context(nc.psum_tensor(f"pS{i}", [128, 128], F32)) for i in range(2)]
        pmC = scope.enter_context(nc.psum_tensor("pmC", [128, 128], F32))
        pT = [scope.enter_context(nc.psum_tensor(f"pT{i}", [128, 128], BF16)) for i in range(2)]

        ve.inc(VE.tensor_copy(identb[:, :], ident[:, :]))
        scrA = dbuf[0:64, 0:T]
        scrB = rbc[0:64, 0:T]
        ae.inc(SE.activation(scrA, dt_raw[0:64, :], AF.Abs, bias=dtb_col[0:64, 0:1]))
        ae.inc(SE.activation(scrB, scrA, AF.Exp, scale=-1.0))
        ae.inc(SE.activation(scrB, scrB, AF.Ln, bias=1.0))
        VE.wait_ge(ae.h, ae.n)
        VE.tensor_scalar(dtv[:, :], dt_raw[0:64, :], dtb_col[0:64, 0:1], 0.0, OP.add, OP.max)
        VE.tensor_tensor(dtv[:, :], dtv[:, :], scrB, OP.add)
        VE.tensor_scalar(dtv[:, :], dtv[:, :], 1e-20, None, OP.max)
        ve.inc(VE.tensor_scalar(scrA, dtv[:, :], nA_col[0:64, 0:1], None, OP.mult))
        for c in range(NCH):
            ve.inc(VE.tensor_tensor_scan(acs[0:64, c * CH:(c + 1) * CH],
                                         scrA[:, c * CH:(c + 1) * CH],
                                         scrB[:, c * CH:(c + 1) * CH], 0.0,
                                         OP.add, OP.bypass))
        SE.wait_ge(ve.h, ve.n)
        ae.inc(SE.activation(eacs[:, :], acs[0:64, :], AF.Exp))
        SY.wait_ge(ve.h, ve.n)
        acs_sp_tgt = ld.dinc(SY.dma_start(out=acs_sp, in_=acs[0:64, :]))
        SY.wait_ge(ae.h, ae.n)
        spl_tgt = ld.dinc(SY.dma_start(out=eacs_sp, in_=eacs[0:64, :]))
        ve.inc(VE.memset(Sv[0][:, :], 0.0))
        ve.inc(VE.memset(Snb[0][:, :], 0.0))

        xt_tgt = {}
        t_done = {}
        cd_done = {}
        eab_tgt = {}
        rbc_tgt = {}
        diff_done = {}
        ops_done = {}
        mm_done = {}
        yevac_done = {}
        supd_done = {}
        exp_done = {}
        yspill_tgt = {}
        gm_done = {}

        xsrc_tgt = {}
        tp_ae_tgt = {}
        tp_state = {"n": 0}

        def issue_xsrc(c):
            if c >= NCH:
                return
            PO.wait_ge(pe.h, pe.n)
            for _rep in range(2):
                gd.dinc(PO.dma_start(
                    out=xsrc[:, :].rearrange("p (pb l) -> p pb l", l=CH),
                    in_=x_sp.rearrange("(pb p) t -> p pb t", p=128)[:, :, c * CH:(c + 1) * CH]))
            xsrc_tgt[c] = gd.n

        def emit_transposes(c, n):
            if c >= NCH:
                return
            bufc = c % 2
            for _ in range(n):
                i = tp_state["n"]
                if i >= 66:
                    return
                tp_state["n"] = i + 1
                bank = pT[i % 2]
                if i == 0:
                    TE.wait_ge(gd.h, xsrc_tgt[c])
                    TE.wait_ge(ve.h, ve.n)
                if i >= 2:
                    TE.wait_ge(ae.h, tp_ae_tgt[i - 2])
                if i < 64:
                    pb, sb2 = i // 2, i % 2
                    srcv = xsrc[:, pb * CH + sb2 * 128: pb * CH + sb2 * 128 + 128]
                    dstv = xT[bufc][:, sb2 * DIN + 128 * pb: sb2 * DIN + 128 * pb + 128]
                else:
                    sb2 = i - 64
                    srcv = Bq[:, c * CH + sb2 * 128: c * CH + sb2 * 128 + 128]
                    dstv = BT[bufc][:, sb2 * 128: sb2 * 128 + 128]
                pe.inc(TE.transpose(bank[:, :], srcv, identb[:, :]))
                SE.wait_ge(pe.h, pe.n)
                tp_ae_tgt[i] = ae.inc(SE.copy(dstv, bank[:, :]))
            if tp_state["n"] >= 66:
                xt_tgt[c] = ("ae", tp_ae_tgt[65])

        eab_user = {}

        def issue_eab(c, q):
            if c >= NCH:
                return
            half = q % 2
            PO.wait_ge(ld.h, spl_tgt)
            if eab_user.get(half) is not None:
                PO.wait_ge(ve.h, eab_user[half])
            gd.dinc(PO.dma_start(
                out=EAB[:, half * 16 * CH: (half + 1) * 16 * CH]
                    .rearrange("p (h l) -> p h l", l=CH),
                in_=bass.AP(eacs_sp.tensor, (q * 16) * T + c * CH,
                            [[0, 128], [T, 16], [1, CH]])))
            eab_tgt[(c, q)] = gd.n

        def issue_rbc(gj):
            c, j = divmod(gj, 32)
            if c >= NCH:
                return
            PO.wait_ge(ld.h, acs_sp_tgt)
            if gj >= 4:
                PO.wait_ge(ve.h, diff_done[gj - 4])
            slot = (2 * gj) % 8
            sd.dinc(PO.dma_start(
                out=rbc[:, slot * CH:(slot + 2) * CH].rearrange("p (hh l) -> p hh l", l=CH),
                in_=bass.AP(acs_sp.tensor, 2 * j * T + c * CH,
                            [[0, 128], [T, 2], [1, CH]])))
            rbc_tgt[gj] = sd.n

        def emit_diff(gj):
            c, j = divmod(gj, 32)
            if c >= NCH:
                return
            par3 = gj % 3
            VE.wait_ge(sd.h, min(rbc_tgt[gj] + 32, max(rbc_tgt.values())))
            if gj >= 3:
                VE.wait_ge(ae.h, exp_done[gj - 3])
            lastd = None
            for hh in range(2):
                h = 2 * j + hh
                slot = (2 * gj + hh) % 8
                for sb2 in range(2):
                    lastd = VE.tensor_scalar(
                        dbuf[:, par3 * 1024 + (2 * hh + sb2) * CH: par3 * 1024 + (2 * hh + sb2 + 1) * CH],
                        rbc[:, slot * CH:(slot + 1) * CH],
                        AmL[:, sb2 * 64 + h: sb2 * 64 + h + 1], 30.0,
                        OP.subtract, OP.min)
            diff_done[gj] = ve.inc(lastd)

        def emit_exp(gj):
            c, j = divmod(gj, 32)
            if c >= NCH:
                return
            par3 = gj % 3
            SE.wait_ge(ve.h, diff_done[gj])
            if gj >= 3:
                SE.wait_ge(ve.h, t_done.get(gj - 3, 0))
            SE.activation(ebuf[:, par3 * 1024:(par3 + 1) * 1024],
                          dbuf[:, par3 * 1024:(par3 + 1) * 1024], AF.Exp)
            exp_done[gj] = ae.inc(SE.copy(fenceA[:, 0:1],
                                          ebuf[:, par3 * 1024 + 1023: par3 * 1024 + 1024]))

        issue_xsrc(0)
        emit_transposes(0, 66)
        tp_state["n"] = 0
        issue_eab(0, 0)
        issue_eab(0, 1)
        for g0 in range(4):
            issue_rbc(g0)

        for c in range(NCH):
            cT = c * CH
            bufc = c % 2
            Sprev, Snext = Sv[c % 2], Sv[1 - c % 2]
            Sbprev, Sbnext = Snb[c % 2], Snb[1 - c % 2]
            SE.wait_ge(ld.h, acs_sp_tgt)
            for _rep in range(3):
                lr_tgt = sd.dinc(SE.dma_start(
                    out=lrow[0:1, 0:64],
                    in_=acs_sp.rearrange("h t -> (h t)")[cT + 255::T][0:64].unsqueeze(0),
                    allow_slow_non_contiguous=True))
            if c >= 1:
                TE.wait_ge(ve.h, gm_done[c - 1])
            TE.wait_ge(ae.h, xt_tgt[c][1])
            VE.wait_ge(ae.h, xt_tgt[c][1])
            if c + 1 < NCH:
                issue_xsrc(c + 1)
            tp_state["n"] = 0
            gmm = None
            for sb2 in range(2):
                gmm = TE.matmul(pG[:, sb2 * CH:(sb2 + 1) * CH],
                                Bq[:, cT + 128 * sb2: cT + 128 * sb2 + 128],
                                Cq[:, cT:cT + CH], start=True, stop=True)
            g_mm = pe.inc(gmm)
            VE.wait_ge(pe.h, g_mm)
            lastg = None
            for sb2 in range(2):
                lastg = VE.tensor_tensor(Gm[:, sb2 * CH:(sb2 + 1) * CH], pG[:, sb2 * CH:(sb2 + 1) * CH],
                                         trimask[:, sb2 * CH:(sb2 + 1) * CH], OP.mult)
            gm_done[c] = ve.inc(lastg)
            VE.tensor_copy(CqDup[:, 0:CH], Cq[:, cT:cT + CH])
            cqdup_tgt = ve.inc(VE.tensor_copy(CqDup[:, CH:2 * CH], Cq[:, cT:cT + CH]))
            if c >= 1:
                SE.wait_ge(ve.h, supd_done[(c - 1) * 32 + 31])
                TE.wait_ge(ve.h, supd_done[(c - 1) * 32 + 31])
            tp_ae = None
            for (srct, dstt) in ((acs, AcsT), (dtv, dtT)):
                for sb2 in range(2):
                    if tp_ae is not None:
                        TE.wait_ge(ae.h, tp_ae)
                    pe.inc(TE.transpose(pmC[0:128, 0:64],
                                        srct[0:64, cT + 128 * sb2: cT + 128 * sb2 + 128],
                                        ident[0:64, 0:64]))
                    SE.wait_ge(pe.h, pe.n)
                    tp_ae = ae.inc(SE.copy(dstt[:, sb2 * 64:(sb2 + 1) * 64], pmC[0:128, 0:64]))
            SE.wait_ge(pe.h, pe.n)
            ae.inc(SE.activation(AmL[:, :], dtT[:, :], AF.Ln))
            VE.wait_ge(ae.h, ae.n)
            ve.inc(VE.tensor_tensor(AmL[:, :], AcsT[:, :], AmL[:, :], OP.subtract))
            SE.wait_ge(sd.h, lr_tgt)
            ae.inc(SE.activation(lrow[0:1, 64:128], lrow[0:1, 0:64], AF.Exp))
            TE.wait_ge(ae.h, ae.n)
            lb_mm = pe.inc(TE.matmul(pmC[:, 0:64], ones1[0:1, :], lrow[0:1, 0:64],
                                     start=True, stop=True))
            VE.wait_ge(pe.h, lb_mm)
            VE.wait_ge(ae.h, ae.n)
            for sb2 in range(2):
                ve.inc(VE.tensor_tensor(ddtmp[:, 0:64], pmC[:, 0:64],
                                        AcsT[:, sb2 * 64:(sb2 + 1) * 64], OP.subtract))
                SE.wait_ge(ve.h, ve.n)
                ae.inc(SE.activation(ddtT[:, sb2 * 64:(sb2 + 1) * 64], ddtmp[:, 0:64], AF.Exp))
                VE.wait_ge(ae.h, ae.n)
                ve.inc(VE.tensor_tensor(ddtT[:, sb2 * 64:(sb2 + 1) * 64],
                                        ddtT[:, sb2 * 64:(sb2 + 1) * 64],
                                        dtT[:, sb2 * 64:(sb2 + 1) * 64], OP.mult))
            TE.wait_ge(ve.h, ve.n)
            cd_mm = pe.inc(TE.matmul(pmC[:, 64:128], ones1[0:1, :], lrow[0:1, 64:128],
                                     start=True, stop=True))
            SE.wait_ge(pe.h, cd_mm)
            cd_ae = ae.inc(SE.copy(cdB[:, :], pmC[:, 64:128]))
            VE.wait_ge(ae.h, cd_ae)
            emit_diff(c * 32 + 0)
            emit_diff(c * 32 + 1)
            emit_exp(c * 32 + 0)
            issue_rbc(c * 32 + 4)
            for j in range(32):
                gj = c * 32 + j
                par = j % 2
                par3 = gj % 3
                if j < 30:
                    emit_diff(gj + 2)
                if j < 31:
                    issue_rbc(gj + 5)
                    emit_exp(gj + 1)
                VE.wait_ge(ae.h, exp_done[gj])
                if gj >= 2:
                    VE.wait_ge(pe.h, mm_done[gj - 2])
                if j % 8 == 0:
                    PO.wait_ge(gd.h, eab_tgt[(c, j // 8)])
                if gj >= 2:
                    PO.wait_ge(pe.h, mm_done[gj - 2])
                if j == 0:
                    PO.wait_ge(ve.h, cqdup_tgt)
                PO.tensor_tensor(
                    cdbuf[:, par * 512: par * 512 + 2 * CH],
                    CqDup[:, :],
                    EAB[:, (((2 * j) // 16) % 2) * 16 * CH + ((2 * j) % 16) * CH:
                         (((2 * j) // 16) % 2) * 16 * CH + ((2 * j) % 16 + 2) * CH],
                    OP.mult)
                cd_done[gj] = po.inc(PO.tensor_copy(fenceA[:, 2:3],
                                                    cdbuf[:, par * 512 + 2 * CH - 1: par * 512 + 2 * CH]))
                lastt = None
                for hh in range(2):
                    h = 2 * j + hh
                    lastt = VE.tensor_tensor(
                        tbuf[:, par * 1024 + 2 * hh * CH: par * 1024 + (2 * hh + 2) * CH],
                        ebuf[:, par3 * 1024 + 2 * hh * CH: par3 * 1024 + (2 * hh + 2) * CH],
                        Gm[:, 0:2 * CH], OP.mult)
                t_done[gj] = ve.inc(lastt)
                for hh in range(2):
                    h = 2 * j + hh
                    for sb2 in range(2):
                        lasto = VE.tensor_scalar(
                            xdbuf[:, par * 256 + sb2 * 128 + hh * 64: par * 256 + sb2 * 128 + hh * 64 + 64],
                            xT[bufc][:, sb2 * DIN + h * 64: sb2 * DIN + h * 64 + 64],
                            ddtT[:, sb2 * 64 + h: sb2 * 64 + h + 1], None, OP.mult)
                ops_done[gj] = ve.inc(lasto)
                if j % 8 == 7:
                    q = j // 8
                    eab_user[q % 2] = ops_done[gj]
                    if q + 2 <= 3:
                        issue_eab(c, q + 2)
                    else:
                        issue_eab(c + 1, q + 2 - 4)
                if c + 1 < NCH:
                    emit_transposes(c + 1, 3)
                TE.wait_ge(ve.h, ops_done[gj])
                TE.wait_ge(po.h, cd_done[gj])
                if gj >= 2:
                    TE.wait_ge(ve.h, yevac_done[gj - 2])
                    TE.wait_ge(ve.h, supd_done[gj - 2])
                lastmm = None
                for hh in range(2):
                    h = 2 * j + hh
                    yr = pY[par][hh * 64:(hh + 1) * 64, :]
                    TE.matmul(yr, xT[bufc][:, 0 * DIN + h * 64: 0 * DIN + h * 64 + 64],
                              tbuf[:, par * 1024 + (2 * hh) * CH: par * 1024 + (2 * hh) * CH + CH],
                              start=True, stop=False)
                    TE.matmul(yr, xT[bufc][:, 1 * DIN + h * 64: 1 * DIN + h * 64 + 64],
                              tbuf[:, par * 1024 + (2 * hh + 1) * CH: par * 1024 + (2 * hh + 1) * CH + CH],
                              start=False, stop=False)
                    TE.matmul(yr, Sbprev[:, h * 64:(h + 1) * 64],
                              cdbuf[:, par * 512 + hh * CH: par * 512 + (hh + 1) * CH],
                              start=False, stop=True)
                    sr = pS[par][:, hh * 64:(hh + 1) * 64]
                    TE.matmul(sr, BT[bufc][:, 0:128],
                              xdbuf[:, par * 256 + 0 * 128 + hh * 64: par * 256 + 0 * 128 + hh * 64 + 64],
                              start=True, stop=False)
                    lastmm = TE.matmul(sr, BT[bufc][:, 128:256],
                                       xdbuf[:, par * 256 + 1 * 128 + hh * 64: par * 256 + 1 * 128 + hh * 64 + 64],
                                       start=False, stop=True)
                mm_done[gj] = pe.inc(lastmm)

                def emit_evac(gje):
                    ce, je = divmod(gje, 32)
                    pare = je % 2
                    VE.wait_ge(pe.h, mm_done[gje])
                    if je % 8 == 0:
                        prevq = (ce, je // 8 - 1) if je >= 8 else (ce - 1, 3)
                        if prevq in yspill_tgt:
                            VE.wait_ge(ld.h, yspill_tgt[prevq])
                    ve.inc(VE.tensor_copy(Yc[:, (je % 8) * CH:((je % 8) + 1) * CH], pY[pare][:, :]))
                    yevac_done[gje] = ve.n
                    if je % 8 == 7:
                        hf = je // 8
                        SY.wait_ge(ve.h, yevac_done[gje])
                        yspill_tgt[(ce, hf)] = ld.dinc(SY.dma_start(
                            out=y_sp.rearrange("(q p) t -> p q t", p=128)[:, hf * 8:(hf + 1) * 8,
                                                                         ce * CH:ce * CH + CH],
                            in_=Yc.rearrange("p (q l) -> p q l", l=CH)))
                    h0 = 2 * je
                    for hh in range(2):
                        h = h0 + hh
                        VE.tensor_scalar(Snext[:, h * 64:(h + 1) * 64], Sprev[:, h * 64:(h + 1) * 64],
                                         cdB[:, h:h + 1], None, OP.mult)
                    VE.tensor_tensor(Snext[:, h0 * 64:h0 * 64 + 128], Snext[:, h0 * 64:h0 * 64 + 128],
                                     pS[pare][:, :], OP.add)
                    supd_done[gje] = ve.inc(VE.tensor_copy(Sbnext[:, h0 * 64:h0 * 64 + 128],
                                                           Snext[:, h0 * 64:h0 * 64 + 128]))

                if j >= 1:
                    emit_evac(gj - 1)
                if j == 31:
                    emit_evac(gj)
                if PARANOID == 1:
                    for eng in (TE, VE, SE, PO, SY):
                        eng.wait_ge(ve.h, ve.n)
                        eng.wait_ge(pe.h, pe.n)
                        eng.wait_ge(ae.h, ae.n)
                        eng.wait_ge(ld.h, ld.n)
                        eng.wait_ge(sd.h, sd.n)
                        eng.wait_ge(gd.h, gd.n)
                elif PARANOID == 2:
                    for eng in (TE, VE, SE):
                        eng.wait_ge(ve.h, ve.n)
                        eng.wait_ge(pe.h, pe.n)
                        eng.wait_ge(ae.h, ae.n)
                elif PARANOID == 3:
                    for eng in (TE, VE, SE):
                        eng.wait_ge(sd.h, sd.n)
                        eng.wait_ge(gd.h, gd.n)
                        eng.wait_ge(ld.h, ld.n)
                elif PARANOID == 4:
                    for eng in (TE, VE, SE):
                        eng.wait_ge(sd.h, sd.n)
                elif PARANOID == 5:
                    for eng in (TE, VE, SE):
                        eng.wait_ge(gd.h, gd.n)
                elif PARANOID == 6:
                    for eng in (TE, VE, SE):
                        eng.wait_ge(ld.h, ld.n)
            if c + 1 < NCH:
                emit_transposes(c + 1, 66)
        SY.wait_ge(ld.h, ld.n)
        if DEBUG:
            SY.wait_ge(ve.h, ve.n)
            ld.dinc(SY.dma_start(out=taps["tap_acs"], in_=acs[0:64, :]))
            ld.dinc(SY.dma_start(out=taps["tap_y"], in_=y_sp))
            SY.wait_ge(ld.h, ld.n)

    def phaseD(scope, Ygn, rstdB):
        yb = scope.enter_context(nc.sbuf_tensor("ybD", [128, 2 * T], BF16))
        zb = scope.enter_context(nc.sbuf_tensor("zbD", [128, 2 * T], BF16))
        xb = scope.enter_context(nc.sbuf_tensor("xbD", [128, 2 * T], BF16))
        szb = scope.enter_context(nc.sbuf_tensor("szbD", [128, 2 * T], BF16))
        sqb = scope.enter_context(nc.sbuf_tensor("sqbD", [128, 2 * T], BF16))
        rowD = scope.enter_context(nc.sbuf_tensor("rowD", [1, T], F32))
        pm = [scope.enter_context(nc.psum_tensor(f"pmD{i}", [1, 512], F32)) for i in range(4)]

        ld_tgt, dve_tgt, sq_tgt, mm_tgt, sz_tgt = {}, {}, {}, {}, {}
        for fc in range(32):
            slot = fc % 2
            if fc >= 2:
                SY.wait_ge(ve.h, dve_tgt[fc - 2])
                SY.wait_ge(ae.h, sz_tgt[fc - 2])
            SY.dma_start(out=yb[:, slot * T:slot * T + T],
                         in_=y_sp[128 * fc:128 * fc + 128, :]).then_inc(ld.h, 16)
            SY.dma_start(out=zb[:, slot * T:slot * T + T],
                         in_=z_sp[128 * fc:128 * fc + 128, :]).then_inc(ld.h, 16)
            SY.dma_start(out=xb[:, slot * T:slot * T + T],
                         in_=x_sp[128 * fc:128 * fc + 128, :]).then_inc(ld.h, 16)
            ld.n += 48
            ld_tgt[fc] = ld.n
            SE.wait_ge(ld.h, ld_tgt[fc])
            if fc >= 2:
                SE.wait_ge(ve.h, dve_tgt[fc - 2])
            sz_tgt[fc] = ae.inc(SE.activation(szb[:, slot * T:slot * T + T],
                                              zb[:, slot * T:slot * T + T], AF.Silu))
            VE.wait_ge(ld.h, ld_tgt[fc])
            VE.wait_ge(ae.h, sz_tgt[fc])
            VE.scalar_tensor_tensor(Ygn[:, fc * T:(fc + 1) * T], xb[:, slot * T:slot * T + T],
                                    DExp[:, fc:fc + 1], yb[:, slot * T:slot * T + T],
                                    OP.mult, OP.add)
            dve_tgt[fc] = ve.inc(VE.tensor_tensor(Ygn[:, fc * T:(fc + 1) * T],
                                                  Ygn[:, fc * T:(fc + 1) * T],
                                                  szb[:, slot * T:slot * T + T], OP.mult))
            if fc >= 2:
                SE.wait_ge(pe.h, mm_tgt[fc - 2])
            SE.wait_ge(ve.h, dve_tgt[fc])
            sq_tgt[fc] = ae.inc(SE.activation(sqb[:, slot * T:slot * T + T],
                                              Ygn[:, fc * T:(fc + 1) * T], AF.Square))
            ve.inc(VE.tensor_scalar(Ygn[:, fc * T:(fc + 1) * T], Ygn[:, fc * T:(fc + 1) * T],
                                    nwcols[:, fc:fc + 1], None, OP.mult))
            TE.wait_ge(ae.h, sq_tgt[fc])
            lastq = None
            for t in range(NT):
                lastq = TE.matmul(pm[t][0:1, :], onesb[:, 0:1],
                                  sqb[:, slot * T + 512 * t: slot * T + 512 * t + 512],
                                  start=(fc == 0), stop=(fc == 31))
            mm_tgt[fc] = pe.inc(lastq)
        VE.wait_ge(pe.h, pe.n)
        lastr = None
        for t in range(NT):
            lastr = VE.tensor_scalar(rowD[0:1, 512 * t:512 * t + 512], pm[t][0:1, :],
                                     1.0 / DIN, EPS, OP.mult, OP.add)
        ve.inc(lastr)
        SE.wait_ge(ve.h, ve.n)
        ae.inc(SE.activation(rowD[0:1, :], rowD[0:1, :], AF.Sqrt))
        VE.wait_ge(ae.h, ae.n)
        ve.inc(VE.reciprocal(rowD[0:1, :], rowD[0:1, :]))
        SY.wait_ge(ve.h, ve.n)
        ld.dinc(SY.dma_start(out=row_sp[2:3, :], in_=rowD[0:1, :]))
        PO.wait_ge(ld.h, ld.n)
        for _rep in range(2):
            gd.dinc(PO.dma_start(out=rstdB[:, :], in_=bass.AP(row_sp.tensor, 2 * T, [[0, 128], [1, T]])))
        VE.wait_ge(gd.h, gd.n)
        SY.wait_ge(ld.h, ld.n)

    def phaseE(scope, Ygn, rstdB):
        wo = scope.enter_context(nc.sbuf_tensor("woE", [128, 2 * 32 * 128], BF16))
        mstage = scope.enter_context(nc.sbuf_tensor("mstageE", [128, 2 * T], BF16))
        sqb = scope.enter_context(nc.sbuf_tensor("sqbE", [128, 2 * 512], BF16))
        statacc = scope.enter_context(nc.sbuf_tensor("staccE", [1, 2 * T], F32))
        pE = [scope.enter_context(nc.psum_tensor(f"pE{i}", [128, 512], F32)) for i in range(4)]
        pstA = scope.enter_context(nc.psum_tensor("pstAE", [1, 512], F32))
        pstB = scope.enter_context(nc.psum_tensor("pstBE", [1, 512], F32))

        ve.inc(VE.memset(statacc[0:1, :], 0.0))
        slab_tgt, mm_tgt, ev_tgt, st_tgt, sadd_tgt, spl_tgt = {}, {}, {}, {}, {}, {}

        def issue_wo(db):
            half = (db % 2) * 32 * 128
            if db >= 2:
                PO.wait_ge(pe.h, mm_tgt[(db - 2, NT - 1)])
            gd.dinc(PO.dma_start(
                out=wo[:, half:half + 32 * 128].rearrange("p (kc m) -> p kc m", m=128),
                in_=d_wout[:, 128 * db:128 * db + 128].rearrange("(kc p) m -> p kc m", p=128)))
            slab_tgt[db] = gd.n

        issue_wo(0)
        issue_wo(1)
        for db in range(DB):
            half = (db % 2) * 32 * 128
            TE.wait_ge(gd.h, slab_tgt[db])
            for t in range(NT):
                if db >= 1:
                    TE.wait_ge(ve.h, ev_tgt[(db - 1, t)])
                lastm = None
                for kc in range(32):
                    lastm = TE.matmul(pE[t][:, :], wo[:, half + kc * 128: half + (kc + 1) * 128],
                                      Ygn[:, kc * T + 512 * t: kc * T + 512 * t + 512],
                                      start=(kc == 0), stop=(kc == 31))
                mm_tgt[(db, t)] = pe.inc(lastm)
            slot = db % 2
            for t in range(NT):
                VE.wait_ge(pe.h, mm_tgt[(db, t)])
                if db >= 2:
                    VE.wait_ge(ld.h, spl_tgt[db - 2])
                ev_tgt[(db, t)] = ve.inc(VE.tensor_tensor(
                    mstage[:, slot * T + 512 * t: slot * T + 512 * t + 512],
                    pE[t][:, :], rstdB[:, 512 * t:512 * t + 512], OP.mult))
                SE.wait_ge(ve.h, ev_tgt[(db, t)])
                if t >= 2:
                    SE.wait_ge(pe.h, st_tgt[(db, t - 2)])
                elif db >= 1:
                    SE.wait_ge(pe.h, st_tgt[(db - 1, t + 2)])
                ae.inc(SE.activation(sqb[:, (t % 2) * 512:(t % 2) * 512 + 512],
                                     mstage[:, slot * T + 512 * t: slot * T + 512 * t + 512],
                                     AF.Square))
                TE.wait_ge(ae.h, ae.n)
                if db >= 1 or t >= 1:
                    TE.wait_ge(ve.h, sadd_tgt[(db, t - 1) if t >= 1 else (db - 1, NT - 1)])
                TE.matmul(pstA[0:1, :], onesb[:, 0:1],
                          mstage[:, slot * T + 512 * t: slot * T + 512 * t + 512],
                          start=True, stop=True)
                st_tgt[(db, t)] = pe.inc(TE.matmul(pstB[0:1, :], onesb[:, 0:1],
                                                   sqb[:, (t % 2) * 512:(t % 2) * 512 + 512],
                                                   start=True, stop=True))
                VE.wait_ge(pe.h, st_tgt[(db, t)])
                VE.tensor_tensor(statacc[0:1, 512 * t:512 * t + 512],
                                 statacc[0:1, 512 * t:512 * t + 512], pstA[0:1, :], OP.add)
                sadd_tgt[(db, t)] = ve.inc(VE.tensor_tensor(
                    statacc[0:1, T + 512 * t:T + 512 * t + 512],
                    statacc[0:1, T + 512 * t:T + 512 * t + 512], pstB[0:1, :], OP.add))
            SY.wait_ge(ve.h, ev_tgt[(db, NT - 1)])
            spl_tgt[db] = ld.dinc(SY.dma_start(out=mix_sp[128 * db:128 * db + 128, :],
                                               in_=mstage[:, slot * T:slot * T + T]))
            if db + 2 < DB:
                issue_wo(db + 2)
        VE.wait_ge(ve.h, ve.n)
        VE.tensor_scalar(statacc[0:1, 0:T], statacc[0:1, 0:T], 1.0 / DM, None, OP.mult)
        VE.tensor_scalar(statacc[0:1, T:2 * T], statacc[0:1, T:2 * T], 1.0 / DM, None, OP.mult)
        VE.tensor_tensor(sqb[0:1, 0:512].bitcast(F32) if False else rowscratchE[0:1, 0:T],
                         statacc[0:1, 0:T], statacc[0:1, 0:T], OP.mult)
        VE.tensor_tensor(statacc[0:1, T:2 * T], statacc[0:1, T:2 * T], rowscratchE[0:1, 0:T], OP.subtract)
        ve.inc(VE.tensor_scalar(statacc[0:1, T:2 * T], statacc[0:1, T:2 * T], EPS, None, OP.add))
        SE.wait_ge(ve.h, ve.n)
        ae.inc(SE.activation(statacc[0:1, T:2 * T], statacc[0:1, T:2 * T], AF.Sqrt))
        VE.wait_ge(ae.h, ae.n)
        ve.inc(VE.reciprocal(statacc[0:1, T:2 * T], statacc[0:1, T:2 * T]))
        SY.wait_ge(ve.h, ve.n)
        ld.dinc(SY.dma_start(out=row_sp.rearrange("r t -> (r t)")[3 * T:5 * T].unsqueeze(0),
                             in_=statacc[0:1, :]))
        SY.wait_ge(ld.h, ld.n)
        if DEBUG:
            ld.dinc(SY.dma_start(out=taps["tap_mix"], in_=mix_sp))
            SY.wait_ge(ld.h, ld.n)

    def phaseF(scope, xn1):
        x1b = scope.enter_context(nc.sbuf_tensor("x1bF", [128, KC * T], BF16))
        mixb = scope.enter_context(nc.sbuf_tensor("mixbF", [128, 2 * T], BF16))
        ub = scope.enter_context(nc.sbuf_tensor("ubF", [128, 2 * T], F32))
        x1st = scope.enter_context(nc.sbuf_tensor("x1stF", [128, 2 * T], F32))
        mBf = scope.enter_context(nc.sbuf_tensor("mBf", [128, T], BF16))
        rBf = scope.enter_context(nc.sbuf_tensor("rBf", [128, T], BF16))
        pm = [scope.enter_context(nc.psum_tensor(f"pmF{i}", [1, 512], F32)) for i in range(4)]
        ps = [scope.enter_context(nc.psum_tensor(f"psF{i}", [1, 512], F32)) for i in range(4)]

        PO.wait_ge(ld.h, ld.n)
        for _rep in range(2):
            gd.dinc(PO.dma_start(out=mBf[:, :], in_=bass.AP(row_sp.tensor, 3 * T, [[0, 128], [1, T]])))
            gd.dinc(PO.dma_start(out=rBf[:, :], in_=bass.AP(row_sp.tensor, 4 * T, [[0, 128], [1, T]])))
        VE.wait_ge(gd.h, gd.n)
        ld_tgt, dve_tgt, sq_tgt, mm_tgt, spl_tgt = {}, {}, {}, {}, {}
        for dc in range(KC):
            slot = dc % 2
            if dc >= 2:
                SY.wait_ge(ve.h, dve_tgt[dc - 2])
                SY.wait_ge(ld.h, spl_tgt[dc - 2])
            SY.dma_start(out=mixb[:, slot * T:slot * T + T],
                         in_=mix_sp[128 * dc:128 * dc + 128, :]).then_inc(ld.h, 16)
            SY.dma_start(out=ub[:, slot * T:slot * T + T],
                         in_=d_uT[128 * dc:128 * dc + 128, :]).then_inc(ld.h, 16)
            ld.n += 32
            ld_tgt[dc] = ld.n
            VE.wait_ge(ld.h, ld_tgt[dc])
            VE.tensor_tensor(mixb[:, slot * T:slot * T + T], mixb[:, slot * T:slot * T + T],
                             mBf[:, :], OP.subtract)
            VE.tensor_tensor(mixb[:, slot * T:slot * T + T], mixb[:, slot * T:slot * T + T],
                             rBf[:, :], OP.mult)
            VE.tensor_tensor(x1st[:, slot * T:slot * T + T], ub[:, slot * T:slot * T + T],
                             mixb[:, slot * T:slot * T + T], OP.add)
            dve_tgt[dc] = ve.inc(VE.tensor_copy(x1b[:, dc * T:(dc + 1) * T],
                                                x1st[:, slot * T:slot * T + T]))
            SY.wait_ge(ve.h, dve_tgt[dc])
            spl_tgt[dc] = ld.dinc(SY.dma_start(out=x1_sp[128 * dc:128 * dc + 128, :],
                                               in_=x1st[:, slot * T:slot * T + T]))
            if dc >= 2:
                SE.wait_ge(pe.h, mm_tgt[dc - 2])
            SE.wait_ge(ve.h, dve_tgt[dc])
            sq_tgt[dc] = ae.inc(SE.activation(mixb[:, slot * T:slot * T + T],
                                              x1b[:, dc * T:(dc + 1) * T], AF.Square))
            TE.wait_ge(ae.h, sq_tgt[dc])
            lastq = None
            for t in range(NT):
                TE.matmul(pm[t][0:1, :], onesb[:, 0:1],
                          x1b[:, dc * T + 512 * t: dc * T + 512 * t + 512],
                          start=(dc == 0), stop=(dc == KC - 1))
                lastq = TE.matmul(ps[t][0:1, :], onesb[:, 0:1],
                                  mixb[:, slot * T + 512 * t: slot * T + 512 * t + 512],
                                  start=(dc == 0), stop=(dc == KC - 1))
            mm_tgt[dc] = pe.inc(lastq)
        VE.wait_ge(pe.h, pe.n)
        rows = x1st[0:1, 0:2 * T]
        for t in range(NT):
            VE.tensor_scalar(rows[0:1, 512 * t:512 * t + 512], pm[t][0:1, :], 1.0 / DM, None, OP.mult)
            ve.inc(VE.tensor_scalar(rows[0:1, T + 512 * t:T + 512 * t + 512], ps[t][0:1, :],
                                    1.0 / DM, None, OP.mult))
        VE.tensor_tensor(ub[0:1, 0:T], rows[0:1, 0:T], rows[0:1, 0:T], OP.mult)
        VE.tensor_tensor(rows[0:1, T:2 * T], rows[0:1, T:2 * T], ub[0:1, 0:T], OP.subtract)
        ve.inc(VE.tensor_scalar(rows[0:1, T:2 * T], rows[0:1, T:2 * T], EPS, None, OP.add))
        SE.wait_ge(ve.h, ve.n)
        ae.inc(SE.activation(rows[0:1, T:2 * T], rows[0:1, T:2 * T], AF.Sqrt))
        VE.wait_ge(ae.h, ae.n)
        ve.inc(VE.reciprocal(rows[0:1, T:2 * T], rows[0:1, T:2 * T]))
        SY.wait_ge(ve.h, ve.n)
        ld.dinc(SY.dma_start(out=row_sp.rearrange("r t -> (r t)")[5 * T:7 * T].unsqueeze(0),
                             in_=rows[0:1, :]))
        PO.wait_ge(ld.h, ld.n)
        for _rep in range(2):
            gd.dinc(PO.dma_start(out=mBf[:, :], in_=bass.AP(row_sp.tensor, 5 * T, [[0, 128], [1, T]])))
            gd.dinc(PO.dma_start(out=rBf[:, :], in_=bass.AP(row_sp.tensor, 6 * T, [[0, 128], [1, T]])))
        VE.wait_ge(gd.h, gd.n)
        for dc in range(KC):
            VE.tensor_tensor(xn1[:, dc * T:(dc + 1) * T], x1b[:, dc * T:(dc + 1) * T],
                             mBf[:, :], OP.subtract)
            ve.inc(VE.tensor_tensor(xn1[:, dc * T:(dc + 1) * T], xn1[:, dc * T:(dc + 1) * T],
                                    rBf[:, :], OP.mult))
        if DEBUG:
            SY.wait_ge(ve.h, ve.n)
            ld.dinc(SY.dma_start(out=taps["tap_x1"], in_=x1_sp))
        SY.wait_ge(ld.h, ld.n)

    def phaseG(scope, xn1):
        wg = scope.enter_context(nc.sbuf_tensor("wgG", [128, 2 * KC * 128], BF16))
        wu = scope.enter_context(nc.sbuf_tensor("wuG", [128, 2 * KC * 128], BF16))
        sg = scope.enter_context(nc.sbuf_tensor("sgG", [128, 2 * 512], BF16))
        hstage = scope.enter_context(nc.sbuf_tensor("hstageG", [128, 2 * T], BF16))
        pGt = [scope.enter_context(nc.psum_tensor(f"pGt{i}", [128, 512], F32)) for i in range(4)]
        pUt = [scope.enter_context(nc.psum_tensor(f"pUt{i}", [128, 512], F32)) for i in range(4)]

        SY.wait_ge(cc.h, 1)
        slab_tgt, mm_tgt, h_tgt, spl_tgt, sg_tgt = {}, {}, {}, {}, {}

        def issue_gu(ob):
            half = (ob % 2) * KC * 128
            if ob >= 2:
                SY.wait_ge(pe.h, mm_tgt[(ob - 2, NT - 1)])
            SY.dma_start(
                out=wg[:, half:half + KC * 128].rearrange("p (kc m) -> p kc m", m=128),
                in_=w_full[(ob * KC) * TILE_EL:(ob * KC + KC) * TILE_EL]
                    .rearrange("(kc p m) -> p kc m", p=128, m=128)).then_inc(ld.h, 16)
            SY.dma_start(
                out=wu[:, half:half + KC * 128].rearrange("p (kc m) -> p kc m", m=128),
                in_=w_full[(GATE_TILES + ob * KC) * TILE_EL:(GATE_TILES + ob * KC + KC) * TILE_EL]
                    .rearrange("(kc p m) -> p kc m", p=128, m=128)).then_inc(ld.h, 16)
            ld.n += 32
            slab_tgt[ob] = ld.n

        issue_gu(0)
        issue_gu(1)
        for ob in range(OB):
            half = (ob % 2) * KC * 128
            slot = ob % 2
            TE.wait_ge(ld.h, slab_tgt[ob])
            for t in range(NT):
                if ob >= 1:
                    TE.wait_ge(ve.h, h_tgt[(ob - 1, t)])
                for kc in range(KC):
                    TE.matmul(pGt[t][:, :], wg[:, half + kc * 128: half + (kc + 1) * 128],
                              xn1[:, kc * T + 512 * t: kc * T + 512 * t + 512],
                              start=(kc == 0), stop=(kc == KC - 1))
                lastm = None
                for kc in range(KC):
                    lastm = TE.matmul(pUt[t][:, :], wu[:, half + kc * 128: half + (kc + 1) * 128],
                                      xn1[:, kc * T + 512 * t: kc * T + 512 * t + 512],
                                      start=(kc == 0), stop=(kc == KC - 1))
                mm_tgt[(ob, t)] = pe.inc(lastm)
            for t in range(NT):
                SE.wait_ge(pe.h, mm_tgt[(ob, t)])
                if t >= 2:
                    SE.wait_ge(ve.h, h_tgt[(ob, t - 2)])
                elif ob >= 1:
                    SE.wait_ge(ve.h, h_tgt[(ob - 1, t + 2)])
                sg_tgt[(ob, t)] = ae.inc(SE.activation(sg[:, (t % 2) * 512:(t % 2) * 512 + 512],
                                                       pGt[t][:, :], AF.Silu))
                VE.wait_ge(ae.h, sg_tgt[(ob, t)])
                if ob >= 2:
                    VE.wait_ge(ld.h, spl_tgt[ob - 2])
                h_tgt[(ob, t)] = ve.inc(VE.tensor_tensor(
                    hstage[:, slot * T + 512 * t: slot * T + 512 * t + 512],
                    sg[:, (t % 2) * 512:(t % 2) * 512 + 512], pUt[t][:, :], OP.mult))
            SY.wait_ge(ve.h, h_tgt[(ob, NT - 1)])
            spl_tgt[ob] = ld.dinc(SY.dma_start(out=h_sp[128 * ob:128 * ob + 128, :],
                                               in_=hstage[:, slot * T:slot * T + T]))
            if ob + 2 < OB:
                issue_gu(ob + 2)
        SY.wait_ge(ld.h, ld.n)
        if DEBUG:
            ld.dinc(SY.dma_start(out=taps["tap_h"], in_=h_sp))
            SY.wait_ge(ld.h, ld.n)

    def phaseH(scope):
        hT = scope.enter_context(nc.sbuf_tensor("hTH", [128, KO * 1024], BF16))
        wd = scope.enter_context(nc.sbuf_tensor("wdH", [128, 2 * KO * 128], BF16))
        fstage = scope.enter_context(nc.sbuf_tensor("fstageH", [128, 2 * 1024], BF16))
        sqb = scope.enter_context(nc.sbuf_tensor("sqbH", [128, 2 * 512], BF16))
        statacc = scope.enter_context(nc.sbuf_tensor("staccH", [1, 2 * T], F32))
        rscr = scope.enter_context(nc.sbuf_tensor("rscrH", [1, T], F32))
        pH = [scope.enter_context(nc.psum_tensor(f"pH{i}", [128, 512], F32)) for i in range(4)]
        pstA = scope.enter_context(nc.psum_tensor("pstAH", [1, 512], F32))
        pstB = scope.enter_context(nc.psum_tensor("pstBH", [1, 512], F32))

        ve.inc(VE.memset(statacc[0:1, :], 0.0))
        slab_tgt, mm_tgt, ev_tgt, st_tgt, sadd_tgt, spl_tgt, ht_tgt = {}, {}, {}, {}, {}, {}, {}

        def issue_wd(th, db):
            half = (db % 2) * KO * 128
            if db >= 2:
                PO.wait_ge(pe.h, mm_tgt[(th, db - 2)][1])
            elif th >= 1:
                PO.wait_ge(pe.h, mm_tgt[(th - 1, DB - 2 + db)][1])
            gd.dinc(PO.dma_start(
                out=wd[:, half:half + KO * 128].rearrange("p (kc m) -> p kc m", m=128),
                in_=w_full[(2 * GATE_TILES + db * KO) * TILE_EL:(2 * GATE_TILES + (db + 1) * KO) * TILE_EL]
                    .rearrange("(kc p m) -> p kc m", p=128, m=128)))
            slab_tgt[(th, db)] = gd.n

        for th in range(2):
            if th >= 1:
                SY.wait_ge(pe.h, mm_tgt[(0, DB - 1)][1])
            ht_tgt[th] = ld.dinc(SY.dma_start(
                out=hT[:, :].rearrange("p (kc q) -> p kc q", q=1024),
                in_=h_sp.rearrange("(kc p) t -> p kc t", p=128)[:, :, th * 1024:(th + 1) * 1024]))
            issue_wd(th, 0)
            issue_wd(th, 1)
            for db in range(DB):
                half = (db % 2) * KO * 128
                TE.wait_ge(gd.h, slab_tgt[(th, db)])
                TE.wait_ge(ld.h, ht_tgt[th])
                lastm = None
                for tq in range(2):
                    bank = pH[(db % 2) * 2 + tq]
                    if db >= 2:
                        TE.wait_ge(ae.h, ev_tgt[(th, db - 2, tq)])
                    elif th >= 1 and db < 2:
                        TE.wait_ge(ae.h, ev_tgt[(th - 1, DB - 2 + db, tq)])
                    for kc in range(KO):
                        lastm = TE.matmul(bank[:, :], wd[:, half + kc * 128: half + (kc + 1) * 128],
                                          hT[:, kc * 1024 + tq * 512: kc * 1024 + tq * 512 + 512],
                                          start=(kc == 0), stop=(kc == KO - 1))
                mm_tgt[(th, db)] = (None, pe.inc(lastm))
                slot = db % 2
                for tq in range(2):
                    bank = pH[(db % 2) * 2 + tq]
                    SE.wait_ge(pe.h, mm_tgt[(th, db)][1])
                    if db >= 2:
                        SE.wait_ge(ld.h, spl_tgt[(th, db - 2)])
                    elif th >= 1:
                        SE.wait_ge(ld.h, spl_tgt[(th - 1, DB - 2 + db)])
                    ev_tgt[(th, db, tq)] = ae.inc(SE.copy(
                        fstage[:, slot * 1024 + tq * 512: slot * 1024 + tq * 512 + 512], bank[:, :]))
                    ae.inc(SE.activation(sqb[:, tq * 512:(tq + 1) * 512],
                                         fstage[:, slot * 1024 + tq * 512: slot * 1024 + tq * 512 + 512],
                                         AF.Square))
                    TE.wait_ge(ae.h, ae.n)
                    if sadd_tgt:
                        TE.wait_ge(ve.h, max(sadd_tgt.values()))
                    TE.matmul(pstA[0:1, :], onesb[:, 0:1],
                              fstage[:, slot * 1024 + tq * 512: slot * 1024 + tq * 512 + 512],
                              start=True, stop=True)
                    stq = pe.inc(TE.matmul(pstB[0:1, :], onesb[:, 0:1],
                                           sqb[:, tq * 512:(tq + 1) * 512],
                                           start=True, stop=True))
                    VE.wait_ge(pe.h, stq)
                    pos = th * 1024 + tq * 512
                    VE.tensor_tensor(statacc[0:1, pos:pos + 512],
                                     statacc[0:1, pos:pos + 512], pstA[0:1, :], OP.add)
                    sadd_tgt[(th, db, tq)] = ve.inc(VE.tensor_tensor(
                        statacc[0:1, T + pos:T + pos + 512],
                        statacc[0:1, T + pos:T + pos + 512], pstB[0:1, :], OP.add))
                SY.wait_ge(ae.h, ev_tgt[(th, db, 1)])
                spl_tgt[(th, db)] = ld.dinc(SY.dma_start(
                    out=ffn_sp[128 * db:128 * db + 128, th * 1024:(th + 1) * 1024],
                    in_=fstage[:, slot * 1024: slot * 1024 + 1024]))
                if db + 2 < DB:
                    issue_wd(th, db + 2)
                elif th == 0 and db >= DB - 2:
                    issue_wd(1, db - (DB - 2))
        VE.wait_ge(ve.h, ve.n)
        VE.tensor_scalar(statacc[0:1, 0:T], statacc[0:1, 0:T], 1.0 / DM, None, OP.mult)
        VE.tensor_scalar(statacc[0:1, T:2 * T], statacc[0:1, T:2 * T], 1.0 / DM, None, OP.mult)
        VE.tensor_tensor(rscr[0:1, :], statacc[0:1, 0:T], statacc[0:1, 0:T], OP.mult)
        VE.tensor_tensor(statacc[0:1, T:2 * T], statacc[0:1, T:2 * T], rscr[0:1, :], OP.subtract)
        ve.inc(VE.tensor_scalar(statacc[0:1, T:2 * T], statacc[0:1, T:2 * T], EPS, None, OP.add))
        SE.wait_ge(ve.h, ve.n)
        ae.inc(SE.activation(statacc[0:1, T:2 * T], statacc[0:1, T:2 * T], AF.Sqrt))
        VE.wait_ge(ae.h, ae.n)
        ve.inc(VE.reciprocal(statacc[0:1, T:2 * T], statacc[0:1, T:2 * T]))
        SY.wait_ge(ve.h, ve.n)
        ld.dinc(SY.dma_start(out=row_sp.rearrange("r t -> (r t)")[7 * T:9 * T].unsqueeze(0),
                             in_=statacc[0:1, :]))
        SY.wait_ge(ld.h, ld.n)
        if DEBUG:
            ld.dinc(SY.dma_start(out=taps["tap_ffn"], in_=ffn_sp))
            ld.dinc(SY.dma_start(out=taps["tap_rows"], in_=row_sp))
            SY.wait_ge(ld.h, ld.n)

    def phaseI(scope):
        fb = scope.enter_context(nc.sbuf_tensor("fbI", [128, 2 * T], BF16))
        x1s = scope.enter_context(nc.sbuf_tensor("x1sI", [128, 2 * T], F32))
        ostage = scope.enter_context(nc.sbuf_tensor("ostageI", [128, 2 * T], F32))
        mBi = scope.enter_context(nc.sbuf_tensor("mBi", [128, T], BF16))
        rBi = scope.enter_context(nc.sbuf_tensor("rBi", [128, T], BF16))

        PO.wait_ge(ld.h, ld.n)
        for _rep in range(2):
            gd.dinc(PO.dma_start(out=mBi[:, :], in_=bass.AP(row_sp.tensor, 7 * T, [[0, 128], [1, T]])))
            gd.dinc(PO.dma_start(out=rBi[:, :], in_=bass.AP(row_sp.tensor, 8 * T, [[0, 128], [1, T]])))
        VE.wait_ge(gd.h, gd.n)
        ld_tgt, dve_tgt, spl_tgt = {}, {}, {}
        for dc in range(KC):
            slot = dc % 2
            if dc >= 2:
                SY.wait_ge(ve.h, dve_tgt[dc - 2])
                SY.wait_ge(ld.h, spl_tgt[dc - 2])
            SY.dma_start(out=fb[:, slot * T:slot * T + T],
                         in_=ffn_sp[128 * dc:128 * dc + 128, :]).then_inc(ld.h, 16)
            SY.dma_start(out=x1s[:, slot * T:slot * T + T],
                         in_=x1_sp[128 * dc:128 * dc + 128, :]).then_inc(ld.h, 16)
            ld.n += 32
            ld_tgt[dc] = ld.n
            VE.wait_ge(ld.h, ld_tgt[dc])
            VE.tensor_tensor(fb[:, slot * T:slot * T + T], fb[:, slot * T:slot * T + T],
                             mBi[:, :], OP.subtract)
            VE.tensor_tensor(fb[:, slot * T:slot * T + T], fb[:, slot * T:slot * T + T],
                             rBi[:, :], OP.mult)
            dve_tgt[dc] = ve.inc(VE.tensor_tensor(ostage[:, slot * T:slot * T + T],
                                                  x1s[:, slot * T:slot * T + T],
                                                  fb[:, slot * T:slot * T + T], OP.add))
            SY.wait_ge(ve.h, dve_tgt[dc])
            spl_tgt[dc] = ld.dinc(SY.dma_start(out=d_y[128 * dc:128 * dc + 128, :],
                                               in_=ostage[:, slot * T:slot * T + T]))
        SY.wait_ge(ld.h, ld.n)
        VE.wait_ge(ld.h, ld.n)

    with contextlib.ExitStack() as s0:
        phase0(s0)
        barrier()
    with contextlib.ExitStack() as sAB:
        mat_alloc(sAB)
        xn = sAB.enter_context(nc.sbuf_tensor("xn", [128, KC * T], BF16))
        with contextlib.ExitStack() as sA:
            phaseA2(sA, xn)
            barrier()
        with contextlib.ExitStack() as sB:
            phaseB(sB, xn)
            barrier()
        mat_finish_and_collective()
        barrier()
    with contextlib.ExitStack() as sC:
        phaseC(sC)
        barrier()
    with contextlib.ExitStack() as sDE:
        Ygn = sDE.enter_context(nc.sbuf_tensor("Ygn", [128, 32 * T], BF16))
        rstdB = sDE.enter_context(nc.sbuf_tensor("rstdB_DE", [128, T], BF16))
        with contextlib.ExitStack() as sD:
            phaseD(sD, Ygn, rstdB)
            barrier()
        with contextlib.ExitStack() as sE:
            rowscratchE_holder = sE.enter_context(nc.sbuf_tensor("rowscrE", [1, T], F32))
            globals()["rowscratchE"] = rowscratchE_holder
            phaseE(sE, Ygn, rstdB)
            barrier()
    with contextlib.ExitStack() as sFGH:
        xn1 = sFGH.enter_context(nc.sbuf_tensor("xn1", [128, KC * T], BF16))
        with contextlib.ExitStack() as sF:
            phaseF(sF, xn1)
            barrier()
        with contextlib.ExitStack() as sG:
            phaseG(sG, xn1)
            barrier()
    with contextlib.ExitStack() as sH:
        phaseH(sH)
        barrier()
    with contextlib.ExitStack() as sI:
        phaseI(sI)
        barrier()
        if DEBUG:
            PO.wait_ge(cc.h, 1)
            gd.dinc(PO.dma_start(out=taps["tap_wfull"], in_=w_full))
            PO.wait_ge(gd.h, gd.n)
            barrier()
    es.close()
    return nc


def make_in_maps(L0_attn_u, L0_attn_in_proj_kernel, L0_attn_conv1d_weight,
                 L0_attn_conv1d_bias, L0_attn_dt_bias, L0_attn_A_log, L0_attn_D,
                 L0_attn_norm_weight, L0_attn_out_proj_kernel, L0_ffn_lut,
                 L0_ffn_gate_idx, L0_ffn_up_idx, L0_ffn_down_idx):
    u = np.asarray(L0_attn_u, dtype=np.float32)
    w_in = np.ascontiguousarray(np.asarray(L0_attn_in_proj_kernel, dtype=np.float32))
    convwT = np.ascontiguousarray(np.asarray(L0_attn_conv1d_weight, dtype=np.float32).T)
    convb = np.ascontiguousarray(np.asarray(L0_attn_conv1d_bias, dtype=np.float32))
    dtb = np.ascontiguousarray(np.asarray(L0_attn_dt_bias, dtype=np.float32))
    alog = np.ascontiguousarray(np.asarray(L0_attn_A_log, dtype=np.float32))
    dvec = np.ascontiguousarray(np.asarray(L0_attn_D, dtype=np.float32))
    nw = np.ascontiguousarray(np.asarray(L0_attn_norm_weight, dtype=np.float32))
    w_out = np.ascontiguousarray(np.asarray(L0_attn_out_proj_kernel, dtype=np.float32))
    lut = np.ascontiguousarray(np.asarray(L0_ffn_lut, dtype=np.float32))
    gT = np.asarray(L0_ffn_gate_idx, dtype=np.int32).T
    upT = np.asarray(L0_ffn_up_idx, dtype=np.int32).T
    dnT = np.asarray(L0_ffn_down_idx, dtype=np.int32).T

    def tiles_a(mm):
        return mm.reshape(KC, 128, OB, 128).transpose(2, 0, 1, 3).reshape(GATE_TILES, 128, 128)

    def tiles_b(mm):
        return mm.reshape(KO, 128, DB, 128).transpose(2, 0, 1, 3).reshape(GATE_TILES, 128, 128)

    all_tiles = np.ascontiguousarray(
        np.concatenate([tiles_a(gT), tiles_a(upT), tiles_b(dnT)], axis=0))

    in_maps = []
    for c in range(NCORES):
        in_maps.append({
            "uT": np.ascontiguousarray(u[c].T),
            "w_in": w_in, "convwT": convwT, "convb": convb, "dt_bias": dtb,
            "a_log": alog, "dvec": dvec, "nw": nw, "w_out": w_out, "lut": lut,
            "idx_sl": np.ascontiguousarray(all_tiles[c * RANK_TILES:(c + 1) * RANK_TILES]),
        })
    return in_maps


def kernel(**inputs):
    in_maps = make_in_maps(**inputs)
    nc = build()
    res = run_bass_kernel_spmd(nc, in_maps, core_ids=list(range(NCORES)))
    return np.stack([np.asarray(r["y"]).T for r in res.results], axis=0)


# version 10
# speedup vs baseline: 1.0209x; 1.0209x over previous
import contextlib

import numpy as np
import ml_dtypes
import concourse.bass as bass
import concourse.mybir as mybir
from concourse.bass_utils import run_bass_kernel_spmd

F32 = mybir.dt.float32
BF16 = mybir.dt.bfloat16
I32 = mybir.dt.int32
AF = mybir.ActivationFunctionType
OP = mybir.AluOpType

DM = 2048
T = 2048
DIN = 4096
NH = 64
HD = 64
DS = 128
CH = 256
NCH = T // CH
CONVD = 4352
DPROJ = 8512
INTER = 5504
NCORES = 8
EPS = 1e-5

MB = 67
KC = 16
OB = 43
DB = 16
KO = 43
NT = 4
TILE_EL = 128 * 128
GATE_TILES = OB * KC
ALL_TILES = 3 * GATE_TILES
RANK_TILES = ALL_TILES // NCORES
MAT_CHUNK = 6
N_MATG = (RANK_TILES + MAT_CHUNK - 1) // MAT_CHUNK

DEBUG = False
PARANOID = False


class S:
    def __init__(self, nc, es, name):
        self.h = es.enter_context(nc.semaphore(name))
        self.n = 0

    def inc(self, instr, k=1):
        instr.then_inc(self.h, k)
        self.n += k
        return self.n

    def dinc(self, instr):
        return self.inc(instr, 16)


def build():
    nc = bass.Bass("TRN2", target_bir_lowering=False, debug=False)

    d_uT = nc.dram_tensor("uT", [DM, T], F32, kind="ExternalInput").ap()
    d_win = nc.dram_tensor("w_in", [DM, DPROJ], F32, kind="ExternalInput").ap()
    d_convwT = nc.dram_tensor("convwT", [4, CONVD], F32, kind="ExternalInput").ap()
    d_convb = nc.dram_tensor("convb", [CONVD], F32, kind="ExternalInput").ap()
    d_dtb = nc.dram_tensor("dt_bias", [NH], F32, kind="ExternalInput").ap()
    d_alog = nc.dram_tensor("a_log", [NH], F32, kind="ExternalInput").ap()
    d_D = nc.dram_tensor("dvec", [NH], F32, kind="ExternalInput").ap()
    d_nw = nc.dram_tensor("nw", [DIN], F32, kind="ExternalInput").ap()
    d_wout = nc.dram_tensor("w_out", [DIN, DM], F32, kind="ExternalInput").ap()
    d_lut = nc.dram_tensor("lut", [16], F32, kind="ExternalInput").ap()
    d_idx = nc.dram_tensor("idx_sl", [RANK_TILES, 128, 128], I32, kind="ExternalInput").ap()

    d_y = nc.dram_tensor("y", [DM, T], F32, kind="ExternalOutput").ap()

    z_sp = nc.dram_tensor("z_sp", [DIN, T], BF16).ap()
    x_sp = nc.dram_tensor("x_sp", [DIN, T], BF16).ap()
    b_sp = nc.dram_tensor("b_sp", [DS, T], BF16).ap()
    y_sp = nc.dram_tensor("y_sp", [DIN, T], BF16).ap()
    acs_sp = nc.dram_tensor("acs_sp", [NH, T], F32).ap()
    eacs_sp = nc.dram_tensor("eacs_sp", [NH, T], BF16).ap()
    mix_sp = nc.dram_tensor("mix_sp", [DM, T], BF16).ap()
    x1_sp = nc.dram_tensor("x1_sp", [DM, T], F32).ap()
    h_sp = nc.dram_tensor("h_sp", [INTER, T], BF16).ap()
    ffn_sp = nc.dram_tensor("ffn_sp", [DM, T], BF16).ap()
    row_sp = nc.dram_tensor("row_sp", [16, T], F32).ap()
    w_sl = nc.dram_tensor("w_sl", [RANK_TILES * TILE_EL], BF16).ap()
    w_full = nc.dram_tensor("w_full", [ALL_TILES * TILE_EL], BF16, addr_space="Shared").ap()

    taps = {}
    if DEBUG:
        for name, shape, dt in [
            ("tap_xn", [128, KC * T], BF16), ("tap_z", [DIN, T], BF16),
            ("tap_x", [DIN, T], BF16), ("tap_b", [DS, T], BF16),
            ("tap_c", [128, T], BF16), ("tap_dt", [NH, T], F32),
            ("tap_acs", [NH, T], F32), ("tap_y", [DIN, T], BF16),
            ("tap_wfull", [ALL_TILES * TILE_EL], BF16),
            ("tap_mix", [DM, T], BF16),
            ("tap_x1", [DM, T], F32), ("tap_h", [INTER, T], BF16),
            ("tap_ffn", [DM, T], BF16), ("tap_rows", [16, T], F32),
        ]:
            taps[name] = nc.dram_tensor(name, shape, dt, kind="ExternalOutput").ap()

    TE, VE, SE, PO, SY = nc.tensor, nc.vector, nc.scalar, nc.gpsimd, nc.sync

    es = contextlib.ExitStack()
    ld = S(nc, es, "ld")
    sd = S(nc, es, "sd")
    gd = S(nc, es, "gd")
    pe = S(nc, es, "pe")
    ve = S(nc, es, "ve")
    ae = S(nc, es, "ae")
    po = S(nc, es, "po")
    cc = S(nc, es, "cc")

    ident = es.enter_context(nc.sbuf_tensor("ident", [128, 128], F32))
    trimask = es.enter_context(nc.sbuf_tensor("trimask", [128, 2 * CH], BF16))
    ones1 = es.enter_context(nc.sbuf_tensor("ones1", [1, 128], F32))
    onesb = es.enter_context(nc.sbuf_tensor("onesb", [128, 1], BF16))
    lutB = es.enter_context(nc.sbuf_tensor("lutB", [128, 16], F32))
    wcols = es.enter_context(nc.sbuf_tensor("wcols", [128, 4 * 34], F32))
    bcols = es.enter_context(nc.sbuf_tensor("bcols", [128, 34], F32))
    nwcols = es.enter_context(nc.sbuf_tensor("nwcols", [128, 32], F32))
    smallrows = es.enter_context(nc.sbuf_tensor("smallrows", [1, 256], F32))
    dtb_col = es.enter_context(nc.sbuf_tensor("dtb_col", [64, 1], F32))
    nA_col = es.enter_context(nc.sbuf_tensor("nA_col", [64, 1], F32))
    DB_sb = es.enter_context(nc.sbuf_tensor("DB_sb", [128, 64], F32))
    DExp = es.enter_context(nc.sbuf_tensor("DExp", [128, 32], F32))

    dt_raw = es.enter_context(nc.sbuf_tensor("dt_raw", [64, T], F32))
    Bq = es.enter_context(nc.sbuf_tensor("Bq", [128, T], BF16))
    Cq = es.enter_context(nc.sbuf_tensor("Cq", [128, T], BF16))

    def barrier():
        nc.all_engine_barrier()

    def phase0(scope):
        scr44 = scope.enter_context(nc.sbuf_tensor("scr44", [4, CONVD], F32))
        scr14 = scope.enter_context(nc.sbuf_tensor("scr14", [1, CONVD], F32))
        nwrow = scope.enter_context(nc.sbuf_tensor("nwrow", [1, DIN], F32))
        io2 = scope.enter_context(nc.sbuf_tensor("io2", [128, 2 * CH], F32))
        pmisc = scope.enter_context(nc.psum_tensor("pmisc0", [128, 128], F32))

        gd.dinc(PO.dma_start(out=lutB[:, :], in_=bass.AP(d_lut.tensor, 0, [[0, 128], [1, 16]])))
        ld.dinc(SY.dma_start(out=scr44[:, :], in_=d_convwT))
        ld.dinc(SY.dma_start(out=scr14[0:1, :], in_=d_convb.unsqueeze(0)))
        ld.dinc(SY.dma_start(out=nwrow[0:1, :], in_=d_nw.unsqueeze(0)))
        ld.dinc(SY.dma_start(out=smallrows[0:1, 0:64], in_=d_dtb.unsqueeze(0)))
        ld.dinc(SY.dma_start(out=smallrows[0:1, 64:128], in_=d_alog.unsqueeze(0)))
        ld.dinc(SY.dma_start(out=smallrows[0:1, 128:192], in_=d_D.unsqueeze(0)))

        po.inc(PO.iota(io2[:, 0:CH], [[1, CH]], channel_multiplier=-1,
                       allow_small_or_imprecise_dtypes=True))
        po.inc(PO.iota(io2[:, CH:2 * CH], [[1, CH]], base=-128, channel_multiplier=-1,
                       allow_small_or_imprecise_dtypes=True))
        VE.wait_ge(po.h, po.n)
        ve.inc(VE.tensor_scalar(trimask[:, 0:CH], io2[:, 0:CH], 0.0, None, OP.is_ge))
        ve.inc(VE.tensor_scalar(trimask[:, CH:2 * CH], io2[:, CH:2 * CH], 0.0, None, OP.is_ge))
        ve.inc(VE.tensor_scalar(ident[:, :], io2[:, 0:128], 0.0, None, OP.is_equal))
        ve.inc(VE.memset(ones1[:, :], 1.0))
        ve.inc(VE.memset(onesb[:, :], 1.0))

        TE.wait_ge(ve.h, ve.n)
        TE.wait_ge(ld.h, ld.n)

        def transpose_group(items):
            col = 0
            plan = []
            for dst_ap, src_ap in items:
                p, f = src_ap.shape
                if col + p > 128:
                    break
                plan.append((dst_ap, src_ap, col, p, f))
                col += p
            for dst_ap, src_ap, c0, p, f in plan:
                pe.inc(TE.transpose(pmisc[0:f, c0:c0 + p], src_ap, ident[0:p, 0:p]))
            SE.wait_ge(pe.h, pe.n)
            for dst_ap, src_ap, c0, p, f in plan:
                ae.inc(SE.copy(dst_ap, pmisc[0:f, c0:c0 + p]))
            TE.wait_ge(ae.h, ae.n)
            return len(plan)

        items = [(wcols[:, 4 * ci:4 * ci + 4], scr44[0:4, 128 * ci:128 * ci + 128]) for ci in range(34)]
        while items:
            items = items[transpose_group(items):]
        items = [(bcols[:, ci:ci + 1], scr14[0:1, 128 * ci:128 * ci + 128]) for ci in range(34)]
        items += [(nwcols[:, ci:ci + 1], nwrow[0:1, 128 * ci:128 * ci + 128]) for ci in range(32)]
        items += [(dtb_col[0:64, 0:1], smallrows[0:1, 0:64]),
                  (nA_col[0:64, 0:1], smallrows[0:1, 64:128])]
        while items:
            items = items[transpose_group(items):]
        SE.wait_ge(ae.h, ae.n)
        ae.inc(SE.activation(nA_col[0:64, 0:1], nA_col[0:64, 0:1], AF.Exp))
        VE.wait_ge(ae.h, ae.n)
        ve.inc(VE.tensor_scalar(nA_col[0:64, 0:1], nA_col[0:64, 0:1], -1.0, None, OP.mult))

        pe.inc(TE.matmul(pmisc[:, 0:64], ones1[0:1, :], smallrows[0:1, 128:192],
                         start=True, stop=True))
        SE.wait_ge(pe.h, pe.n)
        ae.inc(SE.copy(DB_sb[:, 0:64], pmisc[:, 0:64]))
        VE.wait_ge(ae.h, ae.n)
        for fc in range(32):
            VE.tensor_copy(DExp[0:64, fc:fc + 1], DB_sb[0:64, 2 * fc:2 * fc + 1])
            ve.inc(VE.tensor_copy(DExp[64:128, fc:fc + 1], DB_sb[64:128, 2 * fc + 1:2 * fc + 2]))
        VE.wait_ge(gd.h, gd.n)

    mat = {"g": 0, "ld_tgt": [], "sp_tgt": []}

    def mat_alloc(scope):
        mat["idx"] = scope.enter_context(nc.sbuf_tensor("mat_idx", [128, 2 * MAT_CHUNK * 128], BF16))
        mat["acc"] = scope.enter_context(nc.sbuf_tensor("mat_acc", [128, MAT_CHUNK * 128], BF16))
        mat["m"] = scope.enter_context(nc.sbuf_tensor("mat_m", [128, MAT_CHUNK * 128], BF16))
        for g in range(min(2, N_MATG)):
            mat_issue_load(g)

    def mat_tiles(g):
        t0 = g * MAT_CHUNK
        return t0, min(MAT_CHUNK, RANK_TILES - t0)

    def mat_issue_load(g):
        t0, ntl = mat_tiles(g)
        half = (g % 2) * MAT_CHUNK * 128
        gd.dinc(PO.dma_start(
            out=mat["idx"][:, half:half + ntl * 128].rearrange("p (j m) -> p j m", m=128),
            in_=d_idx[t0:t0 + ntl].rearrange("j p m -> p j m")))
        mat["ld_tgt"].append(gd.n)

    def mat_emit_group():
        g = mat["g"]
        if g >= N_MATG:
            return
        mat["g"] += 1
        t0, ntl = mat_tiles(g)
        half = (g % 2) * MAT_CHUNK * 128
        idxv = mat["idx"][:, half:half + ntl * 128]
        acc = mat["acc"][:, 0:ntl * 128]
        m = mat["m"][:, 0:ntl * 128]
        VE.wait_ge(gd.h, mat["ld_tgt"][g])
        if g >= 1:
            VE.wait_ge(ld.h, mat["sp_tgt"][g - 1])
        VE.tensor_scalar(acc, idxv, 0.0, lutB[:, 0:1], OP.is_equal, OP.mult)
        lastm = None
        for k in range(1, 16):
            VE.tensor_scalar(m, idxv, float(k), lutB[:, k:k + 1], OP.is_equal, OP.mult)
            lastm = VE.tensor_tensor(acc, acc, m, OP.add)
        spill_wait = ve.inc(lastm)
        SY.wait_ge(ve.h, spill_wait)
        mat["sp_tgt"].append(ld.dinc(SY.dma_start(
            out=w_sl[t0 * TILE_EL:(t0 + ntl) * TILE_EL].rearrange("(j p m) -> p j m", p=128, m=128),
            in_=acc.rearrange("p (j m) -> p j m", m=128))))
        if g + 2 < N_MATG:
            PO.wait_ge(ve.h, spill_wait)
            mat_issue_load(g + 2)

    def mat_finish_and_collective():
        while mat["g"] < N_MATG:
            mat_emit_group()
        PO.wait_ge(ld.h, ld.n)
        cc.inc(PO.collective_compute(
            "AllGather", OP.bypass,
            replica_groups=[list(range(NCORES))],
            ins=[w_sl[:]], outs=[w_full[:]]))

    def phaseA2(scope, xn):
        uin = scope.enter_context(nc.sbuf_tensor("uin", [128, 2 * T], F32))
        ubf = scope.enter_context(nc.sbuf_tensor("ubf_sl", [128, 2 * T], BF16))
        sqb = scope.enter_context(nc.sbuf_tensor("sqb", [128, 2 * T], BF16))
        mrows = scope.enter_context(nc.sbuf_tensor("mrows_a", [1, 2 * T], F32))
        mB = scope.enter_context(nc.sbuf_tensor("mB_a", [128, T], BF16))
        rB = scope.enter_context(nc.sbuf_tensor("rB_a", [128, T], BF16))
        pm = [scope.enter_context(nc.psum_tensor(f"pmA{i}", [1, 512], F32)) for i in range(4)]
        ps = [scope.enter_context(nc.psum_tensor(f"psA{i}", [1, 512], F32)) for i in range(4)]

        ld_tgt, cast_tgt, sq_tgt, mm_tgt = {}, {}, {}, {}
        for dc in range(KC):
            slot = dc % 2
            if dc >= 2:
                SY.wait_ge(ve.h, cast_tgt[dc - 2])
            ld_tgt[dc] = ld.dinc(SY.dma_start(out=uin[:, slot * T:slot * T + T],
                                              in_=d_uT[128 * dc:128 * dc + 128, :]))
            VE.wait_ge(ld.h, ld_tgt[dc])
            if dc >= 2:
                VE.wait_ge(pe.h, mm_tgt[dc - 2])
            cast_tgt[dc] = ve.inc(VE.tensor_copy(ubf[:, slot * T:slot * T + T],
                                                 uin[:, slot * T:slot * T + T]))
            if dc >= 2:
                SE.wait_ge(pe.h, mm_tgt[dc - 2])
            SE.wait_ge(ve.h, cast_tgt[dc])
            sq_tgt[dc] = ae.inc(SE.activation(sqb[:, slot * T:slot * T + T],
                                              ubf[:, slot * T:slot * T + T], AF.Square))
            TE.wait_ge(ae.h, sq_tgt[dc])
            last = None
            for t in range(NT):
                TE.matmul(pm[t][0:1, :], onesb[:, 0:1],
                          ubf[:, slot * T + 512 * t: slot * T + 512 * t + 512],
                          start=(dc == 0), stop=(dc == KC - 1))
                last = TE.matmul(ps[t][0:1, :], onesb[:, 0:1],
                                 sqb[:, slot * T + 512 * t: slot * T + 512 * t + 512],
                                 start=(dc == 0), stop=(dc == KC - 1))
            mm_tgt[dc] = pe.inc(last)
            if dc % 5 == 4:
                mat_emit_group()

        VE.wait_ge(pe.h, pe.n)
        for t in range(NT):
            VE.tensor_scalar(mrows[0:1, 512 * t:512 * t + 512], pm[t][0:1, :], 1.0 / DM, None, OP.mult)
            ve.inc(VE.tensor_scalar(mrows[0:1, T + 512 * t:T + 512 * t + 512], ps[t][0:1, :],
                                    1.0 / DM, None, OP.mult))
        VE.tensor_tensor(uin[0:1, 0:T], mrows[0:1, 0:T], mrows[0:1, 0:T], OP.mult)
        VE.tensor_tensor(mrows[0:1, T:2 * T], mrows[0:1, T:2 * T], uin[0:1, 0:T], OP.subtract)
        ve.inc(VE.tensor_scalar(mrows[0:1, T:2 * T], mrows[0:1, T:2 * T], EPS, None, OP.add))
        SE.wait_ge(ve.h, ve.n)
        ae.inc(SE.activation(mrows[0:1, T:2 * T], mrows[0:1, T:2 * T], AF.Sqrt))
        VE.wait_ge(ae.h, ae.n)
        ve.inc(VE.reciprocal(mrows[0:1, T:2 * T], mrows[0:1, T:2 * T]))
        SY.wait_ge(ve.h, ve.n)
        ld.dinc(SY.dma_start(out=row_sp.rearrange("r t -> (r t)")[0:2 * T].unsqueeze(0),
                             in_=mrows[0:1, :]))
        PO.wait_ge(ld.h, ld.n)
        for _rep in range(2):
            gd.dinc(PO.dma_start(out=mB[:, :], in_=bass.AP(row_sp.tensor, 0, [[0, 128], [1, T]])))
            gd.dinc(PO.dma_start(out=rB[:, :], in_=bass.AP(row_sp.tensor, T, [[0, 128], [1, T]])))
        ld2_tgt, nrm_tgt = {}, {}
        VE.wait_ge(gd.h, gd.n)
        for dc in range(KC):
            slot = dc % 2
            if dc >= 2:
                SY.wait_ge(ve.h, nrm_tgt[dc - 2])
            ld2_tgt[dc] = ld.dinc(SY.dma_start(out=uin[:, slot * T:slot * T + T],
                                               in_=d_uT[128 * dc:128 * dc + 128, :]))
            VE.wait_ge(ld.h, ld2_tgt[dc])
            VE.tensor_copy(ubf[:, slot * T:slot * T + T], uin[:, slot * T:slot * T + T])
            VE.tensor_tensor(ubf[:, slot * T:slot * T + T], ubf[:, slot * T:slot * T + T],
                             mB[:, :], OP.subtract)
            nrm_tgt[dc] = ve.inc(VE.tensor_tensor(xn[:, dc * T:(dc + 1) * T],
                                                  ubf[:, slot * T:slot * T + T],
                                                  rB[:, :], OP.mult))
            if dc % 4 == 3:
                mat_emit_group()
        if DEBUG:
            SY.wait_ge(ve.h, ve.n)
            ld.dinc(SY.dma_start(out=taps["tap_xn"], in_=xn[:, :]))
        SY.wait_ge(ld.h, ld.n)

    def phaseB(scope, xn):
        slab = scope.enter_context(nc.sbuf_tensor("slabB", [128, 2 * KC * 128], BF16))
        stage = scope.enter_context(nc.sbuf_tensor("stageB", [128, 2 * (T + 3)], BF16))
        xout = scope.enter_context(nc.sbuf_tensor("xoutB", [128, 2 * T], BF16))
        pz = [scope.enter_context(nc.psum_tensor(f"pB{i}", [128, 512], F32)) for i in range(8)]

        ve.inc(VE.memset(stage[:, 0:3], 0.0))
        ve.inc(VE.memset(stage[:, T + 3:T + 6], 0.0))
        stage_base = [0, T + 3]

        slab_tgt, mmdone_tgt, mmdone_t_tgt, evac_tgt = {}, {}, {}, {}
        conv_tgt, silu_tgt, spill_tgt = {}, {}, {}

        def issue_slab(mb):
            mcols = 64 if mb == MB - 1 else 128
            half = (mb % 2) * KC * 128
            if mb >= 2:
                PO.wait_ge(pe.h, mmdone_tgt[mb - 2])
            gd.dinc(PO.dma_start(
                out=slab[:, half:half + KC * mcols].rearrange("p (kc m) -> p kc m", m=mcols),
                in_=d_win[:, 128 * mb:128 * mb + mcols].rearrange("(kc p) m -> p kc m", p=128)))
            slab_tgt[mb] = gd.n

        issue_slab(0)
        issue_slab(1)
        for mb in range(MB):
            mcols = 64 if mb == MB - 1 else 128
            half = (mb % 2) * KC * 128
            sb = stage_base[mb % 2]
            is_dt = (mb == MB - 1)
            TE.wait_ge(gd.h, slab_tgt[mb])
            if mb >= 2:
                TE.wait_ge(ae.h, evac_tgt[mb - 2])
            for t in range(NT):
                bank = pz[(mb % 2) * 4 + t]
                last = None
                for kcq in range(KC):
                    last = TE.matmul(bank[0:mcols, :],
                                     slab[:, half + kcq * mcols: half + (kcq + 1) * mcols],
                                     xn[:, kcq * T + 512 * t: kcq * T + 512 * t + 512],
                                     start=(kcq == 0), stop=(kcq == KC - 1))
                mmdone_t_tgt[(mb, t)] = pe.inc(last)
            mmdone_tgt[mb] = pe.n
            if mb >= 2 and not is_dt:
                pmb = mb - 2
                if pmb < 32:
                    SE.wait_ge(ld.h, spill_tgt[pmb])
                elif pmb >= 32:
                    SE.wait_ge(ve.h, conv_tgt[pmb])
            for t in range(NT):
                bank = pz[(mb % 2) * 4 + t]
                SE.wait_ge(pe.h, mmdone_t_tgt[(mb, t)])
                if is_dt:
                    ev = SE.copy(dt_raw[0:64, 512 * t:512 * t + 512], bank[0:64, :])
                else:
                    ev = SE.copy(stage[:, sb + 3 + 512 * t: sb + 3 + 512 * t + 512], bank[:, :])
                ae.inc(ev)
            evac_tgt[mb] = ae.n
            if is_dt:
                continue
            if mb < 32:
                SY.wait_ge(ae.h, evac_tgt[mb])
                spill_tgt[mb] = ld.dinc(SY.dma_start(out=z_sp[128 * mb:128 * mb + 128, :],
                                                     in_=stage[:, sb + 3: sb + 3 + T]))
            else:
                ci = mb - 32
                xv = xout[:, (mb % 2) * T:(mb % 2) * T + T]
                VE.wait_ge(ae.h, evac_tgt[mb])
                if mb >= 34 and mb - 2 >= 32:
                    pci = mb - 34
                    if pci < 32:
                        VE.wait_ge(ld.h, spill_tgt[mb - 2])
                    else:
                        VE.wait_ge(ae.h, silu_tgt[mb - 2])
                VE.tensor_scalar(xv, stage[:, sb + 0: sb + 0 + T],
                                 wcols[:, 4 * ci:4 * ci + 1], None, OP.mult)
                lastc = None
                for k in (1, 2, 3):
                    lastc = VE.scalar_tensor_tensor(xv, stage[:, sb + k: sb + k + T],
                                                    wcols[:, 4 * ci + k:4 * ci + k + 1], xv,
                                                    OP.mult, OP.add)
                conv_tgt[mb] = ve.inc(lastc)
                SE.wait_ge(ve.h, conv_tgt[mb])
                if ci < 32:
                    silu_tgt[mb] = ae.inc(SE.activation(xv, xv, AF.Silu,
                                                        bias=bcols[:, ci:ci + 1]))
                    SY.wait_ge(ae.h, silu_tgt[mb])
                    spill_tgt[mb] = ld.dinc(SY.dma_start(
                        out=x_sp[128 * ci:128 * ci + 128, :], in_=xv))
                elif ci == 32:
                    silu_tgt[mb] = ae.inc(SE.activation(Bq[:, :], xv, AF.Silu,
                                                        bias=bcols[:, ci:ci + 1]))
                    SY.wait_ge(ae.h, silu_tgt[mb])
                    spill_tgt[mb] = ld.dinc(SY.dma_start(out=b_sp[:, :], in_=Bq[:, :]))
                else:
                    silu_tgt[mb] = ae.inc(SE.activation(Cq[:, :], xv, AF.Silu,
                                                        bias=bcols[:, ci:ci + 1]))
                    spill_tgt[mb] = ld.n
            if mb + 2 < MB:
                issue_slab(mb + 2)
            if mb % 2 == 1:
                mat_emit_group()
        SY.wait_ge(ld.h, ld.n)
        SY.wait_ge(ae.h, ae.n)
        SE.wait_ge(sd.h, sd.n)
        if DEBUG:
            ld.dinc(SY.dma_start(out=taps["tap_z"], in_=z_sp))
            ld.dinc(SY.dma_start(out=taps["tap_x"], in_=x_sp))
            ld.dinc(SY.dma_start(out=taps["tap_b"], in_=b_sp))
            ld.dinc(SY.dma_start(out=taps["tap_c"], in_=Cq[:, :]))
            ld.dinc(SY.dma_start(out=taps["tap_dt"], in_=dt_raw[0:64, :]))
            SY.wait_ge(ld.h, ld.n)

    def phaseC(scope):
        dtv = scope.enter_context(nc.sbuf_tensor("dtv", [64, T], F32))
        acs = scope.enter_context(nc.sbuf_tensor("acs", [64, T], F32))
        eacs = scope.enter_context(nc.sbuf_tensor("eacs", [64, T], BF16))
        Sv = [scope.enter_context(nc.sbuf_tensor(f"Sv{i}", [128, NH * HD], F32)) for i in range(2)]
        Snb = [scope.enter_context(nc.sbuf_tensor(f"Snb{i}", [128, NH * HD], BF16)) for i in range(2)]
        xT = [scope.enter_context(nc.sbuf_tensor(f"xT{i}", [128, 2 * DIN], BF16)) for i in range(2)]
        xsrc = scope.enter_context(nc.sbuf_tensor("xsrc", [128, 32 * CH], BF16))
        identb = scope.enter_context(nc.sbuf_tensor("identb", [128, 128], BF16))
        BT = [scope.enter_context(nc.sbuf_tensor(f"BT{i}", [128, 256], BF16)) for i in range(2)]
        EAB = scope.enter_context(nc.sbuf_tensor("EAB", [128, 2 * 16 * CH], BF16))
        Yc = scope.enter_context(nc.sbuf_tensor("Yc", [128, 16 * CH], BF16))
        Gm = scope.enter_context(nc.sbuf_tensor("Gm", [128, 2 * CH], BF16))
        CqDup = scope.enter_context(nc.sbuf_tensor("CqDup", [128, 2 * CH], BF16))
        AcsT = scope.enter_context(nc.sbuf_tensor("AcsT", [128, 128], F32))
        dtT = scope.enter_context(nc.sbuf_tensor("dtT", [128, 128], F32))
        AmL = scope.enter_context(nc.sbuf_tensor("AmL", [128, 128], F32))
        ddtT = scope.enter_context(nc.sbuf_tensor("ddtT", [128, 128], F32))
        ddtmp = scope.enter_context(nc.sbuf_tensor("ddtmp", [128, 64], F32))
        cdB = scope.enter_context(nc.sbuf_tensor("cdB", [128, 64], F32))
        lrow = scope.enter_context(nc.sbuf_tensor("lrow", [1, 128], F32))
        fenceA = scope.enter_context(nc.sbuf_tensor("fenceA", [128, 4], BF16))
        rbc = scope.enter_context(nc.sbuf_tensor("rbc", [128, 8 * CH], F32))
        dbuf = scope.enter_context(nc.sbuf_tensor("dbuf", [128, 2 * 1024], F32))
        ebuf = scope.enter_context(nc.sbuf_tensor("ebuf", [128, 2 * 1024], BF16))
        tbuf = scope.enter_context(nc.sbuf_tensor("tbuf", [128, 2 * 1024], BF16))
        cdbuf = scope.enter_context(nc.sbuf_tensor("cdbuf", [128, 2 * 512], BF16))
        xdbuf = scope.enter_context(nc.sbuf_tensor("xdbuf", [128, 2 * 256], BF16))
        pG = scope.enter_context(nc.psum_tensor("pG", [128, 512], F32))
        pY = [scope.enter_context(nc.psum_tensor(f"pY{i}", [128, 256], F32)) for i in range(2)]
        pS = [scope.enter_context(nc.psum_tensor(f"pS{i}", [128, 128], F32)) for i in range(2)]
        pmC = scope.enter_context(nc.psum_tensor("pmC", [128, 128], F32))
        pT = [scope.enter_context(nc.psum_tensor(f"pT{i}", [128, 128], BF16)) for i in range(2)]

        ve.inc(VE.tensor_copy(identb[:, :], ident[:, :]))
        scrA = dbuf[0:64, 0:T]
        scrB = rbc[0:64, 0:T]
        ae.inc(SE.activation(scrA, dt_raw[0:64, :], AF.Abs, bias=dtb_col[0:64, 0:1]))
        ae.inc(SE.activation(scrB, scrA, AF.Exp, scale=-1.0))
        ae.inc(SE.activation(scrB, scrB, AF.Ln, bias=1.0))
        VE.wait_ge(ae.h, ae.n)
        VE.tensor_scalar(dtv[:, :], dt_raw[0:64, :], dtb_col[0:64, 0:1], 0.0, OP.add, OP.max)
        VE.tensor_tensor(dtv[:, :], dtv[:, :], scrB, OP.add)
        VE.tensor_scalar(dtv[:, :], dtv[:, :], 1e-20, None, OP.max)
        ve.inc(VE.tensor_scalar(scrA, dtv[:, :], nA_col[0:64, 0:1], None, OP.mult))
        for c in range(NCH):
            ve.inc(VE.tensor_tensor_scan(acs[0:64, c * CH:(c + 1) * CH],
                                         scrA[:, c * CH:(c + 1) * CH],
                                         scrB[:, c * CH:(c + 1) * CH], 0.0,
                                         OP.add, OP.bypass))
        SE.wait_ge(ve.h, ve.n)
        ae.inc(SE.activation(eacs[:, :], acs[0:64, :], AF.Exp))
        SY.wait_ge(ve.h, ve.n)
        acs_sp_tgt = ld.dinc(SY.dma_start(out=acs_sp, in_=acs[0:64, :]))
        SY.wait_ge(ae.h, ae.n)
        spl_tgt = ld.dinc(SY.dma_start(out=eacs_sp, in_=eacs[0:64, :]))
        ve.inc(VE.memset(Sv[0][:, :], 0.0))
        ve.inc(VE.memset(Snb[0][:, :], 0.0))

        xt_tgt = {}
        eab_tgt = {}
        rbc_tgt = {}
        diff_done = {}
        ops_done = {}
        mm_done = {}
        yevac_done = {}
        supd_done = {}
        exp_done = {}
        yspill_tgt = {}
        gm_done = {}

        xsrc_tgt = {}
        tp_ae_tgt = {}
        tp_state = {"n": 0}

        def issue_xsrc(c):
            if c >= NCH:
                return
            PO.wait_ge(pe.h, pe.n)
            for _rep in range(2):
                gd.dinc(PO.dma_start(
                    out=xsrc[:, :].rearrange("p (pb l) -> p pb l", l=CH),
                    in_=x_sp.rearrange("(pb p) t -> p pb t", p=128)[:, :, c * CH:(c + 1) * CH]))
            xsrc_tgt[c] = gd.n

        def emit_transposes(c, n):
            if c >= NCH:
                return
            bufc = c % 2
            for _ in range(n):
                i = tp_state["n"]
                if i >= 66:
                    return
                tp_state["n"] = i + 1
                bank = pT[i % 2]
                if i == 0:
                    TE.wait_ge(gd.h, xsrc_tgt[c])
                    TE.wait_ge(ve.h, ve.n)
                if i >= 2:
                    TE.wait_ge(ae.h, tp_ae_tgt[i - 2])
                if i < 64:
                    pb, sb2 = i // 2, i % 2
                    srcv = xsrc[:, pb * CH + sb2 * 128: pb * CH + sb2 * 128 + 128]
                    dstv = xT[bufc][:, sb2 * DIN + 128 * pb: sb2 * DIN + 128 * pb + 128]
                else:
                    sb2 = i - 64
                    srcv = Bq[:, c * CH + sb2 * 128: c * CH + sb2 * 128 + 128]
                    dstv = BT[bufc][:, sb2 * 128: sb2 * 128 + 128]
                pe.inc(TE.transpose(bank[:, :], srcv, identb[:, :]))
                SE.wait_ge(pe.h, pe.n)
                tp_ae_tgt[i] = ae.inc(SE.copy(dstv, bank[:, :]))
            if tp_state["n"] >= 66:
                xt_tgt[c] = ("ae", tp_ae_tgt[65])

        eab_user = {}

        def issue_eab(c, q):
            if c >= NCH:
                return
            half = q % 2
            PO.wait_ge(ld.h, spl_tgt)
            if eab_user.get(half) is not None:
                PO.wait_ge(ve.h, eab_user[half])
            gd.dinc(PO.dma_start(
                out=EAB[:, half * 16 * CH: (half + 1) * 16 * CH]
                    .rearrange("p (h l) -> p h l", l=CH),
                in_=bass.AP(eacs_sp.tensor, (q * 16) * T + c * CH,
                            [[0, 128], [T, 16], [1, CH]])))
            eab_tgt[(c, q)] = gd.n

        def issue_rbc(gj):
            c, j = divmod(gj, 32)
            if c >= NCH:
                return
            PO.wait_ge(ld.h, acs_sp_tgt)
            if gj >= 4:
                PO.wait_ge(ve.h, diff_done[gj - 4])
            slot = (2 * gj) % 8
            sd.dinc(PO.dma_start(
                out=rbc[:, slot * CH:(slot + 2) * CH].rearrange("p (hh l) -> p hh l", l=CH),
                in_=bass.AP(acs_sp.tensor, 2 * j * T + c * CH,
                            [[0, 128], [T, 2], [1, CH]])))
            rbc_tgt[gj] = sd.n

        def emit_diff(gj):
            c, j = divmod(gj, 32)
            if c >= NCH:
                return
            par = j % 2
            VE.wait_ge(sd.h, min(rbc_tgt[gj] + 32, max(rbc_tgt.values())))
            if gj >= 2:
                VE.wait_ge(ae.h, exp_done[gj - 2])
            lastd = None
            for hh in range(2):
                h = 2 * j + hh
                slot = (2 * gj + hh) % 8
                for sb2 in range(2):
                    lastd = VE.tensor_scalar(
                        dbuf[:, par * 1024 + (2 * hh + sb2) * CH: par * 1024 + (2 * hh + sb2 + 1) * CH],
                        rbc[:, slot * CH:(slot + 1) * CH],
                        AmL[:, sb2 * 64 + h: sb2 * 64 + h + 1], 30.0,
                        OP.subtract, OP.min)
            diff_done[gj] = ve.inc(lastd)

        issue_xsrc(0)
        emit_transposes(0, 66)
        tp_state["n"] = 0
        issue_eab(0, 0)
        issue_eab(0, 1)
        for g0 in range(4):
            issue_rbc(g0)

        for c in range(NCH):
            cT = c * CH
            bufc = c % 2
            Sprev, Snext = Sv[c % 2], Sv[1 - c % 2]
            Sbprev, Sbnext = Snb[c % 2], Snb[1 - c % 2]
            SE.wait_ge(ld.h, acs_sp_tgt)
            for _rep in range(3):
                lr_tgt = sd.dinc(SE.dma_start(
                    out=lrow[0:1, 0:64],
                    in_=acs_sp.rearrange("h t -> (h t)")[cT + 255::T][0:64].unsqueeze(0),
                    allow_slow_non_contiguous=True))
            if c >= 1:
                TE.wait_ge(ve.h, gm_done[c - 1])
            TE.wait_ge(ae.h, xt_tgt[c][1])
            VE.wait_ge(ae.h, xt_tgt[c][1])
            if c + 1 < NCH:
                issue_xsrc(c + 1)
            tp_state["n"] = 0
            gmm = None
            for sb2 in range(2):
                gmm = TE.matmul(pG[:, sb2 * CH:(sb2 + 1) * CH],
                                Bq[:, cT + 128 * sb2: cT + 128 * sb2 + 128],
                                Cq[:, cT:cT + CH], start=True, stop=True)
            g_mm = pe.inc(gmm)
            VE.wait_ge(pe.h, g_mm)
            lastg = None
            for sb2 in range(2):
                lastg = VE.tensor_tensor(Gm[:, sb2 * CH:(sb2 + 1) * CH], pG[:, sb2 * CH:(sb2 + 1) * CH],
                                         trimask[:, sb2 * CH:(sb2 + 1) * CH], OP.mult)
            gm_done[c] = ve.inc(lastg)
            VE.tensor_copy(CqDup[:, 0:CH], Cq[:, cT:cT + CH])
            ve.inc(VE.tensor_copy(CqDup[:, CH:2 * CH], Cq[:, cT:cT + CH]))
            if c >= 1:
                SE.wait_ge(ve.h, supd_done[(c - 1) * 32 + 31])
                TE.wait_ge(ve.h, supd_done[(c - 1) * 32 + 31])
            tp_ae = None
            for (srct, dstt) in ((acs, AcsT), (dtv, dtT)):
                for sb2 in range(2):
                    if tp_ae is not None:
                        TE.wait_ge(ae.h, tp_ae)
                    pe.inc(TE.transpose(pmC[0:128, 0:64],
                                        srct[0:64, cT + 128 * sb2: cT + 128 * sb2 + 128],
                                        ident[0:64, 0:64]))
                    SE.wait_ge(pe.h, pe.n)
                    tp_ae = ae.inc(SE.copy(dstt[:, sb2 * 64:(sb2 + 1) * 64], pmC[0:128, 0:64]))
            SE.wait_ge(pe.h, pe.n)
            ae.inc(SE.activation(AmL[:, :], dtT[:, :], AF.Ln))
            VE.wait_ge(ae.h, ae.n)
            ve.inc(VE.tensor_tensor(AmL[:, :], AcsT[:, :], AmL[:, :], OP.subtract))
            SE.wait_ge(sd.h, lr_tgt)
            ae.inc(SE.activation(lrow[0:1, 64:128], lrow[0:1, 0:64], AF.Exp))
            TE.wait_ge(ae.h, ae.n)
            lb_mm = pe.inc(TE.matmul(pmC[:, 0:64], ones1[0:1, :], lrow[0:1, 0:64],
                                     start=True, stop=True))
            VE.wait_ge(pe.h, lb_mm)
            VE.wait_ge(ae.h, ae.n)
            for sb2 in range(2):
                ve.inc(VE.tensor_tensor(ddtmp[:, 0:64], pmC[:, 0:64],
                                        AcsT[:, sb2 * 64:(sb2 + 1) * 64], OP.subtract))
                SE.wait_ge(ve.h, ve.n)
                ae.inc(SE.activation(ddtT[:, sb2 * 64:(sb2 + 1) * 64], ddtmp[:, 0:64], AF.Exp))
                VE.wait_ge(ae.h, ae.n)
                ve.inc(VE.tensor_tensor(ddtT[:, sb2 * 64:(sb2 + 1) * 64],
                                        ddtT[:, sb2 * 64:(sb2 + 1) * 64],
                                        dtT[:, sb2 * 64:(sb2 + 1) * 64], OP.mult))
            TE.wait_ge(ve.h, ve.n)
            cd_mm = pe.inc(TE.matmul(pmC[:, 64:128], ones1[0:1, :], lrow[0:1, 64:128],
                                     start=True, stop=True))
            SE.wait_ge(pe.h, cd_mm)
            cd_ae = ae.inc(SE.copy(cdB[:, :], pmC[:, 64:128]))
            VE.wait_ge(ae.h, cd_ae)
            emit_diff(c * 32 + 0)
            issue_rbc(c * 32 + 4)
            for j in range(32):
                gj = c * 32 + j
                par = j % 2
                SE.wait_ge(ve.h, diff_done[gj])
                SE.activation(ebuf[:, par * 1024:(par + 1) * 1024],
                              dbuf[:, par * 1024:(par + 1) * 1024], AF.Exp)
                exp_done[gj] = ae.inc(SE.copy(fenceA[:, 0:1],
                                              ebuf[:, par * 1024 + 1023: par * 1024 + 1024]))
                if j < 31:
                    emit_diff(gj + 1)
                    issue_rbc(gj + 5)
                VE.wait_ge(ae.h, exp_done[gj])
                if gj >= 2:
                    VE.wait_ge(pe.h, mm_done[gj - 2])
                if j % 8 == 0:
                    VE.wait_ge(gd.h, eab_tgt[(c, j // 8)])
                for hh in range(2):
                    h = 2 * j + hh
                    VE.tensor_tensor(
                        tbuf[:, par * 1024 + 2 * hh * CH: par * 1024 + (2 * hh + 2) * CH],
                        ebuf[:, par * 1024 + 2 * hh * CH: par * 1024 + (2 * hh + 2) * CH],
                        Gm[:, 0:2 * CH], OP.mult)
                    if hh == 0:
                        VE.tensor_tensor(
                            cdbuf[:, par * 512: par * 512 + 2 * CH],
                            CqDup[:, :],
                            EAB[:, (((2 * j) // 16) % 2) * 16 * CH + ((2 * j) % 16) * CH:
                                 (((2 * j) // 16) % 2) * 16 * CH + ((2 * j) % 16 + 2) * CH],
                            OP.mult)
                    for sb2 in range(2):
                        lasto = VE.tensor_scalar(
                            xdbuf[:, par * 256 + sb2 * 128 + hh * 64: par * 256 + sb2 * 128 + hh * 64 + 64],
                            xT[bufc][:, sb2 * DIN + h * 64: sb2 * DIN + h * 64 + 64],
                            ddtT[:, sb2 * 64 + h: sb2 * 64 + h + 1], None, OP.mult)
                ops_done[gj] = ve.inc(lasto)
                if j % 8 == 7:
                    q = j // 8
                    eab_user[q % 2] = ops_done[gj]
                    if q + 2 <= 3:
                        issue_eab(c, q + 2)
                    else:
                        issue_eab(c + 1, q + 2 - 4)
                if c + 1 < NCH:
                    emit_transposes(c + 1, 3)
                TE.wait_ge(ve.h, ops_done[gj])
                if gj >= 2:
                    TE.wait_ge(ve.h, yevac_done[gj - 2])
                    TE.wait_ge(ve.h, supd_done[gj - 2])
                lastmm = None
                for hh in range(2):
                    h = 2 * j + hh
                    yr = pY[par][hh * 64:(hh + 1) * 64, :]
                    TE.matmul(yr, xT[bufc][:, 0 * DIN + h * 64: 0 * DIN + h * 64 + 64],
                              tbuf[:, par * 1024 + (2 * hh) * CH: par * 1024 + (2 * hh) * CH + CH],
                              start=True, stop=False)
                    TE.matmul(yr, xT[bufc][:, 1 * DIN + h * 64: 1 * DIN + h * 64 + 64],
                              tbuf[:, par * 1024 + (2 * hh + 1) * CH: par * 1024 + (2 * hh + 1) * CH + CH],
                              start=False, stop=False)
                    TE.matmul(yr, Sbprev[:, h * 64:(h + 1) * 64],
                              cdbuf[:, par * 512 + hh * CH: par * 512 + (hh + 1) * CH],
                              start=False, stop=True)
                    sr = pS[par][:, hh * 64:(hh + 1) * 64]
                    TE.matmul(sr, BT[bufc][:, 0:128],
                              xdbuf[:, par * 256 + 0 * 128 + hh * 64: par * 256 + 0 * 128 + hh * 64 + 64],
                              start=True, stop=False)
                    lastmm = TE.matmul(sr, BT[bufc][:, 128:256],
                                       xdbuf[:, par * 256 + 1 * 128 + hh * 64: par * 256 + 1 * 128 + hh * 64 + 64],
                                       start=False, stop=True)
                mm_done[gj] = pe.inc(lastmm)

                def emit_evac(gje):
                    ce, je = divmod(gje, 32)
                    pare = je % 2
                    VE.wait_ge(pe.h, mm_done[gje])
                    if je == 0 and ce >= 1:
                        VE.wait_ge(ld.h, yspill_tgt[(ce - 1, 1)])
                    if je == 16:
                        VE.wait_ge(ld.h, yspill_tgt[(ce, 0)])
                    ve.inc(VE.tensor_copy(Yc[:, (je % 16) * CH:((je % 16) + 1) * CH], pY[pare][:, :]))
                    yevac_done[gje] = ve.n
                    if je == 15 or je == 31:
                        hf = je // 16
                        SY.wait_ge(ve.h, yevac_done[gje])
                        yspill_tgt[(ce, hf)] = ld.dinc(SY.dma_start(
                            out=y_sp.rearrange("(q p) t -> p q t", p=128)[:, hf * 16:(hf + 1) * 16,
                                                                         ce * CH:ce * CH + CH],
                            in_=Yc.rearrange("p (q l) -> p q l", l=CH)))
                    h0 = 2 * je
                    for hh in range(2):
                        h = h0 + hh
                        VE.tensor_scalar(Snext[:, h * 64:(h + 1) * 64], Sprev[:, h * 64:(h + 1) * 64],
                                         cdB[:, h:h + 1], None, OP.mult)
                    VE.tensor_tensor(Snext[:, h0 * 64:h0 * 64 + 128], Snext[:, h0 * 64:h0 * 64 + 128],
                                     pS[pare][:, :], OP.add)
                    supd_done[gje] = ve.inc(VE.tensor_copy(Sbnext[:, h0 * 64:h0 * 64 + 128],
                                                           Snext[:, h0 * 64:h0 * 64 + 128]))

                if j >= 1:
                    emit_evac(gj - 1)
                if j == 31:
                    emit_evac(gj)
                if PARANOID == 1:
                    for eng in (TE, VE, SE, PO, SY):
                        eng.wait_ge(ve.h, ve.n)
                        eng.wait_ge(pe.h, pe.n)
                        eng.wait_ge(ae.h, ae.n)
                        eng.wait_ge(ld.h, ld.n)
                        eng.wait_ge(sd.h, sd.n)
                        eng.wait_ge(gd.h, gd.n)
                elif PARANOID == 2:
                    for eng in (TE, VE, SE):
                        eng.wait_ge(ve.h, ve.n)
                        eng.wait_ge(pe.h, pe.n)
                        eng.wait_ge(ae.h, ae.n)
                elif PARANOID == 3:
                    for eng in (TE, VE, SE):
                        eng.wait_ge(sd.h, sd.n)
                        eng.wait_ge(gd.h, gd.n)
                        eng.wait_ge(ld.h, ld.n)
                elif PARANOID == 4:
                    for eng in (TE, VE, SE):
                        eng.wait_ge(sd.h, sd.n)
                elif PARANOID == 5:
                    for eng in (TE, VE, SE):
                        eng.wait_ge(gd.h, gd.n)
                elif PARANOID == 6:
                    for eng in (TE, VE, SE):
                        eng.wait_ge(ld.h, ld.n)
            if c + 1 < NCH:
                emit_transposes(c + 1, 66)
        SY.wait_ge(ld.h, ld.n)
        if DEBUG:
            SY.wait_ge(ve.h, ve.n)
            ld.dinc(SY.dma_start(out=taps["tap_acs"], in_=acs[0:64, :]))
            ld.dinc(SY.dma_start(out=taps["tap_y"], in_=y_sp))
            SY.wait_ge(ld.h, ld.n)

    def phaseD(scope, Ygn, rstdB):
        yb = scope.enter_context(nc.sbuf_tensor("ybD", [128, 2 * T], BF16))
        zb = scope.enter_context(nc.sbuf_tensor("zbD", [128, 2 * T], BF16))
        xb = scope.enter_context(nc.sbuf_tensor("xbD", [128, 2 * T], BF16))
        szb = scope.enter_context(nc.sbuf_tensor("szbD", [128, 2 * T], BF16))
        sqb = scope.enter_context(nc.sbuf_tensor("sqbD", [128, 2 * T], BF16))
        rowD = scope.enter_context(nc.sbuf_tensor("rowD", [1, T], F32))
        pm = [scope.enter_context(nc.psum_tensor(f"pmD{i}", [1, 512], F32)) for i in range(4)]

        ld_tgt, dve_tgt, sq_tgt, mm_tgt, sz_tgt, xld_tgt = {}, {}, {}, {}, {}, {}
        for fc in range(32):
            slot = fc % 2
            if fc >= 2:
                SY.wait_ge(ve.h, dve_tgt[fc - 2])
                SY.wait_ge(ae.h, sz_tgt[fc - 2])
            SY.dma_start(out=yb[:, slot * T:slot * T + T],
                         in_=y_sp[128 * fc:128 * fc + 128, :]).then_inc(ld.h, 16)
            SY.dma_start(out=zb[:, slot * T:slot * T + T],
                         in_=z_sp[128 * fc:128 * fc + 128, :]).then_inc(ld.h, 16)
            ld.n += 32
            ld_tgt[fc] = ld.n
            if fc >= 2:
                SE.wait_ge(ve.h, dve_tgt[fc - 2])
            sd.dinc(SE.dma_start(out=xb[:, slot * T:slot * T + T],
                                 in_=x_sp[128 * fc:128 * fc + 128, :]))
            xld_tgt[fc] = sd.n
            SE.wait_ge(ld.h, ld_tgt[fc])
            if fc >= 2:
                SE.wait_ge(ve.h, dve_tgt[fc - 2])
            sz_tgt[fc] = ae.inc(SE.activation(szb[:, slot * T:slot * T + T],
                                              zb[:, slot * T:slot * T + T], AF.Silu))
            VE.wait_ge(ld.h, ld_tgt[fc])
            VE.wait_ge(sd.h, min(xld_tgt[fc] + 32, max(xld_tgt.values()) if len(xld_tgt) else 0) if fc >= 0 else 0)
            VE.wait_ge(ae.h, sz_tgt[fc])
            VE.scalar_tensor_tensor(Ygn[:, fc * T:(fc + 1) * T], xb[:, slot * T:slot * T + T],
                                    DExp[:, fc:fc + 1], yb[:, slot * T:slot * T + T],
                                    OP.mult, OP.add)
            dve_tgt[fc] = ve.inc(VE.tensor_tensor(Ygn[:, fc * T:(fc + 1) * T],
                                                  Ygn[:, fc * T:(fc + 1) * T],
                                                  szb[:, slot * T:slot * T + T], OP.mult))
            if fc >= 2:
                SE.wait_ge(pe.h, mm_tgt[fc - 2])
            SE.wait_ge(ve.h, dve_tgt[fc])
            sq_tgt[fc] = ae.inc(SE.activation(sqb[:, slot * T:slot * T + T],
                                              Ygn[:, fc * T:(fc + 1) * T], AF.Square))
            ve.inc(VE.tensor_scalar(Ygn[:, fc * T:(fc + 1) * T], Ygn[:, fc * T:(fc + 1) * T],
                                    nwcols[:, fc:fc + 1], None, OP.mult))
            TE.wait_ge(ae.h, sq_tgt[fc])
            lastq = None
            for t in range(NT):
                lastq = TE.matmul(pm[t][0:1, :], onesb[:, 0:1],
                                  sqb[:, slot * T + 512 * t: slot * T + 512 * t + 512],
                                  start=(fc == 0), stop=(fc == 31))
            mm_tgt[fc] = pe.inc(lastq)
        VE.wait_ge(pe.h, pe.n)
        lastr = None
        for t in range(NT):
            lastr = VE.tensor_scalar(rowD[0:1, 512 * t:512 * t + 512], pm[t][0:1, :],
                                     1.0 / DIN, EPS, OP.mult, OP.add)
        ve.inc(lastr)
        SE.wait_ge(ve.h, ve.n)
        ae.inc(SE.activation(rowD[0:1, :], rowD[0:1, :], AF.Sqrt))
        VE.wait_ge(ae.h, ae.n)
        ve.inc(VE.reciprocal(rowD[0:1, :], rowD[0:1, :]))
        SY.wait_ge(ve.h, ve.n)
        ld.dinc(SY.dma_start(out=row_sp[2:3, :], in_=rowD[0:1, :]))
        PO.wait_ge(ld.h, ld.n)
        for _rep in range(2):
            gd.dinc(PO.dma_start(out=rstdB[:, :], in_=bass.AP(row_sp.tensor, 2 * T, [[0, 128], [1, T]])))
        VE.wait_ge(gd.h, gd.n)
        SY.wait_ge(ld.h, ld.n)

    def phaseE(scope, Ygn, rstdB):
        wo = scope.enter_context(nc.sbuf_tensor("woE", [128, 2 * 32 * 128], BF16))
        mstage = scope.enter_context(nc.sbuf_tensor("mstageE", [128, 2 * T], BF16))
        sqb = scope.enter_context(nc.sbuf_tensor("sqbE", [128, 2 * 512], BF16))
        statacc = scope.enter_context(nc.sbuf_tensor("staccE", [1, 2 * T], F32))
        pE = [scope.enter_context(nc.psum_tensor(f"pE{i}", [128, 512], F32)) for i in range(4)]
        pstA = scope.enter_context(nc.psum_tensor("pstAE", [1, 512], F32))
        pstB = scope.enter_context(nc.psum_tensor("pstBE", [1, 512], F32))

        ve.inc(VE.memset(statacc[0:1, :], 0.0))
        slab_tgt, mm_tgt, ev_tgt, st_tgt, sadd_tgt, spl_tgt = {}, {}, {}, {}, {}, {}

        def issue_wo(db):
            half = (db % 2) * 32 * 128
            if db >= 2:
                PO.wait_ge(pe.h, mm_tgt[(db - 2, NT - 1)])
            gd.dinc(PO.dma_start(
                out=wo[:, half:half + 32 * 128].rearrange("p (kc m) -> p kc m", m=128),
                in_=d_wout[:, 128 * db:128 * db + 128].rearrange("(kc p) m -> p kc m", p=128)))
            slab_tgt[db] = gd.n

        issue_wo(0)
        issue_wo(1)
        for db in range(DB):
            half = (db % 2) * 32 * 128
            TE.wait_ge(gd.h, slab_tgt[db])
            for t in range(NT):
                if db >= 1:
                    TE.wait_ge(ve.h, ev_tgt[(db - 1, t)])
                lastm = None
                for kc in range(32):
                    lastm = TE.matmul(pE[t][:, :], wo[:, half + kc * 128: half + (kc + 1) * 128],
                                      Ygn[:, kc * T + 512 * t: kc * T + 512 * t + 512],
                                      start=(kc == 0), stop=(kc == 31))
                mm_tgt[(db, t)] = pe.inc(lastm)
            slot = db % 2
            for t in range(NT):
                VE.wait_ge(pe.h, mm_tgt[(db, t)])
                if db >= 2:
                    VE.wait_ge(ld.h, spl_tgt[db - 2])
                ev_tgt[(db, t)] = ve.inc(VE.tensor_tensor(
                    mstage[:, slot * T + 512 * t: slot * T + 512 * t + 512],
                    pE[t][:, :], rstdB[:, 512 * t:512 * t + 512], OP.mult))
                SE.wait_ge(ve.h, ev_tgt[(db, t)])
                if t >= 2:
                    SE.wait_ge(pe.h, st_tgt[(db, t - 2)])
                elif db >= 1:
                    SE.wait_ge(pe.h, st_tgt[(db - 1, t + 2)])
                ae.inc(SE.activation(sqb[:, (t % 2) * 512:(t % 2) * 512 + 512],
                                     mstage[:, slot * T + 512 * t: slot * T + 512 * t + 512],
                                     AF.Square))
                TE.wait_ge(ae.h, ae.n)
                if db >= 1 or t >= 1:
                    TE.wait_ge(ve.h, sadd_tgt[(db, t - 1) if t >= 1 else (db - 1, NT - 1)])
                TE.matmul(pstA[0:1, :], onesb[:, 0:1],
                          mstage[:, slot * T + 512 * t: slot * T + 512 * t + 512],
                          start=True, stop=True)
                st_tgt[(db, t)] = pe.inc(TE.matmul(pstB[0:1, :], onesb[:, 0:1],
                                                   sqb[:, (t % 2) * 512:(t % 2) * 512 + 512],
                                                   start=True, stop=True))
                VE.wait_ge(pe.h, st_tgt[(db, t)])
                VE.tensor_tensor(statacc[0:1, 512 * t:512 * t + 512],
                                 statacc[0:1, 512 * t:512 * t + 512], pstA[0:1, :], OP.add)
                sadd_tgt[(db, t)] = ve.inc(VE.tensor_tensor(
                    statacc[0:1, T + 512 * t:T + 512 * t + 512],
                    statacc[0:1, T + 512 * t:T + 512 * t + 512], pstB[0:1, :], OP.add))
            SY.wait_ge(ve.h, ev_tgt[(db, NT - 1)])
            spl_tgt[db] = ld.dinc(SY.dma_start(out=mix_sp[128 * db:128 * db + 128, :],
                                               in_=mstage[:, slot * T:slot * T + T]))
            if db + 2 < DB:
                issue_wo(db + 2)
        VE.wait_ge(ve.h, ve.n)
        VE.tensor_scalar(statacc[0:1, 0:T], statacc[0:1, 0:T], 1.0 / DM, None, OP.mult)
        VE.tensor_scalar(statacc[0:1, T:2 * T], statacc[0:1, T:2 * T], 1.0 / DM, None, OP.mult)
        VE.tensor_tensor(sqb[0:1, 0:512].bitcast(F32) if False else rowscratchE[0:1, 0:T],
                         statacc[0:1, 0:T], statacc[0:1, 0:T], OP.mult)
        VE.tensor_tensor(statacc[0:1, T:2 * T], statacc[0:1, T:2 * T], rowscratchE[0:1, 0:T], OP.subtract)
        ve.inc(VE.tensor_scalar(statacc[0:1, T:2 * T], statacc[0:1, T:2 * T], EPS, None, OP.add))
        SE.wait_ge(ve.h, ve.n)
        ae.inc(SE.activation(statacc[0:1, T:2 * T], statacc[0:1, T:2 * T], AF.Sqrt))
        VE.wait_ge(ae.h, ae.n)
        ve.inc(VE.reciprocal(statacc[0:1, T:2 * T], statacc[0:1, T:2 * T]))
        SY.wait_ge(ve.h, ve.n)
        ld.dinc(SY.dma_start(out=row_sp.rearrange("r t -> (r t)")[3 * T:5 * T].unsqueeze(0),
                             in_=statacc[0:1, :]))
        SY.wait_ge(ld.h, ld.n)
        if DEBUG:
            ld.dinc(SY.dma_start(out=taps["tap_mix"], in_=mix_sp))
            SY.wait_ge(ld.h, ld.n)

    def phaseF(scope, xn1):
        x1b = scope.enter_context(nc.sbuf_tensor("x1bF", [128, KC * T], BF16))
        mixb = scope.enter_context(nc.sbuf_tensor("mixbF", [128, 2 * T], BF16))
        ub = scope.enter_context(nc.sbuf_tensor("ubF", [128, 2 * T], F32))
        x1st = scope.enter_context(nc.sbuf_tensor("x1stF", [128, 2 * T], F32))
        mBf = scope.enter_context(nc.sbuf_tensor("mBf", [128, T], BF16))
        rBf = scope.enter_context(nc.sbuf_tensor("rBf", [128, T], BF16))
        pm = [scope.enter_context(nc.psum_tensor(f"pmF{i}", [1, 512], F32)) for i in range(4)]
        ps = [scope.enter_context(nc.psum_tensor(f"psF{i}", [1, 512], F32)) for i in range(4)]

        PO.wait_ge(ld.h, ld.n)
        for _rep in range(2):
            gd.dinc(PO.dma_start(out=mBf[:, :], in_=bass.AP(row_sp.tensor, 3 * T, [[0, 128], [1, T]])))
            gd.dinc(PO.dma_start(out=rBf[:, :], in_=bass.AP(row_sp.tensor, 4 * T, [[0, 128], [1, T]])))
        VE.wait_ge(gd.h, gd.n)
        ld_tgt, dve_tgt, sq_tgt, mm_tgt, spl_tgt = {}, {}, {}, {}, {}
        for dc in range(KC):
            slot = dc % 2
            if dc >= 2:
                SY.wait_ge(ve.h, dve_tgt[dc - 2])
                SY.wait_ge(ld.h, spl_tgt[dc - 2])
            SY.dma_start(out=mixb[:, slot * T:slot * T + T],
                         in_=mix_sp[128 * dc:128 * dc + 128, :]).then_inc(ld.h, 16)
            SY.dma_start(out=ub[:, slot * T:slot * T + T],
                         in_=d_uT[128 * dc:128 * dc + 128, :]).then_inc(ld.h, 16)
            ld.n += 32
            ld_tgt[dc] = ld.n
            VE.wait_ge(ld.h, ld_tgt[dc])
            VE.tensor_tensor(mixb[:, slot * T:slot * T + T], mixb[:, slot * T:slot * T + T],
                             mBf[:, :], OP.subtract)
            VE.tensor_tensor(mixb[:, slot * T:slot * T + T], mixb[:, slot * T:slot * T + T],
                             rBf[:, :], OP.mult)
            VE.tensor_tensor(x1st[:, slot * T:slot * T + T], ub[:, slot * T:slot * T + T],
                             mixb[:, slot * T:slot * T + T], OP.add)
            dve_tgt[dc] = ve.inc(VE.tensor_copy(x1b[:, dc * T:(dc + 1) * T],
                                                x1st[:, slot * T:slot * T + T]))
            SY.wait_ge(ve.h, dve_tgt[dc])
            spl_tgt[dc] = ld.dinc(SY.dma_start(out=x1_sp[128 * dc:128 * dc + 128, :],
                                               in_=x1st[:, slot * T:slot * T + T]))
            if dc >= 2:
                SE.wait_ge(pe.h, mm_tgt[dc - 2])
            SE.wait_ge(ve.h, dve_tgt[dc])
            sq_tgt[dc] = ae.inc(SE.activation(mixb[:, slot * T:slot * T + T],
                                              x1b[:, dc * T:(dc + 1) * T], AF.Square))
            TE.wait_ge(ae.h, sq_tgt[dc])
            lastq = None
            for t in range(NT):
                TE.matmul(pm[t][0:1, :], onesb[:, 0:1],
                          x1b[:, dc * T + 512 * t: dc * T + 512 * t + 512],
                          start=(dc == 0), stop=(dc == KC - 1))
                lastq = TE.matmul(ps[t][0:1, :], onesb[:, 0:1],
                                  mixb[:, slot * T + 512 * t: slot * T + 512 * t + 512],
                                  start=(dc == 0), stop=(dc == KC - 1))
            mm_tgt[dc] = pe.inc(lastq)
        VE.wait_ge(pe.h, pe.n)
        rows = x1st[0:1, 0:2 * T]
        for t in range(NT):
            VE.tensor_scalar(rows[0:1, 512 * t:512 * t + 512], pm[t][0:1, :], 1.0 / DM, None, OP.mult)
            ve.inc(VE.tensor_scalar(rows[0:1, T + 512 * t:T + 512 * t + 512], ps[t][0:1, :],
                                    1.0 / DM, None, OP.mult))
        VE.tensor_tensor(ub[0:1, 0:T], rows[0:1, 0:T], rows[0:1, 0:T], OP.mult)
        VE.tensor_tensor(rows[0:1, T:2 * T], rows[0:1, T:2 * T], ub[0:1, 0:T], OP.subtract)
        ve.inc(VE.tensor_scalar(rows[0:1, T:2 * T], rows[0:1, T:2 * T], EPS, None, OP.add))
        SE.wait_ge(ve.h, ve.n)
        ae.inc(SE.activation(rows[0:1, T:2 * T], rows[0:1, T:2 * T], AF.Sqrt))
        VE.wait_ge(ae.h, ae.n)
        ve.inc(VE.reciprocal(rows[0:1, T:2 * T], rows[0:1, T:2 * T]))
        SY.wait_ge(ve.h, ve.n)
        ld.dinc(SY.dma_start(out=row_sp.rearrange("r t -> (r t)")[5 * T:7 * T].unsqueeze(0),
                             in_=rows[0:1, :]))
        PO.wait_ge(ld.h, ld.n)
        for _rep in range(2):
            gd.dinc(PO.dma_start(out=mBf[:, :], in_=bass.AP(row_sp.tensor, 5 * T, [[0, 128], [1, T]])))
            gd.dinc(PO.dma_start(out=rBf[:, :], in_=bass.AP(row_sp.tensor, 6 * T, [[0, 128], [1, T]])))
        VE.wait_ge(gd.h, gd.n)
        for dc in range(KC):
            VE.tensor_tensor(xn1[:, dc * T:(dc + 1) * T], x1b[:, dc * T:(dc + 1) * T],
                             mBf[:, :], OP.subtract)
            ve.inc(VE.tensor_tensor(xn1[:, dc * T:(dc + 1) * T], xn1[:, dc * T:(dc + 1) * T],
                                    rBf[:, :], OP.mult))
        if DEBUG:
            SY.wait_ge(ve.h, ve.n)
            ld.dinc(SY.dma_start(out=taps["tap_x1"], in_=x1_sp))
        SY.wait_ge(ld.h, ld.n)

    def phaseG(scope, xn1):
        wg = scope.enter_context(nc.sbuf_tensor("wgG", [128, 2 * KC * 128], BF16))
        wu = scope.enter_context(nc.sbuf_tensor("wuG", [128, 2 * KC * 128], BF16))
        sg = scope.enter_context(nc.sbuf_tensor("sgG", [128, 2 * 512], BF16))
        hstage = scope.enter_context(nc.sbuf_tensor("hstageG", [128, 2 * T], BF16))
        pGt = [scope.enter_context(nc.psum_tensor(f"pGt{i}", [128, 512], F32)) for i in range(4)]
        pUt = [scope.enter_context(nc.psum_tensor(f"pUt{i}", [128, 512], F32)) for i in range(4)]

        SY.wait_ge(cc.h, 1)
        slab_tgt, mm_tgt, h_tgt, spl_tgt, sg_tgt = {}, {}, {}, {}, {}

        def issue_gu(ob):
            half = (ob % 2) * KC * 128
            if ob >= 2:
                SY.wait_ge(pe.h, mm_tgt[(ob - 2, NT - 1)])
                SE.wait_ge(pe.h, mm_tgt[(ob - 2, NT - 1)])
            sd.dinc(SE.dma_start(
                out=wg[:, half:half + KC * 128].rearrange("p (kc m) -> p kc m", m=128),
                in_=w_full[(ob * KC) * TILE_EL:(ob * KC + KC) * TILE_EL]
                    .rearrange("(kc p m) -> p kc m", p=128, m=128)))
            ld.dinc(SY.dma_start(
                out=wu[:, half:half + KC * 128].rearrange("p (kc m) -> p kc m", m=128),
                in_=w_full[(GATE_TILES + ob * KC) * TILE_EL:(GATE_TILES + ob * KC + KC) * TILE_EL]
                    .rearrange("(kc p m) -> p kc m", p=128, m=128)))
            slab_tgt[ob] = (sd.n, ld.n)

        issue_gu(0)
        issue_gu(1)
        for ob in range(OB):
            half = (ob % 2) * KC * 128
            slot = ob % 2
            TE.wait_ge(sd.h, slab_tgt[ob][0])
            TE.wait_ge(ld.h, slab_tgt[ob][1])
            for t in range(NT):
                if ob >= 1:
                    TE.wait_ge(ve.h, h_tgt[(ob - 1, t)])
                for kc in range(KC):
                    TE.matmul(pGt[t][:, :], wg[:, half + kc * 128: half + (kc + 1) * 128],
                              xn1[:, kc * T + 512 * t: kc * T + 512 * t + 512],
                              start=(kc == 0), stop=(kc == KC - 1))
                lastm = None
                for kc in range(KC):
                    lastm = TE.matmul(pUt[t][:, :], wu[:, half + kc * 128: half + (kc + 1) * 128],
                                      xn1[:, kc * T + 512 * t: kc * T + 512 * t + 512],
                                      start=(kc == 0), stop=(kc == KC - 1))
                mm_tgt[(ob, t)] = pe.inc(lastm)
            for t in range(NT):
                SE.wait_ge(pe.h, mm_tgt[(ob, t)])
                if t >= 2:
                    SE.wait_ge(ve.h, h_tgt[(ob, t - 2)])
                elif ob >= 1:
                    SE.wait_ge(ve.h, h_tgt[(ob - 1, t + 2)])
                sg_tgt[(ob, t)] = ae.inc(SE.activation(sg[:, (t % 2) * 512:(t % 2) * 512 + 512],
                                                       pGt[t][:, :], AF.Silu))
                VE.wait_ge(ae.h, sg_tgt[(ob, t)])
                if ob >= 2:
                    VE.wait_ge(ld.h, spl_tgt[ob - 2])
                h_tgt[(ob, t)] = ve.inc(VE.tensor_tensor(
                    hstage[:, slot * T + 512 * t: slot * T + 512 * t + 512],
                    sg[:, (t % 2) * 512:(t % 2) * 512 + 512], pUt[t][:, :], OP.mult))
            SY.wait_ge(ve.h, h_tgt[(ob, NT - 1)])
            spl_tgt[ob] = ld.dinc(SY.dma_start(out=h_sp[128 * ob:128 * ob + 128, :],
                                               in_=hstage[:, slot * T:slot * T + T]))
            if ob + 2 < OB:
                issue_gu(ob + 2)
        SY.wait_ge(ld.h, ld.n)
        if DEBUG:
            ld.dinc(SY.dma_start(out=taps["tap_h"], in_=h_sp))
            SY.wait_ge(ld.h, ld.n)

    def phaseH(scope):
        hT = scope.enter_context(nc.sbuf_tensor("hTH", [128, KO * 1024], BF16))
        wd = scope.enter_context(nc.sbuf_tensor("wdH", [128, 2 * KO * 128], BF16))
        fstage = scope.enter_context(nc.sbuf_tensor("fstageH", [128, 2 * 1024], BF16))
        sqb = scope.enter_context(nc.sbuf_tensor("sqbH", [128, 2 * 512], BF16))
        statacc = scope.enter_context(nc.sbuf_tensor("staccH", [1, 2 * T], F32))
        rscr = scope.enter_context(nc.sbuf_tensor("rscrH", [1, T], F32))
        pH = [scope.enter_context(nc.psum_tensor(f"pH{i}", [128, 512], F32)) for i in range(4)]
        pstA = scope.enter_context(nc.psum_tensor("pstAH", [1, 512], F32))
        pstB = scope.enter_context(nc.psum_tensor("pstBH", [1, 512], F32))

        ve.inc(VE.memset(statacc[0:1, :], 0.0))
        slab_tgt, mm_tgt, ev_tgt, st_tgt, sadd_tgt, spl_tgt, ht_tgt = {}, {}, {}, {}, {}, {}, {}

        def issue_wd(th, db):
            half = (db % 2) * KO * 128
            if db >= 2:
                PO.wait_ge(pe.h, mm_tgt[(th, db - 2)][1])
            elif th >= 1:
                PO.wait_ge(pe.h, mm_tgt[(th - 1, DB - 2 + db)][1])
            gd.dinc(PO.dma_start(
                out=wd[:, half:half + KO * 128].rearrange("p (kc m) -> p kc m", m=128),
                in_=w_full[(2 * GATE_TILES + db * KO) * TILE_EL:(2 * GATE_TILES + (db + 1) * KO) * TILE_EL]
                    .rearrange("(kc p m) -> p kc m", p=128, m=128)))
            slab_tgt[(th, db)] = gd.n

        for th in range(2):
            if th >= 1:
                SY.wait_ge(pe.h, mm_tgt[(0, DB - 1)][1])
            ht_tgt[th] = ld.dinc(SY.dma_start(
                out=hT[:, :].rearrange("p (kc q) -> p kc q", q=1024),
                in_=h_sp.rearrange("(kc p) t -> p kc t", p=128)[:, :, th * 1024:(th + 1) * 1024]))
            issue_wd(th, 0)
            issue_wd(th, 1)
            for db in range(DB):
                half = (db % 2) * KO * 128
                TE.wait_ge(gd.h, slab_tgt[(th, db)])
                TE.wait_ge(ld.h, ht_tgt[th])
                lastm = None
                for tq in range(2):
                    bank = pH[(db % 2) * 2 + tq]
                    if db >= 2:
                        TE.wait_ge(ae.h, ev_tgt[(th, db - 2, tq)])
                    elif th >= 1 and db < 2:
                        TE.wait_ge(ae.h, ev_tgt[(th - 1, DB - 2 + db, tq)])
                    for kc in range(KO):
                        lastm = TE.matmul(bank[:, :], wd[:, half + kc * 128: half + (kc + 1) * 128],
                                          hT[:, kc * 1024 + tq * 512: kc * 1024 + tq * 512 + 512],
                                          start=(kc == 0), stop=(kc == KO - 1))
                mm_tgt[(th, db)] = (None, pe.inc(lastm))
                slot = db % 2
                for tq in range(2):
                    bank = pH[(db % 2) * 2 + tq]
                    SE.wait_ge(pe.h, mm_tgt[(th, db)][1])
                    if db >= 2:
                        SE.wait_ge(ld.h, spl_tgt[(th, db - 2)])
                    elif th >= 1:
                        SE.wait_ge(ld.h, spl_tgt[(th - 1, DB - 2 + db)])
                    ev_tgt[(th, db, tq)] = ae.inc(SE.copy(
                        fstage[:, slot * 1024 + tq * 512: slot * 1024 + tq * 512 + 512], bank[:, :]))
                    ae.inc(SE.activation(sqb[:, tq * 512:(tq + 1) * 512],
                                         fstage[:, slot * 1024 + tq * 512: slot * 1024 + tq * 512 + 512],
                                         AF.Square))
                    TE.wait_ge(ae.h, ae.n)
                    if sadd_tgt:
                        TE.wait_ge(ve.h, max(sadd_tgt.values()))
                    TE.matmul(pstA[0:1, :], onesb[:, 0:1],
                              fstage[:, slot * 1024 + tq * 512: slot * 1024 + tq * 512 + 512],
                              start=True, stop=True)
                    stq = pe.inc(TE.matmul(pstB[0:1, :], onesb[:, 0:1],
                                           sqb[:, tq * 512:(tq + 1) * 512],
                                           start=True, stop=True))
                    VE.wait_ge(pe.h, stq)
                    pos = th * 1024 + tq * 512
                    VE.tensor_tensor(statacc[0:1, pos:pos + 512],
                                     statacc[0:1, pos:pos + 512], pstA[0:1, :], OP.add)
                    sadd_tgt[(th, db, tq)] = ve.inc(VE.tensor_tensor(
                        statacc[0:1, T + pos:T + pos + 512],
                        statacc[0:1, T + pos:T + pos + 512], pstB[0:1, :], OP.add))
                SY.wait_ge(ae.h, ev_tgt[(th, db, 1)])
                spl_tgt[(th, db)] = ld.dinc(SY.dma_start(
                    out=ffn_sp[128 * db:128 * db + 128, th * 1024:(th + 1) * 1024],
                    in_=fstage[:, slot * 1024: slot * 1024 + 1024]))
                if db + 2 < DB:
                    issue_wd(th, db + 2)
                elif th == 0 and db >= DB - 2:
                    issue_wd(1, db - (DB - 2))
        VE.wait_ge(ve.h, ve.n)
        VE.tensor_scalar(statacc[0:1, 0:T], statacc[0:1, 0:T], 1.0 / DM, None, OP.mult)
        VE.tensor_scalar(statacc[0:1, T:2 * T], statacc[0:1, T:2 * T], 1.0 / DM, None, OP.mult)
        VE.tensor_tensor(rscr[0:1, :], statacc[0:1, 0:T], statacc[0:1, 0:T], OP.mult)
        VE.tensor_tensor(statacc[0:1, T:2 * T], statacc[0:1, T:2 * T], rscr[0:1, :], OP.subtract)
        ve.inc(VE.tensor_scalar(statacc[0:1, T:2 * T], statacc[0:1, T:2 * T], EPS, None, OP.add))
        SE.wait_ge(ve.h, ve.n)
        ae.inc(SE.activation(statacc[0:1, T:2 * T], statacc[0:1, T:2 * T], AF.Sqrt))
        VE.wait_ge(ae.h, ae.n)
        ve.inc(VE.reciprocal(statacc[0:1, T:2 * T], statacc[0:1, T:2 * T]))
        SY.wait_ge(ve.h, ve.n)
        ld.dinc(SY.dma_start(out=row_sp.rearrange("r t -> (r t)")[7 * T:9 * T].unsqueeze(0),
                             in_=statacc[0:1, :]))
        SY.wait_ge(ld.h, ld.n)
        if DEBUG:
            ld.dinc(SY.dma_start(out=taps["tap_ffn"], in_=ffn_sp))
            ld.dinc(SY.dma_start(out=taps["tap_rows"], in_=row_sp))
            SY.wait_ge(ld.h, ld.n)

    def phaseI(scope):
        fb = scope.enter_context(nc.sbuf_tensor("fbI", [128, 2 * T], BF16))
        x1s = scope.enter_context(nc.sbuf_tensor("x1sI", [128, 2 * T], F32))
        ostage = scope.enter_context(nc.sbuf_tensor("ostageI", [128, 2 * T], F32))
        mBi = scope.enter_context(nc.sbuf_tensor("mBi", [128, T], BF16))
        rBi = scope.enter_context(nc.sbuf_tensor("rBi", [128, T], BF16))

        PO.wait_ge(ld.h, ld.n)
        for _rep in range(2):
            gd.dinc(PO.dma_start(out=mBi[:, :], in_=bass.AP(row_sp.tensor, 7 * T, [[0, 128], [1, T]])))
            gd.dinc(PO.dma_start(out=rBi[:, :], in_=bass.AP(row_sp.tensor, 8 * T, [[0, 128], [1, T]])))
        VE.wait_ge(gd.h, gd.n)
        ld_tgt, dve_tgt, spl_tgt = {}, {}, {}
        for dc in range(KC):
            slot = dc % 2
            if dc >= 2:
                SY.wait_ge(ve.h, dve_tgt[dc - 2])
                SY.wait_ge(ld.h, spl_tgt[dc - 2])
            SY.dma_start(out=fb[:, slot * T:slot * T + T],
                         in_=ffn_sp[128 * dc:128 * dc + 128, :]).then_inc(ld.h, 16)
            SY.dma_start(out=x1s[:, slot * T:slot * T + T],
                         in_=x1_sp[128 * dc:128 * dc + 128, :]).then_inc(ld.h, 16)
            ld.n += 32
            ld_tgt[dc] = ld.n
            VE.wait_ge(ld.h, ld_tgt[dc])
            VE.tensor_tensor(fb[:, slot * T:slot * T + T], fb[:, slot * T:slot * T + T],
                             mBi[:, :], OP.subtract)
            VE.tensor_tensor(fb[:, slot * T:slot * T + T], fb[:, slot * T:slot * T + T],
                             rBi[:, :], OP.mult)
            dve_tgt[dc] = ve.inc(VE.tensor_tensor(ostage[:, slot * T:slot * T + T],
                                                  x1s[:, slot * T:slot * T + T],
                                                  fb[:, slot * T:slot * T + T], OP.add))
            SY.wait_ge(ve.h, dve_tgt[dc])
            spl_tgt[dc] = ld.dinc(SY.dma_start(out=d_y[128 * dc:128 * dc + 128, :],
                                               in_=ostage[:, slot * T:slot * T + T]))
        SY.wait_ge(ld.h, ld.n)
        VE.wait_ge(ld.h, ld.n)

    with contextlib.ExitStack() as s0:
        phase0(s0)
        barrier()
    with contextlib.ExitStack() as sAB:
        mat_alloc(sAB)
        xn = sAB.enter_context(nc.sbuf_tensor("xn", [128, KC * T], BF16))
        with contextlib.ExitStack() as sA:
            phaseA2(sA, xn)
            barrier()
        with contextlib.ExitStack() as sB:
            phaseB(sB, xn)
            barrier()
        mat_finish_and_collective()
        barrier()
    with contextlib.ExitStack() as sC:
        phaseC(sC)
        barrier()
    with contextlib.ExitStack() as sDE:
        Ygn = sDE.enter_context(nc.sbuf_tensor("Ygn", [128, 32 * T], BF16))
        rstdB = sDE.enter_context(nc.sbuf_tensor("rstdB_DE", [128, T], BF16))
        with contextlib.ExitStack() as sD:
            phaseD(sD, Ygn, rstdB)
            barrier()
        with contextlib.ExitStack() as sE:
            rowscratchE_holder = sE.enter_context(nc.sbuf_tensor("rowscrE", [1, T], F32))
            globals()["rowscratchE"] = rowscratchE_holder
            phaseE(sE, Ygn, rstdB)
            barrier()
    with contextlib.ExitStack() as sFGH:
        xn1 = sFGH.enter_context(nc.sbuf_tensor("xn1", [128, KC * T], BF16))
        with contextlib.ExitStack() as sF:
            phaseF(sF, xn1)
            barrier()
        with contextlib.ExitStack() as sG:
            phaseG(sG, xn1)
            barrier()
    with contextlib.ExitStack() as sH:
        phaseH(sH)
        barrier()
    with contextlib.ExitStack() as sI:
        phaseI(sI)
        barrier()
        if DEBUG:
            PO.wait_ge(cc.h, 1)
            gd.dinc(PO.dma_start(out=taps["tap_wfull"], in_=w_full))
            PO.wait_ge(gd.h, gd.n)
            barrier()
    es.close()
    return nc


def make_in_maps(L0_attn_u, L0_attn_in_proj_kernel, L0_attn_conv1d_weight,
                 L0_attn_conv1d_bias, L0_attn_dt_bias, L0_attn_A_log, L0_attn_D,
                 L0_attn_norm_weight, L0_attn_out_proj_kernel, L0_ffn_lut,
                 L0_ffn_gate_idx, L0_ffn_up_idx, L0_ffn_down_idx):
    u = np.asarray(L0_attn_u, dtype=np.float32)
    w_in = np.ascontiguousarray(np.asarray(L0_attn_in_proj_kernel, dtype=np.float32))
    convwT = np.ascontiguousarray(np.asarray(L0_attn_conv1d_weight, dtype=np.float32).T)
    convb = np.ascontiguousarray(np.asarray(L0_attn_conv1d_bias, dtype=np.float32))
    dtb = np.ascontiguousarray(np.asarray(L0_attn_dt_bias, dtype=np.float32))
    alog = np.ascontiguousarray(np.asarray(L0_attn_A_log, dtype=np.float32))
    dvec = np.ascontiguousarray(np.asarray(L0_attn_D, dtype=np.float32))
    nw = np.ascontiguousarray(np.asarray(L0_attn_norm_weight, dtype=np.float32))
    w_out = np.ascontiguousarray(np.asarray(L0_attn_out_proj_kernel, dtype=np.float32))
    lut = np.ascontiguousarray(np.asarray(L0_ffn_lut, dtype=np.float32))
    gT = np.asarray(L0_ffn_gate_idx, dtype=np.int32).T
    upT = np.asarray(L0_ffn_up_idx, dtype=np.int32).T
    dnT = np.asarray(L0_ffn_down_idx, dtype=np.int32).T

    def tiles_a(mm):
        return mm.reshape(KC, 128, OB, 128).transpose(2, 0, 1, 3).reshape(GATE_TILES, 128, 128)

    def tiles_b(mm):
        return mm.reshape(KO, 128, DB, 128).transpose(2, 0, 1, 3).reshape(GATE_TILES, 128, 128)

    all_tiles = np.ascontiguousarray(
        np.concatenate([tiles_a(gT), tiles_a(upT), tiles_b(dnT)], axis=0))

    in_maps = []
    for c in range(NCORES):
        in_maps.append({
            "uT": np.ascontiguousarray(u[c].T),
            "w_in": w_in, "convwT": convwT, "convb": convb, "dt_bias": dtb,
            "a_log": alog, "dvec": dvec, "nw": nw, "w_out": w_out, "lut": lut,
            "idx_sl": np.ascontiguousarray(all_tiles[c * RANK_TILES:(c + 1) * RANK_TILES]),
        })
    return in_maps


def kernel(**inputs):
    in_maps = make_in_maps(**inputs)
    nc = build()
    res = run_bass_kernel_spmd(nc, in_maps, core_ids=list(range(NCORES)))
    return np.stack([np.asarray(r["y"]).T for r in res.results], axis=0)


# version 16 (final)
# speedup vs baseline: 1.0367x; 1.0155x over previous
import contextlib

import numpy as np
import ml_dtypes
import concourse.bass as bass
import concourse.mybir as mybir
from concourse.bass_utils import run_bass_kernel_spmd

F32 = mybir.dt.float32
BF16 = mybir.dt.bfloat16
I32 = mybir.dt.int32
AF = mybir.ActivationFunctionType
OP = mybir.AluOpType

DM = 2048
T = 2048
DIN = 4096
NH = 64
HD = 64
DS = 128
CH = 256
NCH = T // CH
CONVD = 4352
DPROJ = 8512
INTER = 5504
NCORES = 8
EPS = 1e-5

MB = 67
KC = 16
OB = 43
DB = 16
KO = 43
NT = 4
TILE_EL = 128 * 128
GATE_TILES = OB * KC
ALL_TILES = 3 * GATE_TILES
RANK_TILES = ALL_TILES // NCORES
MAT_CHUNK = 6
N_MATG = (RANK_TILES + MAT_CHUNK - 1) // MAT_CHUNK
NPAR = 3

DEBUG = False
PARANOID = False


class S:
    def __init__(self, nc, es, name):
        self.h = es.enter_context(nc.semaphore(name))
        self.n = 0

    def inc(self, instr, k=1):
        instr.then_inc(self.h, k)
        self.n += k
        return self.n

    def dinc(self, instr):
        return self.inc(instr, 16)


def build():
    nc = bass.Bass("TRN2", target_bir_lowering=False, debug=False)

    d_uT = nc.dram_tensor("uT", [DM, T], F32, kind="ExternalInput").ap()
    d_win = nc.dram_tensor("w_in", [DM, DPROJ], F32, kind="ExternalInput").ap()
    d_convwT = nc.dram_tensor("convwT", [4, CONVD], F32, kind="ExternalInput").ap()
    d_convb = nc.dram_tensor("convb", [CONVD], F32, kind="ExternalInput").ap()
    d_dtb = nc.dram_tensor("dt_bias", [NH], F32, kind="ExternalInput").ap()
    d_alog = nc.dram_tensor("a_log", [NH], F32, kind="ExternalInput").ap()
    d_D = nc.dram_tensor("dvec", [NH], F32, kind="ExternalInput").ap()
    d_nw = nc.dram_tensor("nw", [DIN], F32, kind="ExternalInput").ap()
    d_wout = nc.dram_tensor("w_out", [DIN, DM], F32, kind="ExternalInput").ap()
    d_lut = nc.dram_tensor("lut", [16], F32, kind="ExternalInput").ap()
    d_idx = nc.dram_tensor("idx_sl", [RANK_TILES, 128, 128], I32, kind="ExternalInput").ap()

    d_y = nc.dram_tensor("y", [DM, T], F32, kind="ExternalOutput").ap()

    z_sp = nc.dram_tensor("z_sp", [DIN, T], BF16).ap()
    x_sp = nc.dram_tensor("x_sp", [DIN, T], BF16).ap()
    b_sp = nc.dram_tensor("b_sp", [DS, T], BF16).ap()
    y_sp = nc.dram_tensor("y_sp", [DIN, T], BF16).ap()
    acs_sp = nc.dram_tensor("acs_sp", [NH, T], F32).ap()
    eacs_sp = nc.dram_tensor("eacs_sp", [NH, T], BF16).ap()
    mix_sp = nc.dram_tensor("mix_sp", [DM, T], BF16).ap()
    x1_sp = nc.dram_tensor("x1_sp", [DM, T], F32).ap()
    h_sp = nc.dram_tensor("h_sp", [INTER, T], BF16).ap()
    ffn_sp = nc.dram_tensor("ffn_sp", [DM, T], BF16).ap()
    row_sp = nc.dram_tensor("row_sp", [16, T], F32).ap()
    w_sl = nc.dram_tensor("w_sl", [RANK_TILES * TILE_EL], BF16).ap()
    w_full = nc.dram_tensor("w_full", [ALL_TILES * TILE_EL], BF16, addr_space="Shared").ap()

    taps = {}
    if DEBUG:
        for name, shape, dt in [
            ("tap_xn", [128, KC * T], BF16), ("tap_z", [DIN, T], BF16),
            ("tap_x", [DIN, T], BF16), ("tap_b", [DS, T], BF16),
            ("tap_c", [128, T], BF16), ("tap_dt", [NH, T], F32),
            ("tap_acs", [NH, T], F32), ("tap_y", [DIN, T], BF16),
            ("tap_wfull", [ALL_TILES * TILE_EL], BF16),
            ("tap_mix", [DM, T], BF16),
            ("tap_x1", [DM, T], F32), ("tap_h", [INTER, T], BF16),
            ("tap_ffn", [DM, T], BF16), ("tap_rows", [16, T], F32),
        ]:
            taps[name] = nc.dram_tensor(name, shape, dt, kind="ExternalOutput").ap()

    TE, VE, SE, PO, SY = nc.tensor, nc.vector, nc.scalar, nc.gpsimd, nc.sync

    es = contextlib.ExitStack()
    ld = S(nc, es, "ld")
    sd = S(nc, es, "sd")
    gd = S(nc, es, "gd")
    pe = S(nc, es, "pe")
    ve = S(nc, es, "ve")
    ae = S(nc, es, "ae")
    po = S(nc, es, "po")
    cc = S(nc, es, "cc")

    ident = es.enter_context(nc.sbuf_tensor("ident", [128, 128], F32))
    trimask = es.enter_context(nc.sbuf_tensor("trimask", [128, 2 * CH], BF16))
    ones1 = es.enter_context(nc.sbuf_tensor("ones1", [1, 128], F32))
    onesb = es.enter_context(nc.sbuf_tensor("onesb", [128, 1], BF16))
    lutB = es.enter_context(nc.sbuf_tensor("lutB", [128, 16], F32))
    wcols = es.enter_context(nc.sbuf_tensor("wcols", [128, 4 * 34], F32))
    bcols = es.enter_context(nc.sbuf_tensor("bcols", [128, 34], F32))
    nwcols = es.enter_context(nc.sbuf_tensor("nwcols", [128, 32], F32))
    smallrows = es.enter_context(nc.sbuf_tensor("smallrows", [1, 256], F32))
    dtb_col = es.enter_context(nc.sbuf_tensor("dtb_col", [64, 1], F32))
    nA_col = es.enter_context(nc.sbuf_tensor("nA_col", [64, 1], F32))
    DB_sb = es.enter_context(nc.sbuf_tensor("DB_sb", [128, 64], F32))
    DExp = es.enter_context(nc.sbuf_tensor("DExp", [128, 32], F32))

    dt_raw = es.enter_context(nc.sbuf_tensor("dt_raw", [64, T], F32))
    Bq = es.enter_context(nc.sbuf_tensor("Bq", [128, T], BF16))
    Cq = es.enter_context(nc.sbuf_tensor("Cq", [128, T], BF16))

    def barrier():
        nc.all_engine_barrier()

    def phase0(scope):
        scr44 = scope.enter_context(nc.sbuf_tensor("scr44", [4, CONVD], F32))
        scr14 = scope.enter_context(nc.sbuf_tensor("scr14", [1, CONVD], F32))
        nwrow = scope.enter_context(nc.sbuf_tensor("nwrow", [1, DIN], F32))
        io2 = scope.enter_context(nc.sbuf_tensor("io2", [128, 2 * CH], F32))
        pmisc = scope.enter_context(nc.psum_tensor("pmisc0", [128, 128], F32))

        gd.dinc(PO.dma_start(out=lutB[:, :], in_=bass.AP(d_lut.tensor, 0, [[0, 128], [1, 16]])))
        ld.dinc(SY.dma_start(out=scr44[:, :], in_=d_convwT))
        ld.dinc(SY.dma_start(out=scr14[0:1, :], in_=d_convb.unsqueeze(0)))
        ld.dinc(SY.dma_start(out=nwrow[0:1, :], in_=d_nw.unsqueeze(0)))
        ld.dinc(SY.dma_start(out=smallrows[0:1, 0:64], in_=d_dtb.unsqueeze(0)))
        ld.dinc(SY.dma_start(out=smallrows[0:1, 64:128], in_=d_alog.unsqueeze(0)))
        ld.dinc(SY.dma_start(out=smallrows[0:1, 128:192], in_=d_D.unsqueeze(0)))

        po.inc(PO.iota(io2[:, 0:CH], [[1, CH]], channel_multiplier=-1,
                       allow_small_or_imprecise_dtypes=True))
        po.inc(PO.iota(io2[:, CH:2 * CH], [[1, CH]], base=-128, channel_multiplier=-1,
                       allow_small_or_imprecise_dtypes=True))
        VE.wait_ge(po.h, po.n)
        ve.inc(VE.tensor_scalar(trimask[:, 0:CH], io2[:, 0:CH], 0.0, None, OP.is_ge))
        ve.inc(VE.tensor_scalar(trimask[:, CH:2 * CH], io2[:, CH:2 * CH], 0.0, None, OP.is_ge))
        ve.inc(VE.tensor_scalar(ident[:, :], io2[:, 0:128], 0.0, None, OP.is_equal))
        ve.inc(VE.memset(ones1[:, :], 1.0))
        ve.inc(VE.memset(onesb[:, :], 1.0))

        TE.wait_ge(ve.h, ve.n)
        TE.wait_ge(ld.h, ld.n)

        def transpose_group(items):
            col = 0
            plan = []
            for dst_ap, src_ap in items:
                p, f = src_ap.shape
                if col + p > 128:
                    break
                plan.append((dst_ap, src_ap, col, p, f))
                col += p
            for dst_ap, src_ap, c0, p, f in plan:
                pe.inc(TE.transpose(pmisc[0:f, c0:c0 + p], src_ap, ident[0:p, 0:p]))
            SE.wait_ge(pe.h, pe.n)
            for dst_ap, src_ap, c0, p, f in plan:
                ae.inc(SE.copy(dst_ap, pmisc[0:f, c0:c0 + p]))
            TE.wait_ge(ae.h, ae.n)
            return len(plan)

        items = [(wcols[:, 4 * ci:4 * ci + 4], scr44[0:4, 128 * ci:128 * ci + 128]) for ci in range(34)]
        while items:
            items = items[transpose_group(items):]
        items = [(bcols[:, ci:ci + 1], scr14[0:1, 128 * ci:128 * ci + 128]) for ci in range(34)]
        items += [(nwcols[:, ci:ci + 1], nwrow[0:1, 128 * ci:128 * ci + 128]) for ci in range(32)]
        items += [(dtb_col[0:64, 0:1], smallrows[0:1, 0:64]),
                  (nA_col[0:64, 0:1], smallrows[0:1, 64:128])]
        while items:
            items = items[transpose_group(items):]
        SE.wait_ge(ae.h, ae.n)
        ae.inc(SE.activation(nA_col[0:64, 0:1], nA_col[0:64, 0:1], AF.Exp))
        VE.wait_ge(ae.h, ae.n)
        ve.inc(VE.tensor_scalar(nA_col[0:64, 0:1], nA_col[0:64, 0:1], -1.0, None, OP.mult))

        pe.inc(TE.matmul(pmisc[:, 0:64], ones1[0:1, :], smallrows[0:1, 128:192],
                         start=True, stop=True))
        SE.wait_ge(pe.h, pe.n)
        ae.inc(SE.copy(DB_sb[:, 0:64], pmisc[:, 0:64]))
        VE.wait_ge(ae.h, ae.n)
        for fc in range(32):
            VE.tensor_copy(DExp[0:64, fc:fc + 1], DB_sb[0:64, 2 * fc:2 * fc + 1])
            ve.inc(VE.tensor_copy(DExp[64:128, fc:fc + 1], DB_sb[64:128, 2 * fc + 1:2 * fc + 2]))
        VE.wait_ge(gd.h, gd.n)

    mat = {"g": 0, "ld_tgt": [], "sp_tgt": []}

    def mat_alloc(scope):
        mat["idx"] = scope.enter_context(nc.sbuf_tensor("mat_idx", [128, 2 * MAT_CHUNK * 128], BF16))
        mat["acc"] = scope.enter_context(nc.sbuf_tensor("mat_acc", [128, MAT_CHUNK * 128], BF16))
        mat["m"] = scope.enter_context(nc.sbuf_tensor("mat_m", [128, MAT_CHUNK * 128], BF16))
        for g in range(min(2, N_MATG)):
            mat_issue_load(g)

    def mat_tiles(g):
        t0 = g * MAT_CHUNK
        return t0, min(MAT_CHUNK, RANK_TILES - t0)

    def mat_issue_load(g):
        t0, ntl = mat_tiles(g)
        half = (g % 2) * MAT_CHUNK * 128
        gd.dinc(PO.dma_start(
            out=mat["idx"][:, half:half + ntl * 128].rearrange("p (j m) -> p j m", m=128),
            in_=d_idx[t0:t0 + ntl].rearrange("j p m -> p j m")))
        mat["ld_tgt"].append(gd.n)

    def mat_emit_group():
        g = mat["g"]
        if g >= N_MATG:
            return
        mat["g"] += 1
        t0, ntl = mat_tiles(g)
        half = (g % 2) * MAT_CHUNK * 128
        idxv = mat["idx"][:, half:half + ntl * 128]
        acc = mat["acc"][:, 0:ntl * 128]
        m = mat["m"][:, 0:ntl * 128]
        VE.wait_ge(gd.h, mat["ld_tgt"][g])
        if g >= 1:
            VE.wait_ge(ld.h, mat["sp_tgt"][g - 1])
        VE.tensor_scalar(acc, idxv, 0.0, lutB[:, 0:1], OP.is_equal, OP.mult)
        lastm = None
        for k in range(1, 16):
            VE.tensor_scalar(m, idxv, float(k), lutB[:, k:k + 1], OP.is_equal, OP.mult)
            lastm = VE.tensor_tensor(acc, acc, m, OP.add)
        spill_wait = ve.inc(lastm)
        SY.wait_ge(ve.h, spill_wait)
        mat["sp_tgt"].append(ld.dinc(SY.dma_start(
            out=w_sl[t0 * TILE_EL:(t0 + ntl) * TILE_EL].rearrange("(j p m) -> p j m", p=128, m=128),
            in_=acc.rearrange("p (j m) -> p j m", m=128))))
        if g + 2 < N_MATG:
            PO.wait_ge(ve.h, spill_wait)
            mat_issue_load(g + 2)

    def mat_finish_and_collective():
        while mat["g"] < N_MATG:
            mat_emit_group()
        PO.wait_ge(ld.h, ld.n)
        cc.inc(PO.collective_compute(
            "AllGather", OP.bypass,
            replica_groups=[list(range(NCORES))],
            ins=[w_sl[:]], outs=[w_full[:]]))

    def phaseA2(scope, xn):
        uin = scope.enter_context(nc.sbuf_tensor("uin", [128, 2 * T], F32))
        ubf = scope.enter_context(nc.sbuf_tensor("ubf_sl", [128, 2 * T], BF16))
        sqb = scope.enter_context(nc.sbuf_tensor("sqb", [128, 2 * T], BF16))
        mrows = scope.enter_context(nc.sbuf_tensor("mrows_a", [1, 2 * T], F32))
        mB = scope.enter_context(nc.sbuf_tensor("mB_a", [128, T], BF16))
        rB = scope.enter_context(nc.sbuf_tensor("rB_a", [128, T], BF16))
        pm = [scope.enter_context(nc.psum_tensor(f"pmA{i}", [1, 512], F32)) for i in range(4)]
        ps = [scope.enter_context(nc.psum_tensor(f"psA{i}", [1, 512], F32)) for i in range(4)]

        ld_tgt, cast_tgt, sq_tgt, mm_tgt = {}, {}, {}, {}
        for dc in range(KC):
            slot = dc % 2
            if dc >= 2:
                SY.wait_ge(ve.h, cast_tgt[dc - 2])
            ld_tgt[dc] = ld.dinc(SY.dma_start(out=uin[:, slot * T:slot * T + T],
                                              in_=d_uT[128 * dc:128 * dc + 128, :]))
            VE.wait_ge(ld.h, ld_tgt[dc])
            if dc >= 2:
                VE.wait_ge(pe.h, mm_tgt[dc - 2])
            cast_tgt[dc] = ve.inc(VE.tensor_copy(ubf[:, slot * T:slot * T + T],
                                                 uin[:, slot * T:slot * T + T]))
            if dc >= 2:
                SE.wait_ge(pe.h, mm_tgt[dc - 2])
            SE.wait_ge(ve.h, cast_tgt[dc])
            sq_tgt[dc] = ae.inc(SE.activation(sqb[:, slot * T:slot * T + T],
                                              ubf[:, slot * T:slot * T + T], AF.Square))
            TE.wait_ge(ae.h, sq_tgt[dc])
            last = None
            for t in range(NT):
                TE.matmul(pm[t][0:1, :], onesb[:, 0:1],
                          ubf[:, slot * T + 512 * t: slot * T + 512 * t + 512],
                          start=(dc == 0), stop=(dc == KC - 1))
                last = TE.matmul(ps[t][0:1, :], onesb[:, 0:1],
                                 sqb[:, slot * T + 512 * t: slot * T + 512 * t + 512],
                                 start=(dc == 0), stop=(dc == KC - 1))
            mm_tgt[dc] = pe.inc(last)

        VE.wait_ge(pe.h, pe.n)
        for t in range(NT):
            VE.tensor_scalar(mrows[0:1, 512 * t:512 * t + 512], pm[t][0:1, :], 1.0 / DM, None, OP.mult)
            ve.inc(VE.tensor_scalar(mrows[0:1, T + 512 * t:T + 512 * t + 512], ps[t][0:1, :],
                                    1.0 / DM, None, OP.mult))
        VE.tensor_tensor(uin[0:1, 0:T], mrows[0:1, 0:T], mrows[0:1, 0:T], OP.mult)
        VE.tensor_tensor(mrows[0:1, T:2 * T], mrows[0:1, T:2 * T], uin[0:1, 0:T], OP.subtract)
        ve.inc(VE.tensor_scalar(mrows[0:1, T:2 * T], mrows[0:1, T:2 * T], EPS, None, OP.add))
        SE.wait_ge(ve.h, ve.n)
        ae.inc(SE.activation(mrows[0:1, T:2 * T], mrows[0:1, T:2 * T], AF.Sqrt))
        VE.wait_ge(ae.h, ae.n)
        ve.inc(VE.reciprocal(mrows[0:1, T:2 * T], mrows[0:1, T:2 * T]))
        SY.wait_ge(ve.h, ve.n)
        ld.dinc(SY.dma_start(out=row_sp.rearrange("r t -> (r t)")[0:2 * T].unsqueeze(0),
                             in_=mrows[0:1, :]))
        PO.wait_ge(ld.h, ld.n)
        for _rep in range(2):
            gd.dinc(PO.dma_start(out=mB[:, :], in_=bass.AP(row_sp.tensor, 0, [[0, 128], [1, T]])))
            gd.dinc(PO.dma_start(out=rB[:, :], in_=bass.AP(row_sp.tensor, T, [[0, 128], [1, T]])))
        ld2_tgt, nrm_tgt = {}, {}
        VE.wait_ge(gd.h, gd.n)
        for dc in range(KC):
            slot = dc % 2
            if dc >= 2:
                SY.wait_ge(ve.h, nrm_tgt[dc - 2])
            ld2_tgt[dc] = ld.dinc(SY.dma_start(out=uin[:, slot * T:slot * T + T],
                                               in_=d_uT[128 * dc:128 * dc + 128, :]))
            VE.wait_ge(ld.h, ld2_tgt[dc])
            VE.tensor_copy(ubf[:, slot * T:slot * T + T], uin[:, slot * T:slot * T + T])
            VE.tensor_tensor(ubf[:, slot * T:slot * T + T], ubf[:, slot * T:slot * T + T],
                             mB[:, :], OP.subtract)
            nrm_tgt[dc] = ve.inc(VE.tensor_tensor(xn[:, dc * T:(dc + 1) * T],
                                                  ubf[:, slot * T:slot * T + T],
                                                  rB[:, :], OP.mult))
        if DEBUG:
            SY.wait_ge(ve.h, ve.n)
            ld.dinc(SY.dma_start(out=taps["tap_xn"], in_=xn[:, :]))
        SY.wait_ge(ld.h, ld.n)

    def phaseB(scope, xn):
        slab = scope.enter_context(nc.sbuf_tensor("slabB", [128, 2 * KC * 128], BF16))
        stage = scope.enter_context(nc.sbuf_tensor("stageB", [128, 2 * (T + 3)], BF16))
        xout = scope.enter_context(nc.sbuf_tensor("xoutB", [128, 2 * T], BF16))
        pz = [scope.enter_context(nc.psum_tensor(f"pB{i}", [128, 512], F32)) for i in range(8)]

        ve.inc(VE.memset(stage[:, 0:3], 0.0))
        ve.inc(VE.memset(stage[:, T + 3:T + 6], 0.0))
        stage_base = [0, T + 3]

        slab_tgt, mmdone_tgt, mmdone_t_tgt, evac_tgt = {}, {}, {}, {}
        conv_tgt, silu_tgt, spill_tgt = {}, {}, {}

        def issue_slab(mb):
            mcols = 64 if mb == MB - 1 else 128
            half = (mb % 2) * KC * 128
            if mb >= 2:
                PO.wait_ge(pe.h, mmdone_tgt[mb - 2])
            gd.dinc(PO.dma_start(
                out=slab[:, half:half + KC * mcols].rearrange("p (kc m) -> p kc m", m=mcols),
                in_=d_win[:, 128 * mb:128 * mb + mcols].rearrange("(kc p) m -> p kc m", p=128)))
            slab_tgt[mb] = gd.n

        issue_slab(0)
        issue_slab(1)
        for mb in range(MB):
            mcols = 64 if mb == MB - 1 else 128
            half = (mb % 2) * KC * 128
            sb = stage_base[mb % 2]
            is_dt = (mb == MB - 1)
            TE.wait_ge(gd.h, slab_tgt[mb])
            if mb >= 2:
                TE.wait_ge(ae.h, evac_tgt[mb - 2])
            for t in range(NT):
                bank = pz[(mb % 2) * 4 + t]
                last = None
                for kcq in range(KC):
                    last = TE.matmul(bank[0:mcols, :],
                                     slab[:, half + kcq * mcols: half + (kcq + 1) * mcols],
                                     xn[:, kcq * T + 512 * t: kcq * T + 512 * t + 512],
                                     start=(kcq == 0), stop=(kcq == KC - 1))
                mmdone_t_tgt[(mb, t)] = pe.inc(last)
            mmdone_tgt[mb] = pe.n
            if mb >= 2 and not is_dt:
                pmb = mb - 2
                if pmb < 32:
                    SE.wait_ge(ld.h, spill_tgt[pmb])
                elif pmb >= 32:
                    SE.wait_ge(ve.h, conv_tgt[pmb])
            for t in range(NT):
                bank = pz[(mb % 2) * 4 + t]
                SE.wait_ge(pe.h, mmdone_t_tgt[(mb, t)])
                if is_dt:
                    ev = SE.copy(dt_raw[0:64, 512 * t:512 * t + 512], bank[0:64, :])
                else:
                    ev = SE.copy(stage[:, sb + 3 + 512 * t: sb + 3 + 512 * t + 512], bank[:, :])
                ae.inc(ev)
            evac_tgt[mb] = ae.n
            if is_dt:
                continue
            if mb < 32:
                SY.wait_ge(ae.h, evac_tgt[mb])
                spill_tgt[mb] = ld.dinc(SY.dma_start(out=z_sp[128 * mb:128 * mb + 128, :],
                                                     in_=stage[:, sb + 3: sb + 3 + T]))
            else:
                ci = mb - 32
                xv = xout[:, (mb % 2) * T:(mb % 2) * T + T]
                VE.wait_ge(ae.h, evac_tgt[mb])
                if mb >= 34 and mb - 2 >= 32:
                    pci = mb - 34
                    if pci < 32:
                        VE.wait_ge(ld.h, spill_tgt[mb - 2])
                    else:
                        VE.wait_ge(ae.h, silu_tgt[mb - 2])
                VE.tensor_scalar(xv, stage[:, sb + 0: sb + 0 + T],
                                 wcols[:, 4 * ci:4 * ci + 1], None, OP.mult)
                lastc = None
                for k in (1, 2, 3):
                    lastc = VE.scalar_tensor_tensor(xv, stage[:, sb + k: sb + k + T],
                                                    wcols[:, 4 * ci + k:4 * ci + k + 1], xv,
                                                    OP.mult, OP.add)
                conv_tgt[mb] = ve.inc(lastc)
                SE.wait_ge(ve.h, conv_tgt[mb])
                if ci < 32:
                    silu_tgt[mb] = ae.inc(SE.activation(xv, xv, AF.Silu,
                                                        bias=bcols[:, ci:ci + 1]))
                    SY.wait_ge(ae.h, silu_tgt[mb])
                    spill_tgt[mb] = ld.dinc(SY.dma_start(
                        out=x_sp[128 * ci:128 * ci + 128, :], in_=xv))
                elif ci == 32:
                    silu_tgt[mb] = ae.inc(SE.activation(Bq[:, :], xv, AF.Silu,
                                                        bias=bcols[:, ci:ci + 1]))
                    SY.wait_ge(ae.h, silu_tgt[mb])
                    spill_tgt[mb] = ld.dinc(SY.dma_start(out=b_sp[:, :], in_=Bq[:, :]))
                else:
                    silu_tgt[mb] = ae.inc(SE.activation(Cq[:, :], xv, AF.Silu,
                                                        bias=bcols[:, ci:ci + 1]))
                    spill_tgt[mb] = ld.n
            if mb + 2 < MB:
                issue_slab(mb + 2)
            mat_emit_group()
        SY.wait_ge(ld.h, ld.n)
        SY.wait_ge(ae.h, ae.n)
        SE.wait_ge(sd.h, sd.n)
        if DEBUG:
            ld.dinc(SY.dma_start(out=taps["tap_z"], in_=z_sp))
            ld.dinc(SY.dma_start(out=taps["tap_x"], in_=x_sp))
            ld.dinc(SY.dma_start(out=taps["tap_b"], in_=b_sp))
            ld.dinc(SY.dma_start(out=taps["tap_c"], in_=Cq[:, :]))
            ld.dinc(SY.dma_start(out=taps["tap_dt"], in_=dt_raw[0:64, :]))
            SY.wait_ge(ld.h, ld.n)

    def phaseC(scope):
        dtv = scope.enter_context(nc.sbuf_tensor("dtv", [64, T], F32))
        acs = scope.enter_context(nc.sbuf_tensor("acs", [64, T], F32))
        eacs = scope.enter_context(nc.sbuf_tensor("eacs", [64, T], BF16))
        Sv = [scope.enter_context(nc.sbuf_tensor(f"Sv{i}", [128, NH * HD], F32)) for i in range(2)]
        Snb = [scope.enter_context(nc.sbuf_tensor(f"Snb{i}", [128, NH * HD], BF16)) for i in range(2)]
        xT = [scope.enter_context(nc.sbuf_tensor(f"xT{i}", [128, 2 * DIN], BF16)) for i in range(2)]
        xsrc = scope.enter_context(nc.sbuf_tensor("xsrc", [128, 32 * CH], BF16))
        identb = scope.enter_context(nc.sbuf_tensor("identb", [128, 128], BF16))
        BT = [scope.enter_context(nc.sbuf_tensor(f"BT{i}", [128, 256], BF16)) for i in range(2)]
        EAB = scope.enter_context(nc.sbuf_tensor("EAB", [128, 2 * 16 * CH], BF16))
        Yc = scope.enter_context(nc.sbuf_tensor("Yc", [128, 16 * CH], BF16))
        Gm = scope.enter_context(nc.sbuf_tensor("Gm", [128, 2 * CH], BF16))
        CqDup = scope.enter_context(nc.sbuf_tensor("CqDup", [128, 2 * CH], BF16))
        AcsT = scope.enter_context(nc.sbuf_tensor("AcsT", [128, 128], F32))
        dtT = scope.enter_context(nc.sbuf_tensor("dtT", [128, 128], F32))
        AmL = scope.enter_context(nc.sbuf_tensor("AmL", [128, 128], F32))
        ddtT = scope.enter_context(nc.sbuf_tensor("ddtT", [128, 128], F32))
        ddtmp = scope.enter_context(nc.sbuf_tensor("ddtmp", [128, 64], F32))
        cdB = scope.enter_context(nc.sbuf_tensor("cdB", [128, 64], F32))
        lrow = scope.enter_context(nc.sbuf_tensor("lrow", [1, 128], F32))
        fenceA = scope.enter_context(nc.sbuf_tensor("fenceA", [128, 4], BF16))
        rbc = scope.enter_context(nc.sbuf_tensor("rbc", [128, 8 * CH], F32))
        dbuf = scope.enter_context(nc.sbuf_tensor("dbuf", [128, NPAR * 1024], F32))
        ebuf = scope.enter_context(nc.sbuf_tensor("ebuf", [128, NPAR * 1024], BF16))
        tbuf = scope.enter_context(nc.sbuf_tensor("tbuf", [128, NPAR * 1024], BF16))
        cdbuf = scope.enter_context(nc.sbuf_tensor("cdbuf", [128, NPAR * 512], BF16))
        xdbuf = scope.enter_context(nc.sbuf_tensor("xdbuf", [128, NPAR * 256], BF16))
        pG = scope.enter_context(nc.psum_tensor("pG", [128, 512], F32))
        pYS = [scope.enter_context(nc.psum_tensor(f"pYS{i}", [128, 512], F32)) for i in range(NPAR)]
        pY = [b[:, 0:256] for b in pYS]
        pS = [b[:, 256:384] for b in pYS]
        pmC = scope.enter_context(nc.psum_tensor("pmC", [128, 128], F32))
        pT = [scope.enter_context(nc.psum_tensor(f"pT{i}", [128, 128], BF16)) for i in range(2)]

        ve.inc(VE.tensor_copy(identb[:, :], ident[:, :]))
        scrA = dbuf[0:64, 0:T]
        scrB = rbc[0:64, 0:T]
        ae.inc(SE.activation(scrA, dt_raw[0:64, :], AF.Abs, bias=dtb_col[0:64, 0:1]))
        ae.inc(SE.activation(scrB, scrA, AF.Exp, scale=-1.0))
        ae.inc(SE.activation(scrB, scrB, AF.Ln, bias=1.0))
        VE.wait_ge(ae.h, ae.n)
        VE.tensor_scalar(dtv[:, :], dt_raw[0:64, :], dtb_col[0:64, 0:1], 0.0, OP.add, OP.max)
        VE.tensor_tensor(dtv[:, :], dtv[:, :], scrB, OP.add)
        VE.tensor_scalar(dtv[:, :], dtv[:, :], 1e-20, None, OP.max)
        ve.inc(VE.tensor_scalar(scrA, dtv[:, :], nA_col[0:64, 0:1], None, OP.mult))
        for c in range(NCH):
            ve.inc(VE.tensor_tensor_scan(acs[0:64, c * CH:(c + 1) * CH],
                                         scrA[:, c * CH:(c + 1) * CH],
                                         scrB[:, c * CH:(c + 1) * CH], 0.0,
                                         OP.add, OP.bypass))
        SE.wait_ge(ve.h, ve.n)
        ae.inc(SE.activation(eacs[:, :], acs[0:64, :], AF.Exp))
        SY.wait_ge(ve.h, ve.n)
        acs_sp_tgt = ld.dinc(SY.dma_start(out=acs_sp, in_=acs[0:64, :]))
        SY.wait_ge(ae.h, ae.n)
        spl_tgt = ld.dinc(SY.dma_start(out=eacs_sp, in_=eacs[0:64, :]))
        ve.inc(VE.memset(Sv[0][:, :], 0.0))
        ve.inc(VE.memset(Snb[0][:, :], 0.0))

        xt_tgt = {}
        eab_tgt = {}
        rbc_tgt = {}
        diff_done = {}
        ops_done = {}
        mm_done = {}
        yevac_done = {}
        supd_done = {}
        exp_done = {}
        yspill_tgt = {}
        gm_done = {}

        xsrc_tgt = {}
        tp_ae_tgt = {}
        tp_state = {"n": 0}

        def issue_xsrc(c):
            if c >= NCH:
                return
            PO.wait_ge(pe.h, pe.n)
            for _rep in range(2):
                gd.dinc(PO.dma_start(
                    out=xsrc[:, :].rearrange("p (pb l) -> p pb l", l=CH),
                    in_=x_sp.rearrange("(pb p) t -> p pb t", p=128)[:, :, c * CH:(c + 1) * CH]))
            xsrc_tgt[c] = gd.n

        def emit_transposes(c, n):
            if c >= NCH:
                return
            bufc = c % 2
            for _ in range(n):
                i = tp_state["n"]
                if i >= 66:
                    return
                tp_state["n"] = i + 1
                bank = pT[i % 2]
                if i == 0:
                    TE.wait_ge(gd.h, xsrc_tgt[c])
                    TE.wait_ge(ve.h, ve.n)
                if i >= 2:
                    TE.wait_ge(ae.h, tp_ae_tgt[i - 2])
                if i < 64:
                    pb, sb2 = i // 2, i % 2
                    srcv = xsrc[:, pb * CH + sb2 * 128: pb * CH + sb2 * 128 + 128]
                    dstv = xT[bufc][:, sb2 * DIN + 128 * pb: sb2 * DIN + 128 * pb + 128]
                else:
                    sb2 = i - 64
                    srcv = Bq[:, c * CH + sb2 * 128: c * CH + sb2 * 128 + 128]
                    dstv = BT[bufc][:, sb2 * 128: sb2 * 128 + 128]
                pe.inc(TE.transpose(bank[:, :], srcv, identb[:, :]))
                SE.wait_ge(pe.h, pe.n)
                tp_ae_tgt[i] = ae.inc(SE.copy(dstv, bank[:, :]))
            if tp_state["n"] >= 66:
                xt_tgt[c] = ("ae", tp_ae_tgt[65])

        eab_user = {}

        def issue_eab(c, q):
            if c >= NCH:
                return
            half = q % 2
            PO.wait_ge(ld.h, spl_tgt)
            if eab_user.get(half) is not None:
                PO.wait_ge(ve.h, eab_user[half])
            gd.dinc(PO.dma_start(
                out=EAB[:, half * 16 * CH: (half + 1) * 16 * CH]
                    .rearrange("p (h l) -> p h l", l=CH),
                in_=bass.AP(eacs_sp.tensor, (q * 16) * T + c * CH,
                            [[0, 128], [T, 16], [1, CH]])))
            eab_tgt[(c, q)] = gd.n

        def issue_rbc(gj):
            c, j = divmod(gj, 32)
            if c >= NCH:
                return
            PO.wait_ge(ld.h, acs_sp_tgt)
            if gj >= 4:
                PO.wait_ge(ve.h, diff_done[gj - 4])
            slot = (2 * gj) % 8
            sd.dinc(PO.dma_start(
                out=rbc[:, slot * CH:(slot + 2) * CH].rearrange("p (hh l) -> p hh l", l=CH),
                in_=bass.AP(acs_sp.tensor, 2 * j * T + c * CH,
                            [[0, 128], [T, 2], [1, CH]])))
            rbc_tgt[gj] = sd.n

        def emit_diff(gj):
            c, j = divmod(gj, 32)
            if c >= NCH:
                return
            par = gj % NPAR
            VE.wait_ge(sd.h, min(rbc_tgt[gj] + 32, max(rbc_tgt.values())))
            if gj >= NPAR:
                VE.wait_ge(ae.h, exp_done[gj - NPAR])
            lastd = None
            for hh in range(2):
                h = 2 * j + hh
                slot = (2 * gj + hh) % 8
                for sb2 in range(2):
                    lastd = VE.tensor_scalar(
                        dbuf[:, par * 1024 + (2 * hh + sb2) * CH: par * 1024 + (2 * hh + sb2 + 1) * CH],
                        rbc[:, slot * CH:(slot + 1) * CH],
                        AmL[:, sb2 * 64 + h: sb2 * 64 + h + 1], 30.0,
                        OP.subtract, OP.min)
            diff_done[gj] = ve.inc(lastd)

        issue_xsrc(0)
        emit_transposes(0, 66)
        tp_state["n"] = 0
        issue_eab(0, 0)
        issue_eab(0, 1)
        for g0 in range(4):
            issue_rbc(g0)

        for c in range(NCH):
            cT = c * CH
            bufc = c % 2
            Sprev, Snext = Sv[c % 2], Sv[1 - c % 2]
            Sbprev, Sbnext = Snb[c % 2], Snb[1 - c % 2]
            SE.wait_ge(ld.h, acs_sp_tgt)
            for _rep in range(3):
                lr_tgt = sd.dinc(SE.dma_start(
                    out=lrow[0:1, 0:64],
                    in_=acs_sp.rearrange("h t -> (h t)")[cT + 255::T][0:64].unsqueeze(0),
                    allow_slow_non_contiguous=True))
            if c >= 1:
                TE.wait_ge(ve.h, gm_done[c - 1])
            TE.wait_ge(ae.h, xt_tgt[c][1])
            VE.wait_ge(ae.h, xt_tgt[c][1])
            if c + 1 < NCH:
                issue_xsrc(c + 1)
            tp_state["n"] = 0
            gmm = None
            for sb2 in range(2):
                gmm = TE.matmul(pG[:, sb2 * CH:(sb2 + 1) * CH],
                                Bq[:, cT + 128 * sb2: cT + 128 * sb2 + 128],
                                Cq[:, cT:cT + CH], start=True, stop=True)
            g_mm = pe.inc(gmm)
            VE.wait_ge(pe.h, g_mm)
            lastg = None
            for sb2 in range(2):
                lastg = VE.tensor_tensor(Gm[:, sb2 * CH:(sb2 + 1) * CH], pG[:, sb2 * CH:(sb2 + 1) * CH],
                                         trimask[:, sb2 * CH:(sb2 + 1) * CH], OP.mult)
            gm_done[c] = ve.inc(lastg)
            VE.tensor_copy(CqDup[:, 0:CH], Cq[:, cT:cT + CH])
            ve.inc(VE.tensor_copy(CqDup[:, CH:2 * CH], Cq[:, cT:cT + CH]))
            if c >= 1:
                SE.wait_ge(ve.h, supd_done[(c - 1) * 32 + 31])
                TE.wait_ge(ve.h, supd_done[(c - 1) * 32 + 31])
            tp_ae = None
            for (srct, dstt) in ((acs, AcsT), (dtv, dtT)):
                for sb2 in range(2):
                    if tp_ae is not None:
                        TE.wait_ge(ae.h, tp_ae)
                    pe.inc(TE.transpose(pmC[0:128, 0:64],
                                        srct[0:64, cT + 128 * sb2: cT + 128 * sb2 + 128],
                                        ident[0:64, 0:64]))
                    SE.wait_ge(pe.h, pe.n)
                    tp_ae = ae.inc(SE.copy(dstt[:, sb2 * 64:(sb2 + 1) * 64], pmC[0:128, 0:64]))
            SE.wait_ge(pe.h, pe.n)
            ae.inc(SE.activation(AmL[:, :], dtT[:, :], AF.Ln))
            VE.wait_ge(ae.h, ae.n)
            ve.inc(VE.tensor_tensor(AmL[:, :], AcsT[:, :], AmL[:, :], OP.subtract))
            SE.wait_ge(sd.h, lr_tgt)
            ae.inc(SE.activation(lrow[0:1, 64:128], lrow[0:1, 0:64], AF.Exp))
            TE.wait_ge(ae.h, ae.n)
            lb_mm = pe.inc(TE.matmul(pmC[:, 0:64], ones1[0:1, :], lrow[0:1, 0:64],
                                     start=True, stop=True))
            VE.wait_ge(pe.h, lb_mm)
            VE.wait_ge(ae.h, ae.n)
            for sb2 in range(2):
                ve.inc(VE.tensor_tensor(ddtmp[:, 0:64], pmC[:, 0:64],
                                        AcsT[:, sb2 * 64:(sb2 + 1) * 64], OP.subtract))
                SE.wait_ge(ve.h, ve.n)
                ae.inc(SE.activation(ddtT[:, sb2 * 64:(sb2 + 1) * 64], ddtmp[:, 0:64], AF.Exp))
                VE.wait_ge(ae.h, ae.n)
                ve.inc(VE.tensor_tensor(ddtT[:, sb2 * 64:(sb2 + 1) * 64],
                                        ddtT[:, sb2 * 64:(sb2 + 1) * 64],
                                        dtT[:, sb2 * 64:(sb2 + 1) * 64], OP.mult))
            TE.wait_ge(ve.h, ve.n)
            cd_mm = pe.inc(TE.matmul(pmC[:, 64:128], ones1[0:1, :], lrow[0:1, 64:128],
                                     start=True, stop=True))
            SE.wait_ge(pe.h, cd_mm)
            cd_ae = ae.inc(SE.copy(cdB[:, :], pmC[:, 64:128]))
            VE.wait_ge(ae.h, cd_ae)
            emit_diff(c * 32 + 0)
            issue_rbc(c * 32 + 4)
            for j in range(32):
                gj = c * 32 + j
                par = gj % NPAR
                SE.wait_ge(ve.h, diff_done[gj])
                SE.activation(ebuf[:, par * 1024:(par + 1) * 1024],
                              dbuf[:, par * 1024:(par + 1) * 1024], AF.Exp)
                exp_done[gj] = ae.inc(SE.copy(fenceA[:, 0:1],
                                              ebuf[:, par * 1024 + 1023: par * 1024 + 1024]))
                if j < 31:
                    emit_diff(gj + 1)
                    issue_rbc(gj + 5)
                VE.wait_ge(ae.h, exp_done[gj])
                if gj >= NPAR:
                    VE.wait_ge(pe.h, mm_done[gj - NPAR])
                if j % 8 == 0:
                    VE.wait_ge(gd.h, eab_tgt[(c, j // 8)])
                for hh in range(2):
                    h = 2 * j + hh
                    VE.tensor_tensor(
                        tbuf[:, par * 1024 + 2 * hh * CH: par * 1024 + (2 * hh + 2) * CH],
                        ebuf[:, par * 1024 + 2 * hh * CH: par * 1024 + (2 * hh + 2) * CH],
                        Gm[:, 0:2 * CH], OP.mult)
                    if hh == 0:
                        VE.tensor_tensor(
                            cdbuf[:, par * 512: par * 512 + 2 * CH],
                            CqDup[:, :],
                            EAB[:, (((2 * j) // 16) % 2) * 16 * CH + ((2 * j) % 16) * CH:
                                 (((2 * j) // 16) % 2) * 16 * CH + ((2 * j) % 16 + 2) * CH],
                            OP.mult)
                    for sb2 in range(2):
                        lasto = VE.tensor_scalar(
                            xdbuf[:, par * 256 + sb2 * 128 + hh * 64: par * 256 + sb2 * 128 + hh * 64 + 64],
                            xT[bufc][:, sb2 * DIN + h * 64: sb2 * DIN + h * 64 + 64],
                            ddtT[:, sb2 * 64 + h: sb2 * 64 + h + 1], None, OP.mult)
                ops_done[gj] = ve.inc(lasto)
                if j % 8 == 7:
                    q = j // 8
                    eab_user[q % 2] = ops_done[gj]
                    if q + 2 <= 3:
                        issue_eab(c, q + 2)
                    else:
                        issue_eab(c + 1, q + 2 - 4)
                if c + 1 < NCH:
                    emit_transposes(c + 1, 3)
                TE.wait_ge(ve.h, ops_done[gj])
                if gj >= NPAR:
                    TE.wait_ge(ve.h, yevac_done[gj - NPAR])
                    TE.wait_ge(ve.h, supd_done[gj - NPAR])
                lastmm = None
                for hh in range(2):
                    h = 2 * j + hh
                    yr = pY[par][hh * 64:(hh + 1) * 64, :]
                    TE.matmul(yr, xT[bufc][:, 0 * DIN + h * 64: 0 * DIN + h * 64 + 64],
                              tbuf[:, par * 1024 + (2 * hh) * CH: par * 1024 + (2 * hh) * CH + CH],
                              start=True, stop=False)
                    TE.matmul(yr, xT[bufc][:, 1 * DIN + h * 64: 1 * DIN + h * 64 + 64],
                              tbuf[:, par * 1024 + (2 * hh + 1) * CH: par * 1024 + (2 * hh + 1) * CH + CH],
                              start=False, stop=False)
                    TE.matmul(yr, Sbprev[:, h * 64:(h + 1) * 64],
                              cdbuf[:, par * 512 + hh * CH: par * 512 + (hh + 1) * CH],
                              start=False, stop=True)
                    sr = pS[par][:, hh * 64:(hh + 1) * 64]
                    TE.matmul(sr, BT[bufc][:, 0:128],
                              xdbuf[:, par * 256 + 0 * 128 + hh * 64: par * 256 + 0 * 128 + hh * 64 + 64],
                              start=True, stop=False)
                    lastmm = TE.matmul(sr, BT[bufc][:, 128:256],
                                       xdbuf[:, par * 256 + 1 * 128 + hh * 64: par * 256 + 1 * 128 + hh * 64 + 64],
                                       start=False, stop=True)
                mm_done[gj] = pe.inc(lastmm)

                def emit_evac(gje):
                    ce, je = divmod(gje, 32)
                    pare = gje % NPAR
                    VE.wait_ge(pe.h, mm_done[gje])
                    if je == 0 and ce >= 1:
                        VE.wait_ge(ld.h, yspill_tgt[(ce - 1, 1)])
                    if je == 16:
                        VE.wait_ge(ld.h, yspill_tgt[(ce, 0)])
                    ve.inc(VE.tensor_copy(Yc[:, (je % 16) * CH:((je % 16) + 1) * CH], pY[pare][:, :]))
                    yevac_done[gje] = ve.n
                    if je == 15 or je == 31:
                        hf = je // 16
                        SY.wait_ge(ve.h, yevac_done[gje])
                        yspill_tgt[(ce, hf)] = ld.dinc(SY.dma_start(
                            out=y_sp.rearrange("(q p) t -> p q t", p=128)[:, hf * 16:(hf + 1) * 16,
                                                                         ce * CH:ce * CH + CH],
                            in_=Yc.rearrange("p (q l) -> p q l", l=CH)))
                    h0 = 2 * je
                    for hh in range(2):
                        h = h0 + hh
                        VE.tensor_scalar(Snext[:, h * 64:(h + 1) * 64], Sprev[:, h * 64:(h + 1) * 64],
                                         cdB[:, h:h + 1], None, OP.mult)
                    VE.tensor_tensor(Snext[:, h0 * 64:h0 * 64 + 128], Snext[:, h0 * 64:h0 * 64 + 128],
                                     pS[pare][:, :], OP.add)
                    supd_done[gje] = ve.inc(VE.tensor_copy(Sbnext[:, h0 * 64:h0 * 64 + 128],
                                                           Snext[:, h0 * 64:h0 * 64 + 128]))

                if j >= 1:
                    emit_evac(gj - 1)
                if j == 31:
                    emit_evac(gj)
                if PARANOID == 1:
                    for eng in (TE, VE, SE, PO, SY):
                        eng.wait_ge(ve.h, ve.n)
                        eng.wait_ge(pe.h, pe.n)
                        eng.wait_ge(ae.h, ae.n)
                        eng.wait_ge(ld.h, ld.n)
                        eng.wait_ge(sd.h, sd.n)
                        eng.wait_ge(gd.h, gd.n)
                elif PARANOID == 2:
                    for eng in (TE, VE, SE):
                        eng.wait_ge(ve.h, ve.n)
                        eng.wait_ge(pe.h, pe.n)
                        eng.wait_ge(ae.h, ae.n)
                elif PARANOID == 3:
                    for eng in (TE, VE, SE):
                        eng.wait_ge(sd.h, sd.n)
                        eng.wait_ge(gd.h, gd.n)
                        eng.wait_ge(ld.h, ld.n)
                elif PARANOID == 4:
                    for eng in (TE, VE, SE):
                        eng.wait_ge(sd.h, sd.n)
                elif PARANOID == 5:
                    for eng in (TE, VE, SE):
                        eng.wait_ge(gd.h, gd.n)
                elif PARANOID == 6:
                    for eng in (TE, VE, SE):
                        eng.wait_ge(ld.h, ld.n)
            if c + 1 < NCH:
                emit_transposes(c + 1, 66)
        SY.wait_ge(ld.h, ld.n)
        if DEBUG:
            SY.wait_ge(ve.h, ve.n)
            ld.dinc(SY.dma_start(out=taps["tap_acs"], in_=acs[0:64, :]))
            ld.dinc(SY.dma_start(out=taps["tap_y"], in_=y_sp))
            SY.wait_ge(ld.h, ld.n)

    def phaseD(scope, Ygn, rstdB):
        yb = scope.enter_context(nc.sbuf_tensor("ybD", [128, 2 * T], BF16))
        zb = scope.enter_context(nc.sbuf_tensor("zbD", [128, 2 * T], BF16))
        xb = scope.enter_context(nc.sbuf_tensor("xbD", [128, 2 * T], BF16))
        szb = scope.enter_context(nc.sbuf_tensor("szbD", [128, 2 * T], BF16))
        sqb = scope.enter_context(nc.sbuf_tensor("sqbD", [128, 2 * T], BF16))
        rowD = scope.enter_context(nc.sbuf_tensor("rowD", [1, T], F32))
        pm = [scope.enter_context(nc.psum_tensor(f"pmD{i}", [1, 512], F32)) for i in range(4)]

        ld_tgt, dve_tgt, sq_tgt, mm_tgt, sz_tgt, xld_tgt = {}, {}, {}, {}, {}, {}
        for fc in range(32):
            slot = fc % 2
            if fc >= 2:
                SY.wait_ge(ve.h, dve_tgt[fc - 2])
                SY.wait_ge(ae.h, sz_tgt[fc - 2])
            SY.dma_start(out=yb[:, slot * T:slot * T + T],
                         in_=y_sp[128 * fc:128 * fc + 128, :]).then_inc(ld.h, 16)
            SY.dma_start(out=zb[:, slot * T:slot * T + T],
                         in_=z_sp[128 * fc:128 * fc + 128, :]).then_inc(ld.h, 16)
            ld.n += 32
            ld_tgt[fc] = ld.n
            if fc >= 2:
                SE.wait_ge(ve.h, dve_tgt[fc - 2])
            sd.dinc(SE.dma_start(out=xb[:, slot * T:slot * T + T],
                                 in_=x_sp[128 * fc:128 * fc + 128, :]))
            xld_tgt[fc] = sd.n
            SE.wait_ge(ld.h, ld_tgt[fc])
            if fc >= 2:
                SE.wait_ge(ve.h, dve_tgt[fc - 2])
            sz_tgt[fc] = ae.inc(SE.activation(szb[:, slot * T:slot * T + T],
                                              zb[:, slot * T:slot * T + T], AF.Silu))
            VE.wait_ge(ld.h, ld_tgt[fc])
            VE.wait_ge(sd.h, min(xld_tgt[fc] + 32, max(xld_tgt.values()) if len(xld_tgt) else 0) if fc >= 0 else 0)
            VE.wait_ge(ae.h, sz_tgt[fc])
            VE.scalar_tensor_tensor(Ygn[:, fc * T:(fc + 1) * T], xb[:, slot * T:slot * T + T],
                                    DExp[:, fc:fc + 1], yb[:, slot * T:slot * T + T],
                                    OP.mult, OP.add)
            dve_tgt[fc] = ve.inc(VE.tensor_tensor(Ygn[:, fc * T:(fc + 1) * T],
                                                  Ygn[:, fc * T:(fc + 1) * T],
                                                  szb[:, slot * T:slot * T + T], OP.mult))
            if fc >= 2:
                SE.wait_ge(pe.h, mm_tgt[fc - 2])
            SE.wait_ge(ve.h, dve_tgt[fc])
            sq_tgt[fc] = ae.inc(SE.activation(sqb[:, slot * T:slot * T + T],
                                              Ygn[:, fc * T:(fc + 1) * T], AF.Square))
            ve.inc(VE.tensor_scalar(Ygn[:, fc * T:(fc + 1) * T], Ygn[:, fc * T:(fc + 1) * T],
                                    nwcols[:, fc:fc + 1], None, OP.mult))
            TE.wait_ge(ae.h, sq_tgt[fc])
            lastq = None
            for t in range(NT):
                lastq = TE.matmul(pm[t][0:1, :], onesb[:, 0:1],
                                  sqb[:, slot * T + 512 * t: slot * T + 512 * t + 512],
                                  start=(fc == 0), stop=(fc == 31))
            mm_tgt[fc] = pe.inc(lastq)
        VE.wait_ge(pe.h, pe.n)
        lastr = None
        for t in range(NT):
            lastr = VE.tensor_scalar(rowD[0:1, 512 * t:512 * t + 512], pm[t][0:1, :],
                                     1.0 / DIN, EPS, OP.mult, OP.add)
        ve.inc(lastr)
        SE.wait_ge(ve.h, ve.n)
        ae.inc(SE.activation(rowD[0:1, :], rowD[0:1, :], AF.Sqrt))
        VE.wait_ge(ae.h, ae.n)
        ve.inc(VE.reciprocal(rowD[0:1, :], rowD[0:1, :]))
        SY.wait_ge(ve.h, ve.n)
        ld.dinc(SY.dma_start(out=row_sp[2:3, :], in_=rowD[0:1, :]))
        PO.wait_ge(ld.h, ld.n)
        for _rep in range(2):
            gd.dinc(PO.dma_start(out=rstdB[:, :], in_=bass.AP(row_sp.tensor, 2 * T, [[0, 128], [1, T]])))
        VE.wait_ge(gd.h, gd.n)
        SY.wait_ge(ld.h, ld.n)

    def phaseE(scope, Ygn, rstdB):
        wo = scope.enter_context(nc.sbuf_tensor("woE", [128, 2 * 32 * 128], BF16))
        mstage = scope.enter_context(nc.sbuf_tensor("mstageE", [128, 2 * T], BF16))
        sqb = scope.enter_context(nc.sbuf_tensor("sqbE", [128, 2 * 512], BF16))
        statacc = scope.enter_context(nc.sbuf_tensor("staccE", [1, 2 * T], F32))
        pE = [scope.enter_context(nc.psum_tensor(f"pE{i}", [128, 512], F32)) for i in range(4)]
        pstA = scope.enter_context(nc.psum_tensor("pstAE", [1, 512], F32))
        pstB = scope.enter_context(nc.psum_tensor("pstBE", [1, 512], F32))

        ve.inc(VE.memset(statacc[0:1, :], 0.0))
        slab_tgt, mm_tgt, ev_tgt, st_tgt, sadd_tgt, spl_tgt = {}, {}, {}, {}, {}, {}

        def issue_wo(db):
            half = (db % 2) * 32 * 128
            if db >= 2:
                PO.wait_ge(pe.h, mm_tgt[(db - 2, NT - 1)])
            gd.dinc(PO.dma_start(
                out=wo[:, half:half + 32 * 128].rearrange("p (kc m) -> p kc m", m=128),
                in_=d_wout[:, 128 * db:128 * db + 128].rearrange("(kc p) m -> p kc m", p=128)))
            slab_tgt[db] = gd.n

        issue_wo(0)
        issue_wo(1)
        for db in range(DB):
            half = (db % 2) * 32 * 128
            TE.wait_ge(gd.h, slab_tgt[db])
            for t in range(NT):
                if db >= 1:
                    TE.wait_ge(ve.h, ev_tgt[(db - 1, t)])
                lastm = None
                for kc in range(32):
                    lastm = TE.matmul(pE[t][:, :], wo[:, half + kc * 128: half + (kc + 1) * 128],
                                      Ygn[:, kc * T + 512 * t: kc * T + 512 * t + 512],
                                      start=(kc == 0), stop=(kc == 31))
                mm_tgt[(db, t)] = pe.inc(lastm)
            slot = db % 2
            for t in range(NT):
                VE.wait_ge(pe.h, mm_tgt[(db, t)])
                if db >= 2:
                    VE.wait_ge(ld.h, spl_tgt[db - 2])
                ev_tgt[(db, t)] = ve.inc(VE.tensor_tensor(
                    mstage[:, slot * T + 512 * t: slot * T + 512 * t + 512],
                    pE[t][:, :], rstdB[:, 512 * t:512 * t + 512], OP.mult))
                SE.wait_ge(ve.h, ev_tgt[(db, t)])
                if t >= 2:
                    SE.wait_ge(pe.h, st_tgt[(db, t - 2)])
                elif db >= 1:
                    SE.wait_ge(pe.h, st_tgt[(db - 1, t + 2)])
                ae.inc(SE.activation(sqb[:, (t % 2) * 512:(t % 2) * 512 + 512],
                                     mstage[:, slot * T + 512 * t: slot * T + 512 * t + 512],
                                     AF.Square))
                TE.wait_ge(ae.h, ae.n)
                if db >= 1 or t >= 1:
                    TE.wait_ge(ve.h, sadd_tgt[(db, t - 1) if t >= 1 else (db - 1, NT - 1)])
                TE.matmul(pstA[0:1, :], onesb[:, 0:1],
                          mstage[:, slot * T + 512 * t: slot * T + 512 * t + 512],
                          start=True, stop=True)
                st_tgt[(db, t)] = pe.inc(TE.matmul(pstB[0:1, :], onesb[:, 0:1],
                                                   sqb[:, (t % 2) * 512:(t % 2) * 512 + 512],
                                                   start=True, stop=True))
                VE.wait_ge(pe.h, st_tgt[(db, t)])
                VE.tensor_tensor(statacc[0:1, 512 * t:512 * t + 512],
                                 statacc[0:1, 512 * t:512 * t + 512], pstA[0:1, :], OP.add)
                sadd_tgt[(db, t)] = ve.inc(VE.tensor_tensor(
                    statacc[0:1, T + 512 * t:T + 512 * t + 512],
                    statacc[0:1, T + 512 * t:T + 512 * t + 512], pstB[0:1, :], OP.add))
            SY.wait_ge(ve.h, ev_tgt[(db, NT - 1)])
            spl_tgt[db] = ld.dinc(SY.dma_start(out=mix_sp[128 * db:128 * db + 128, :],
                                               in_=mstage[:, slot * T:slot * T + T]))
            if db + 2 < DB:
                issue_wo(db + 2)
        VE.wait_ge(ve.h, ve.n)
        VE.tensor_scalar(statacc[0:1, 0:T], statacc[0:1, 0:T], 1.0 / DM, None, OP.mult)
        VE.tensor_scalar(statacc[0:1, T:2 * T], statacc[0:1, T:2 * T], 1.0 / DM, None, OP.mult)
        VE.tensor_tensor(sqb[0:1, 0:512].bitcast(F32) if False else rowscratchE[0:1, 0:T],
                         statacc[0:1, 0:T], statacc[0:1, 0:T], OP.mult)
        VE.tensor_tensor(statacc[0:1, T:2 * T], statacc[0:1, T:2 * T], rowscratchE[0:1, 0:T], OP.subtract)
        ve.inc(VE.tensor_scalar(statacc[0:1, T:2 * T], statacc[0:1, T:2 * T], EPS, None, OP.add))
        SE.wait_ge(ve.h, ve.n)
        ae.inc(SE.activation(statacc[0:1, T:2 * T], statacc[0:1, T:2 * T], AF.Sqrt))
        VE.wait_ge(ae.h, ae.n)
        ve.inc(VE.reciprocal(statacc[0:1, T:2 * T], statacc[0:1, T:2 * T]))
        SY.wait_ge(ve.h, ve.n)
        ld.dinc(SY.dma_start(out=row_sp.rearrange("r t -> (r t)")[3 * T:5 * T].unsqueeze(0),
                             in_=statacc[0:1, :]))
        SY.wait_ge(ld.h, ld.n)
        if DEBUG:
            ld.dinc(SY.dma_start(out=taps["tap_mix"], in_=mix_sp))
            SY.wait_ge(ld.h, ld.n)

    def phaseF(scope, xn1):
        x1b = scope.enter_context(nc.sbuf_tensor("x1bF", [128, KC * T], BF16))
        mixb = scope.enter_context(nc.sbuf_tensor("mixbF", [128, 2 * T], BF16))
        ub = scope.enter_context(nc.sbuf_tensor("ubF", [128, 2 * T], F32))
        x1st = scope.enter_context(nc.sbuf_tensor("x1stF", [128, 2 * T], F32))
        mBf = scope.enter_context(nc.sbuf_tensor("mBf", [128, T], BF16))
        rBf = scope.enter_context(nc.sbuf_tensor("rBf", [128, T], BF16))
        pm = [scope.enter_context(nc.psum_tensor(f"pmF{i}", [1, 512], F32)) for i in range(4)]
        ps = [scope.enter_context(nc.psum_tensor(f"psF{i}", [1, 512], F32)) for i in range(4)]

        PO.wait_ge(ld.h, ld.n)
        for _rep in range(2):
            gd.dinc(PO.dma_start(out=mBf[:, :], in_=bass.AP(row_sp.tensor, 3 * T, [[0, 128], [1, T]])))
            gd.dinc(PO.dma_start(out=rBf[:, :], in_=bass.AP(row_sp.tensor, 4 * T, [[0, 128], [1, T]])))
        VE.wait_ge(gd.h, gd.n)
        ld_tgt, dve_tgt, sq_tgt, mm_tgt, spl_tgt = {}, {}, {}, {}, {}
        for dc in range(KC):
            slot = dc % 2
            if dc >= 2:
                SY.wait_ge(ve.h, dve_tgt[dc - 2])
                SY.wait_ge(ld.h, spl_tgt[dc - 2])
            SY.dma_start(out=mixb[:, slot * T:slot * T + T],
                         in_=mix_sp[128 * dc:128 * dc + 128, :]).then_inc(ld.h, 16)
            SY.dma_start(out=ub[:, slot * T:slot * T + T],
                         in_=d_uT[128 * dc:128 * dc + 128, :]).then_inc(ld.h, 16)
            ld.n += 32
            ld_tgt[dc] = ld.n
            VE.wait_ge(ld.h, ld_tgt[dc])
            VE.tensor_tensor(mixb[:, slot * T:slot * T + T], mixb[:, slot * T:slot * T + T],
                             mBf[:, :], OP.subtract)
            VE.tensor_tensor(mixb[:, slot * T:slot * T + T], mixb[:, slot * T:slot * T + T],
                             rBf[:, :], OP.mult)
            VE.tensor_tensor(x1st[:, slot * T:slot * T + T], ub[:, slot * T:slot * T + T],
                             mixb[:, slot * T:slot * T + T], OP.add)
            dve_tgt[dc] = ve.inc(VE.tensor_copy(x1b[:, dc * T:(dc + 1) * T],
                                                x1st[:, slot * T:slot * T + T]))
            SY.wait_ge(ve.h, dve_tgt[dc])
            spl_tgt[dc] = ld.dinc(SY.dma_start(out=x1_sp[128 * dc:128 * dc + 128, :],
                                               in_=x1st[:, slot * T:slot * T + T]))
            if dc >= 2:
                SE.wait_ge(pe.h, mm_tgt[dc - 2])
            SE.wait_ge(ve.h, dve_tgt[dc])
            sq_tgt[dc] = ae.inc(SE.activation(mixb[:, slot * T:slot * T + T],
                                              x1b[:, dc * T:(dc + 1) * T], AF.Square))
            TE.wait_ge(ae.h, sq_tgt[dc])
            lastq = None
            for t in range(NT):
                TE.matmul(pm[t][0:1, :], onesb[:, 0:1],
                          x1b[:, dc * T + 512 * t: dc * T + 512 * t + 512],
                          start=(dc == 0), stop=(dc == KC - 1))
                lastq = TE.matmul(ps[t][0:1, :], onesb[:, 0:1],
                                  mixb[:, slot * T + 512 * t: slot * T + 512 * t + 512],
                                  start=(dc == 0), stop=(dc == KC - 1))
            mm_tgt[dc] = pe.inc(lastq)
        VE.wait_ge(pe.h, pe.n)
        rows = x1st[0:1, 0:2 * T]
        for t in range(NT):
            VE.tensor_scalar(rows[0:1, 512 * t:512 * t + 512], pm[t][0:1, :], 1.0 / DM, None, OP.mult)
            ve.inc(VE.tensor_scalar(rows[0:1, T + 512 * t:T + 512 * t + 512], ps[t][0:1, :],
                                    1.0 / DM, None, OP.mult))
        VE.tensor_tensor(ub[0:1, 0:T], rows[0:1, 0:T], rows[0:1, 0:T], OP.mult)
        VE.tensor_tensor(rows[0:1, T:2 * T], rows[0:1, T:2 * T], ub[0:1, 0:T], OP.subtract)
        ve.inc(VE.tensor_scalar(rows[0:1, T:2 * T], rows[0:1, T:2 * T], EPS, None, OP.add))
        SE.wait_ge(ve.h, ve.n)
        ae.inc(SE.activation(rows[0:1, T:2 * T], rows[0:1, T:2 * T], AF.Sqrt))
        VE.wait_ge(ae.h, ae.n)
        ve.inc(VE.reciprocal(rows[0:1, T:2 * T], rows[0:1, T:2 * T]))
        SY.wait_ge(ve.h, ve.n)
        ld.dinc(SY.dma_start(out=row_sp.rearrange("r t -> (r t)")[5 * T:7 * T].unsqueeze(0),
                             in_=rows[0:1, :]))
        PO.wait_ge(ld.h, ld.n)
        for _rep in range(2):
            gd.dinc(PO.dma_start(out=mBf[:, :], in_=bass.AP(row_sp.tensor, 5 * T, [[0, 128], [1, T]])))
            gd.dinc(PO.dma_start(out=rBf[:, :], in_=bass.AP(row_sp.tensor, 6 * T, [[0, 128], [1, T]])))
        VE.wait_ge(gd.h, gd.n)
        for dc in range(KC):
            VE.tensor_tensor(xn1[:, dc * T:(dc + 1) * T], x1b[:, dc * T:(dc + 1) * T],
                             mBf[:, :], OP.subtract)
            ve.inc(VE.tensor_tensor(xn1[:, dc * T:(dc + 1) * T], xn1[:, dc * T:(dc + 1) * T],
                                    rBf[:, :], OP.mult))
        if DEBUG:
            SY.wait_ge(ve.h, ve.n)
            ld.dinc(SY.dma_start(out=taps["tap_x1"], in_=x1_sp))
        SY.wait_ge(ld.h, ld.n)

    def phaseG(scope, xn1):
        wg = scope.enter_context(nc.sbuf_tensor("wgG", [128, 2 * KC * 128], BF16))
        wu = scope.enter_context(nc.sbuf_tensor("wuG", [128, 2 * KC * 128], BF16))
        sg = scope.enter_context(nc.sbuf_tensor("sgG", [128, 2 * 512], BF16))
        hstage = scope.enter_context(nc.sbuf_tensor("hstageG", [128, 2 * T], BF16))
        pGt = [scope.enter_context(nc.psum_tensor(f"pGt{i}", [128, 512], F32)) for i in range(4)]
        pUt = [scope.enter_context(nc.psum_tensor(f"pUt{i}", [128, 512], F32)) for i in range(4)]

        SY.wait_ge(cc.h, 1)
        slab_tgt, mm_tgt, h_tgt, spl_tgt, sg_tgt = {}, {}, {}, {}, {}

        def issue_gu(ob):
            half = (ob % 2) * KC * 128
            if ob >= 2:
                SY.wait_ge(pe.h, mm_tgt[(ob - 2, NT - 1)])
                SE.wait_ge(pe.h, mm_tgt[(ob - 2, NT - 1)])
            sd.dinc(SE.dma_start(
                out=wg[:, half:half + KC * 128].rearrange("p (kc m) -> p kc m", m=128),
                in_=w_full[(ob * KC) * TILE_EL:(ob * KC + KC) * TILE_EL]
                    .rearrange("(kc p m) -> p kc m", p=128, m=128)))
            ld.dinc(SY.dma_start(
                out=wu[:, half:half + KC * 128].rearrange("p (kc m) -> p kc m", m=128),
                in_=w_full[(GATE_TILES + ob * KC) * TILE_EL:(GATE_TILES + ob * KC + KC) * TILE_EL]
                    .rearrange("(kc p m) -> p kc m", p=128, m=128)))
            slab_tgt[ob] = (sd.n, ld.n)

        issue_gu(0)
        issue_gu(1)
        for ob in range(OB):
            half = (ob % 2) * KC * 128
            slot = ob % 2
            TE.wait_ge(sd.h, slab_tgt[ob][0])
            TE.wait_ge(ld.h, slab_tgt[ob][1])
            for t in range(NT):
                if ob >= 1:
                    TE.wait_ge(ve.h, h_tgt[(ob - 1, t)])
                for kc in range(KC):
                    TE.matmul(pGt[t][:, :], wg[:, half + kc * 128: half + (kc + 1) * 128],
                              xn1[:, kc * T + 512 * t: kc * T + 512 * t + 512],
                              start=(kc == 0), stop=(kc == KC - 1))
                lastm = None
                for kc in range(KC):
                    lastm = TE.matmul(pUt[t][:, :], wu[:, half + kc * 128: half + (kc + 1) * 128],
                                      xn1[:, kc * T + 512 * t: kc * T + 512 * t + 512],
                                      start=(kc == 0), stop=(kc == KC - 1))
                mm_tgt[(ob, t)] = pe.inc(lastm)
            for t in range(NT):
                SE.wait_ge(pe.h, mm_tgt[(ob, t)])
                if t >= 2:
                    SE.wait_ge(ve.h, h_tgt[(ob, t - 2)])
                elif ob >= 1:
                    SE.wait_ge(ve.h, h_tgt[(ob - 1, t + 2)])
                sg_tgt[(ob, t)] = ae.inc(SE.activation(sg[:, (t % 2) * 512:(t % 2) * 512 + 512],
                                                       pGt[t][:, :], AF.Silu))
                VE.wait_ge(ae.h, sg_tgt[(ob, t)])
                if ob >= 2:
                    VE.wait_ge(ld.h, spl_tgt[ob - 2])
                h_tgt[(ob, t)] = ve.inc(VE.tensor_tensor(
                    hstage[:, slot * T + 512 * t: slot * T + 512 * t + 512],
                    sg[:, (t % 2) * 512:(t % 2) * 512 + 512], pUt[t][:, :], OP.mult))
            SY.wait_ge(ve.h, h_tgt[(ob, NT - 1)])
            spl_tgt[ob] = ld.dinc(SY.dma_start(out=h_sp[128 * ob:128 * ob + 128, :],
                                               in_=hstage[:, slot * T:slot * T + T]))
            if ob + 2 < OB:
                issue_gu(ob + 2)
        SY.wait_ge(ld.h, ld.n)
        if DEBUG:
            ld.dinc(SY.dma_start(out=taps["tap_h"], in_=h_sp))
            SY.wait_ge(ld.h, ld.n)

    def phaseH(scope):
        hT = scope.enter_context(nc.sbuf_tensor("hTH", [128, KO * 1024], BF16))
        wd = scope.enter_context(nc.sbuf_tensor("wdH", [128, 2 * KO * 128], BF16))
        fstage = scope.enter_context(nc.sbuf_tensor("fstageH", [128, 2 * 1024], BF16))
        sqb = scope.enter_context(nc.sbuf_tensor("sqbH", [128, 2 * 512], BF16))
        statacc = scope.enter_context(nc.sbuf_tensor("staccH", [1, 2 * T], F32))
        rscr = scope.enter_context(nc.sbuf_tensor("rscrH", [1, T], F32))
        pH = [scope.enter_context(nc.psum_tensor(f"pH{i}", [128, 512], F32)) for i in range(4)]
        pstA = scope.enter_context(nc.psum_tensor("pstAH", [1, 512], F32))
        pstB = scope.enter_context(nc.psum_tensor("pstBH", [1, 512], F32))

        ve.inc(VE.memset(statacc[0:1, :], 0.0))
        slab_tgt, mm_tgt, ev_tgt, st_tgt, sadd_tgt, spl_tgt, ht_tgt = {}, {}, {}, {}, {}, {}, {}

        def issue_wd(th, db):
            half = (db % 2) * KO * 128
            if db >= 2:
                PO.wait_ge(pe.h, mm_tgt[(th, db - 2)][1])
            elif th >= 1:
                PO.wait_ge(pe.h, mm_tgt[(th - 1, DB - 2 + db)][1])
            gd.dinc(PO.dma_start(
                out=wd[:, half:half + KO * 128].rearrange("p (kc m) -> p kc m", m=128),
                in_=w_full[(2 * GATE_TILES + db * KO) * TILE_EL:(2 * GATE_TILES + (db + 1) * KO) * TILE_EL]
                    .rearrange("(kc p m) -> p kc m", p=128, m=128)))
            slab_tgt[(th, db)] = gd.n

        for th in range(2):
            if th >= 1:
                SY.wait_ge(pe.h, mm_tgt[(0, DB - 1)][1])
            ht_tgt[th] = ld.dinc(SY.dma_start(
                out=hT[:, :].rearrange("p (kc q) -> p kc q", q=1024),
                in_=h_sp.rearrange("(kc p) t -> p kc t", p=128)[:, :, th * 1024:(th + 1) * 1024]))
            issue_wd(th, 0)
            issue_wd(th, 1)
            for db in range(DB):
                half = (db % 2) * KO * 128
                TE.wait_ge(gd.h, slab_tgt[(th, db)])
                TE.wait_ge(ld.h, ht_tgt[th])
                lastm = None
                for tq in range(2):
                    bank = pH[(db % 2) * 2 + tq]
                    if db >= 2:
                        TE.wait_ge(ae.h, ev_tgt[(th, db - 2, tq)])
                    elif th >= 1 and db < 2:
                        TE.wait_ge(ae.h, ev_tgt[(th - 1, DB - 2 + db, tq)])
                    for kc in range(KO):
                        lastm = TE.matmul(bank[:, :], wd[:, half + kc * 128: half + (kc + 1) * 128],
                                          hT[:, kc * 1024 + tq * 512: kc * 1024 + tq * 512 + 512],
                                          start=(kc == 0), stop=(kc == KO - 1))
                mm_tgt[(th, db)] = (None, pe.inc(lastm))
                slot = db % 2
                for tq in range(2):
                    bank = pH[(db % 2) * 2 + tq]
                    SE.wait_ge(pe.h, mm_tgt[(th, db)][1])
                    if db >= 2:
                        SE.wait_ge(ld.h, spl_tgt[(th, db - 2)])
                    elif th >= 1:
                        SE.wait_ge(ld.h, spl_tgt[(th - 1, DB - 2 + db)])
                    ev_tgt[(th, db, tq)] = ae.inc(SE.copy(
                        fstage[:, slot * 1024 + tq * 512: slot * 1024 + tq * 512 + 512], bank[:, :]))
                    ae.inc(SE.activation(sqb[:, tq * 512:(tq + 1) * 512],
                                         fstage[:, slot * 1024 + tq * 512: slot * 1024 + tq * 512 + 512],
                                         AF.Square))
                    TE.wait_ge(ae.h, ae.n)
                    if sadd_tgt:
                        TE.wait_ge(ve.h, max(sadd_tgt.values()))
                    TE.matmul(pstA[0:1, :], onesb[:, 0:1],
                              fstage[:, slot * 1024 + tq * 512: slot * 1024 + tq * 512 + 512],
                              start=True, stop=True)
                    stq = pe.inc(TE.matmul(pstB[0:1, :], onesb[:, 0:1],
                                           sqb[:, tq * 512:(tq + 1) * 512],
                                           start=True, stop=True))
                    VE.wait_ge(pe.h, stq)
                    pos = th * 1024 + tq * 512
                    VE.tensor_tensor(statacc[0:1, pos:pos + 512],
                                     statacc[0:1, pos:pos + 512], pstA[0:1, :], OP.add)
                    sadd_tgt[(th, db, tq)] = ve.inc(VE.tensor_tensor(
                        statacc[0:1, T + pos:T + pos + 512],
                        statacc[0:1, T + pos:T + pos + 512], pstB[0:1, :], OP.add))
                SY.wait_ge(ae.h, ev_tgt[(th, db, 1)])
                spl_tgt[(th, db)] = ld.dinc(SY.dma_start(
                    out=ffn_sp[128 * db:128 * db + 128, th * 1024:(th + 1) * 1024],
                    in_=fstage[:, slot * 1024: slot * 1024 + 1024]))
                if db + 2 < DB:
                    issue_wd(th, db + 2)
                elif th == 0 and db >= DB - 2:
                    issue_wd(1, db - (DB - 2))
        VE.wait_ge(ve.h, ve.n)
        VE.tensor_scalar(statacc[0:1, 0:T], statacc[0:1, 0:T], 1.0 / DM, None, OP.mult)
        VE.tensor_scalar(statacc[0:1, T:2 * T], statacc[0:1, T:2 * T], 1.0 / DM, None, OP.mult)
        VE.tensor_tensor(rscr[0:1, :], statacc[0:1, 0:T], statacc[0:1, 0:T], OP.mult)
        VE.tensor_tensor(statacc[0:1, T:2 * T], statacc[0:1, T:2 * T], rscr[0:1, :], OP.subtract)
        ve.inc(VE.tensor_scalar(statacc[0:1, T:2 * T], statacc[0:1, T:2 * T], EPS, None, OP.add))
        SE.wait_ge(ve.h, ve.n)
        ae.inc(SE.activation(statacc[0:1, T:2 * T], statacc[0:1, T:2 * T], AF.Sqrt))
        VE.wait_ge(ae.h, ae.n)
        ve.inc(VE.reciprocal(statacc[0:1, T:2 * T], statacc[0:1, T:2 * T]))
        SY.wait_ge(ve.h, ve.n)
        ld.dinc(SY.dma_start(out=row_sp.rearrange("r t -> (r t)")[7 * T:9 * T].unsqueeze(0),
                             in_=statacc[0:1, :]))
        SY.wait_ge(ld.h, ld.n)
        if DEBUG:
            ld.dinc(SY.dma_start(out=taps["tap_ffn"], in_=ffn_sp))
            ld.dinc(SY.dma_start(out=taps["tap_rows"], in_=row_sp))
            SY.wait_ge(ld.h, ld.n)

    def phaseI(scope):
        fb = scope.enter_context(nc.sbuf_tensor("fbI", [128, 2 * T], BF16))
        x1s = scope.enter_context(nc.sbuf_tensor("x1sI", [128, 2 * T], F32))
        ostage = scope.enter_context(nc.sbuf_tensor("ostageI", [128, 2 * T], F32))
        mBi = scope.enter_context(nc.sbuf_tensor("mBi", [128, T], BF16))
        rBi = scope.enter_context(nc.sbuf_tensor("rBi", [128, T], BF16))

        PO.wait_ge(ld.h, ld.n)
        for _rep in range(2):
            gd.dinc(PO.dma_start(out=mBi[:, :], in_=bass.AP(row_sp.tensor, 7 * T, [[0, 128], [1, T]])))
            gd.dinc(PO.dma_start(out=rBi[:, :], in_=bass.AP(row_sp.tensor, 8 * T, [[0, 128], [1, T]])))
        VE.wait_ge(gd.h, gd.n)
        ld_tgt, dve_tgt, spl_tgt = {}, {}, {}
        for dc in range(KC):
            slot = dc % 2
            if dc >= 2:
                SY.wait_ge(ve.h, dve_tgt[dc - 2])
                SY.wait_ge(ld.h, spl_tgt[dc - 2])
            SY.dma_start(out=fb[:, slot * T:slot * T + T],
                         in_=ffn_sp[128 * dc:128 * dc + 128, :]).then_inc(ld.h, 16)
            SY.dma_start(out=x1s[:, slot * T:slot * T + T],
                         in_=x1_sp[128 * dc:128 * dc + 128, :]).then_inc(ld.h, 16)
            ld.n += 32
            ld_tgt[dc] = ld.n
            VE.wait_ge(ld.h, ld_tgt[dc])
            VE.tensor_tensor(fb[:, slot * T:slot * T + T], fb[:, slot * T:slot * T + T],
                             mBi[:, :], OP.subtract)
            VE.tensor_tensor(fb[:, slot * T:slot * T + T], fb[:, slot * T:slot * T + T],
                             rBi[:, :], OP.mult)
            dve_tgt[dc] = ve.inc(VE.tensor_tensor(ostage[:, slot * T:slot * T + T],
                                                  x1s[:, slot * T:slot * T + T],
                                                  fb[:, slot * T:slot * T + T], OP.add))
            SY.wait_ge(ve.h, dve_tgt[dc])
            spl_tgt[dc] = ld.dinc(SY.dma_start(out=d_y[128 * dc:128 * dc + 128, :],
                                               in_=ostage[:, slot * T:slot * T + T]))
        SY.wait_ge(ld.h, ld.n)
        VE.wait_ge(ld.h, ld.n)

    with contextlib.ExitStack() as s0:
        phase0(s0)
        barrier()
    with contextlib.ExitStack() as sAB:
        mat_alloc(sAB)
        xn = sAB.enter_context(nc.sbuf_tensor("xn", [128, KC * T], BF16))
        with contextlib.ExitStack() as sA:
            phaseA2(sA, xn)
            barrier()
        with contextlib.ExitStack() as sB:
            phaseB(sB, xn)
            barrier()
        mat_finish_and_collective()
        barrier()
    with contextlib.ExitStack() as sC:
        phaseC(sC)
        barrier()
    with contextlib.ExitStack() as sDE:
        Ygn = sDE.enter_context(nc.sbuf_tensor("Ygn", [128, 32 * T], BF16))
        rstdB = sDE.enter_context(nc.sbuf_tensor("rstdB_DE", [128, T], BF16))
        with contextlib.ExitStack() as sD:
            phaseD(sD, Ygn, rstdB)
            barrier()
        with contextlib.ExitStack() as sE:
            rowscratchE_holder = sE.enter_context(nc.sbuf_tensor("rowscrE", [1, T], F32))
            globals()["rowscratchE"] = rowscratchE_holder
            phaseE(sE, Ygn, rstdB)
            barrier()
    with contextlib.ExitStack() as sFGH:
        xn1 = sFGH.enter_context(nc.sbuf_tensor("xn1", [128, KC * T], BF16))
        with contextlib.ExitStack() as sF:
            phaseF(sF, xn1)
            barrier()
        with contextlib.ExitStack() as sG:
            phaseG(sG, xn1)
            barrier()
    with contextlib.ExitStack() as sH:
        phaseH(sH)
        barrier()
    with contextlib.ExitStack() as sI:
        phaseI(sI)
        barrier()
        if DEBUG:
            PO.wait_ge(cc.h, 1)
            gd.dinc(PO.dma_start(out=taps["tap_wfull"], in_=w_full))
            PO.wait_ge(gd.h, gd.n)
            barrier()
    es.close()
    return nc


def make_in_maps(L0_attn_u, L0_attn_in_proj_kernel, L0_attn_conv1d_weight,
                 L0_attn_conv1d_bias, L0_attn_dt_bias, L0_attn_A_log, L0_attn_D,
                 L0_attn_norm_weight, L0_attn_out_proj_kernel, L0_ffn_lut,
                 L0_ffn_gate_idx, L0_ffn_up_idx, L0_ffn_down_idx):
    u = np.asarray(L0_attn_u, dtype=np.float32)
    w_in = np.ascontiguousarray(np.asarray(L0_attn_in_proj_kernel, dtype=np.float32))
    convwT = np.ascontiguousarray(np.asarray(L0_attn_conv1d_weight, dtype=np.float32).T)
    convb = np.ascontiguousarray(np.asarray(L0_attn_conv1d_bias, dtype=np.float32))
    dtb = np.ascontiguousarray(np.asarray(L0_attn_dt_bias, dtype=np.float32))
    alog = np.ascontiguousarray(np.asarray(L0_attn_A_log, dtype=np.float32))
    dvec = np.ascontiguousarray(np.asarray(L0_attn_D, dtype=np.float32))
    nw = np.ascontiguousarray(np.asarray(L0_attn_norm_weight, dtype=np.float32))
    w_out = np.ascontiguousarray(np.asarray(L0_attn_out_proj_kernel, dtype=np.float32))
    lut = np.ascontiguousarray(np.asarray(L0_ffn_lut, dtype=np.float32))
    gT = np.asarray(L0_ffn_gate_idx, dtype=np.int32).T
    upT = np.asarray(L0_ffn_up_idx, dtype=np.int32).T
    dnT = np.asarray(L0_ffn_down_idx, dtype=np.int32).T

    def tiles_a(mm):
        return mm.reshape(KC, 128, OB, 128).transpose(2, 0, 1, 3).reshape(GATE_TILES, 128, 128)

    def tiles_b(mm):
        return mm.reshape(KO, 128, DB, 128).transpose(2, 0, 1, 3).reshape(GATE_TILES, 128, 128)

    all_tiles = np.ascontiguousarray(
        np.concatenate([tiles_a(gT), tiles_a(upT), tiles_b(dnT)], axis=0))

    in_maps = []
    for c in range(NCORES):
        in_maps.append({
            "uT": np.ascontiguousarray(u[c].T),
            "w_in": w_in, "convwT": convwT, "convb": convb, "dt_bias": dtb,
            "a_log": alog, "dvec": dvec, "nw": nw, "w_out": w_out, "lut": lut,
            "idx_sl": np.ascontiguousarray(all_tiles[c * RANK_TILES:(c + 1) * RANK_TILES]),
        })
    return in_maps


def kernel(**inputs):
    in_maps = make_in_maps(**inputs)
    nc = build()
    res = run_bass_kernel_spmd(nc, in_maps, core_ids=list(range(NCORES)))
    return np.stack([np.asarray(r["y"]).T for r in res.results], axis=0)
